# version 13
# baseline (speedup 1.0000x reference)
import numpy as np
import ml_dtypes
import concourse.bass as bass
import concourse.mybir as mybir
from concourse.bass import IndirectOffsetOnAxis
from concourse.bass_utils import run_bass_kernel_spmd
from contextlib import ExitStack

F32 = mybir.dt.float32
BF16 = mybir.dt.bfloat16
U32 = mybir.dt.uint32
AX = mybir.AxisListType.X
ALU = mybir.AluOpType
AF = mybir.ActivationFunctionType
BF = ml_dtypes.bfloat16

NCORES = 8
B, S, D = 4, 4096, 2048
H, DH = 16, 128
QP = 1024
KVP = 1365
KVPP = 1408
INTER = 5632
EPS = 1e-5
SCALE = 1.0 / float(np.sqrt(np.float32(128.0)))
NEG = -1e9
T = B * S
TC = 2048
PAIRS = [[0, 1], [2, 3], [4, 5], [6, 7]]

ENGS = ("sync", "gpsimd", "tensor", "vector", "scalar")
DMA_POOL = 8


class Prog:

    def __init__(self, nc, ctx):
        self.nc = nc
        self.ops = []
        self.last_writer = {}
        self.readers = {}
        self.eng_count = {e: 0 for e in ENGS}
        self.sems = {e: ctx.enter_context(nc.semaphore(f"se_{e}")) for e in ENGS}
        self.dpool = {e: [ctx.enter_context(nc.semaphore(f"sd_{e}{i}"))
                          for i in range(DMA_POOL)] for e in ("sync", "gpsimd", "scalar")}
        self.dcount = {e: [0] * DMA_POOL for e in ("sync", "gpsimd", "scalar")}
        self.drr = {e: 0 for e in ("sync", "gpsimd", "scalar")}
        self.ccsems = [ctx.enter_context(nc.semaphore(f"scc{i}")) for i in range(8)]
        self.ncc = 0
        self.fence_deps = ()

    def op(self, engine, fn, reads=(), writes=(), dma=False, after=()):
        deps = set(after)
        deps.update(self.fence_deps)
        for k in reads:
            if k in self.last_writer:
                deps.add(self.last_writer[k])
            if k.startswith("pb"):
                for rd in self.readers.get(k, ()):
                    if self.ops[rd][0] != engine:
                        deps.add(rd)
        for k in writes:
            if k in self.last_writer:
                deps.add(self.last_writer[k])
            for rd in self.readers.get(k, ()):
                deps.add(rd)
        i = len(self.ops)
        if dma:
            idx = self.drr[engine] % DMA_POOL
            self.drr[engine] += 1
            before = self.dcount[engine][idx]
            self.dcount[engine][idx] = before + 16
            info = ("dma", idx, before)
            pos = 0
        else:
            info = None
            self.eng_count[engine] += 1
            pos = self.eng_count[engine]
        self.ops.append((engine, fn, deps, info, pos))
        for k in reads:
            self.readers.setdefault(k, []).append(i)
        for k in writes:
            self.last_writer[k] = i
            self.readers[k] = []
        return i

    def cc(self, fn):
        sem = self.ccsems[self.ncc]
        self.ncc += 1
        deps = set(self.fence_deps)
        i = len(self.ops)
        self.ops.append(("gpsimd", fn, deps, ("cc", sem), 0))
        return i

    def fence(self):
        deps = []
        last_eng = {}
        last_dma = {}
        last_cc = []
        for i, (e, fn, d, info, pos) in enumerate(self.ops):
            if info is None:
                last_eng[e] = i
            elif info[0] == "dma":
                last_dma[(e, info[1])] = i
            else:
                last_cc.append(i)
        deps.extend(last_eng.values())
        deps.extend(last_dma.values())
        deps.extend(last_cc[-4:])
        self.fence_deps = tuple(deps)

    def emit(self):
        nc = self.nc
        with nc.Block() as block:
            for ename in ENGS:
                self._emit_engine(block, ename)

    def _emit_engine(self, block, ename):
        prog = self

        def body(eng):
            wm = {}

            def wait(sem, val):
                if val > wm.get(id(sem), 0):
                    eng.wait_ge(sem, val)
                    wm[id(sem)] = val

            def do_waits(deps):
                for d in sorted(deps):
                    de, dfn, ddeps, dinfo, dpos = prog.ops[d]
                    if dinfo is not None and dinfo[0] == "dma":
                        wait(prog.dpool[de][dinfo[1]], dinfo[2] + 16)
                    elif dinfo is not None and dinfo[0] == "cc":
                        wait(dinfo[1], 1)
                    else:
                        if de == ename and ename == "tensor":
                            continue
                        wait(prog.sems[de], dpos)

            for i, (e, fn, deps, info, pos) in enumerate(prog.ops):
                if e != ename:
                    continue
                do_waits(deps)
                if info is not None and info[0] == "dma":
                    idx, before = info[1], info[2]
                    if before > 0:
                        wait(prog.dpool[ename][idx], before)
                    inst = fn(eng)
                    inst.then_inc(prog.dpool[ename][idx], 16)
                elif info is not None and info[0] == "cc":
                    inst = fn(eng)
                    inst.then_inc(info[1])
                else:
                    inst = fn(eng)
                    inst.then_inc(prog.sems[ename], 1)
            if ename == "sync":
                for q, pool in prog.dpool.items():
                    for idx, sem in enumerate(pool):
                        if prog.dcount[q][idx] > 0:
                            wait(sem, prog.dcount[q][idx])
                for i in range(prog.ncc):
                    wait(prog.ccsems[i], 1)

        getattr(block, ename)(body)


class Arena:

    def __init__(self, nc, ctx, cols_f32):
        self.t = ctx.enter_context(nc.sbuf_tensor("arena", [128, cols_f32], F32))
        self.cap = cols_f32
        self.off = 0
        self.peak = 0

    def reset(self, to=0):
        self.off = to

    def f32(self, parts, cols):
        o = self.off
        self.off += cols
        self.peak = max(self.peak, self.off)
        assert self.off <= self.cap, f"arena overflow {self.off}>{self.cap}"
        return self.t[0:parts, o:o + cols]

    def bf16(self, parts, cols):
        cf = (cols + 1) // 2
        o = self.off
        self.off += cf
        self.peak = max(self.peak, self.off)
        assert self.off <= self.cap, f"arena overflow {self.off}>{self.cap}"
        return self.t.bitcast(BF16)[0:parts, 2 * o:2 * o + cols]


class Rows:

    def __init__(self, nc, ctx, cols):
        self.t = ctx.enter_context(nc.sbuf_tensor("rows", [1, cols], F32))
        self.cap = cols
        self.off = 0

    def reset(self):
        self.off = 0

    def f32(self, cols):
        o = self.off
        self.off += cols
        assert self.off <= self.cap, "rows overflow"
        return self.t[0:1, o:o + cols]


def _f32(a):
    return np.ascontiguousarray(np.asarray(a, dtype=np.float32))


def _bf(a):
    return np.ascontiguousarray(np.asarray(a, dtype=np.float32).astype(BF))


def _rope_tables_np():
    freqs = (1.0 / (np.float32(10000.0) **
                    (np.arange(0, 128, 2, dtype=np.float32) / np.float32(128.0))))
    pos = np.arange(S, dtype=np.float32)
    emb = pos[:, None] * freqs[None, :32]
    return np.cos(emb).astype(np.float32).T.copy(), \
        np.sin(emb).astype(np.float32).T.copy()


def q_pack_perm8(r):
    perm = np.zeros(1024, np.int64)
    for c in range(4):
        for half in range(2):
            j = 2 * c + half
            h = 8 * r + j
            perm[c * 128 + half * 64: c * 128 + half * 64 + 64] = \
                h * 128 + np.arange(64)
    for g in range(2):
        for hh in range(4):
            j = 4 * g + hh
            h = 8 * r + j
            perm[(4 + g) * 128 + hh * 32:(4 + g) * 128 + hh * 32 + 32] = \
                h * 128 + 64 + 2 * np.arange(32)
    for g in range(2):
        for hh in range(4):
            j = 4 * g + hh
            h = 8 * r + j
            perm[(6 + g) * 128 + hh * 32:(6 + g) * 128 + hh * 32 + 32] = \
                h * 128 + 64 + 2 * np.arange(32) + 1
    return perm


def q_head_rows8(j):
    return ((j // 2) * 128 + (j % 2) * 64,
            (4 + j // 4) * 128 + (j % 4) * 32,
            (6 + j // 4) * 128 + (j % 4) * 32)


def stage_inputs(ins):
    sh = {}
    x = ins["L0_attn_x"]
    xf = _f32(x.reshape(T, D).T)

    sh["wq_down"] = _bf(ins["L0_attn_Wq_down"])
    wkv = np.asarray(ins["L0_attn_Wkv_down"], np.float32)
    lora = np.zeros((D, KVPP), np.float32)
    lora[:, :KVP] = wkv[:, :KVP]
    sh["wkv_lora"] = _bf(lora)
    wr = wkv[:, KVP:]
    sh["wkv_rope"] = _bf(np.concatenate([wr[:, 0::2], wr[:, 1::2]], axis=1))

    sh["gq"] = _f32(np.asarray(ins["L0_attn_gq"]).reshape(8, 128).T)
    gkv = np.zeros(KVPP, np.float32)
    gkv[:KVP] = np.asarray(ins["L0_attn_gkv"])
    sh["gkv"] = _f32(gkv.reshape(11, 128).T)
    sh["wo"] = _bf(ins["L0_attn_Wo"])

    cosT, sinT = _rope_tables_np()
    sh["cos4"] = _f32(np.tile(cosT, (4, 1)))
    sh["sin4"] = _f32(np.tile(sinT, (4, 1)))

    wkvu = np.asarray(ins["L0_attn_Wkv_up"], np.float32)
    wkvu_p = np.zeros((KVPP, 3072), np.float32)
    wkvu_p[:KVP] = wkvu
    per_head = wkvu_p.reshape(KVPP, H, 192)

    for L in ("L0", "L1"):
        gw = np.asarray(ins[f"{L}_ffn_gate_w"], np.float32)
        uw = np.asarray(ins[f"{L}_ffn_up_w"], np.float32)
        dw = np.asarray(ins[f"{L}_ffn_down_w"], np.float32)
        sh[f"{L}_gws"] = _bf(gw.reshape(44, 128, 16, 128).transpose(0, 3, 2, 1)
                              .reshape(44, 128, 2048))
        sh[f"{L}_uws"] = _bf(uw.reshape(44, 128, 16, 128).transpose(0, 3, 2, 1)
                              .reshape(44, 128, 2048))
        sh[f"{L}_dws"] = _bf(dw.reshape(16, 128, 44, 128).transpose(0, 3, 2, 1)
                              .reshape(16, 128, 5632))
        sh[f"{L}_gs"] = _f32(np.asarray(ins[f"{L}_ffn_gate_s"]).reshape(44, 128).T)
        sh[f"{L}_us"] = _f32(np.asarray(ins[f"{L}_ffn_up_s"]).reshape(44, 128).T)
        sh[f"{L}_ds"] = _f32(np.asarray(ins[f"{L}_ffn_down_s"]).reshape(16, 128).T)

    sh["ones_colb"] = np.ones((128, 1), BF)
    sh["ones_colf"] = np.ones((128, 1), np.float32)
    sh["ones_row"] = np.ones((1, 128), np.float32)
    sh["ones512"] = np.ones((1, 512), np.float32)
    sh["identd"] = np.eye(128).astype(BF)
    sh["ident8f"] = np.eye(8).astype(np.float32)
    kk2 = np.arange(128)[:, None]
    qq2 = np.arange(512)[None, :]
    me = np.zeros((128, 4, 512), np.float32)
    for jj in range(4):
        me[:, jj, :] = np.where(jj * 128 + kk2 > qq2, NEG, 0.0)
    sh["maskE"] = _f32(me.reshape(128, 2048))
    ml = np.zeros((128, 384), np.float32)
    ml[64:128, 0:128] = NEG
    ml[0:64, 256:384] = NEG
    sh["maskL"] = _f32(ml)
    sh["io_row"] = _f32(np.tile(np.arange(64, dtype=np.float32)[None, :], (128, 1)))
    sh["io_pp"] = _f32(np.arange(64, dtype=np.float32).reshape(64, 1))

    wq1 = np.asarray(ins["L1_attn_Wq"], np.float32)
    wk1 = np.asarray(ins["L1_attn_Wk"], np.float32)
    wv1 = np.asarray(ins["L1_attn_Wv"], np.float32)
    bq1 = np.asarray(ins["L1_attn_bq"], np.float32)
    bk1 = np.asarray(ins["L1_attn_bk"], np.float32)
    bv1 = np.asarray(ins["L1_attn_bv"], np.float32)

    i128 = np.arange(128, dtype=np.uint32)[:, None]
    cores = []
    for c in range(NCORES):
        b, r = c // 2, c % 2
        pc = {}
        pc["x_f"] = _f32(xf[:, c * TC:(c + 1) * TC])
        pc["wq_up"] = _bf(np.asarray(ins["L0_attn_Wq_up"], np.float32)[:, q_pack_perm8(r)])
        hs = slice(8 * r, 8 * r + 8)
        pc["wkv_up_k"] = _bf(per_head[:, hs, :64].reshape(KVPP, 512))
        pc["wkv_up_v"] = _bf(per_head[:, hs, 64:].reshape(KVPP, 1024))
        ps = slice(r * 2048, r * 2048 + 2048)
        pc["cs_own"] = _f32(np.concatenate([cosT[:, ps], sinT[:, ps]], 0))
        pc["sc_own"] = _f32(np.concatenate([sinT[:, ps], cosT[:, ps]], 0))
        cs = slice(1024 * r, 1024 * r + 1024)
        pc["wq1"] = _bf(wq1[:, cs])
        pc["wk1"] = _bf(wk1[:, cs])
        pc["wv1"] = _bf(wv1[:, cs])
        pc["bq1"] = _f32(bq1[cs].reshape(1, 1024))
        pc["bqpp"] = _f32(bq1[cs].reshape(8, 128).T)
        pc["bkpp"] = _f32(bk1[cs].reshape(8, 128).T)
        pc["bk1"] = _f32(bk1[cs].reshape(1, 1024))
        pc["bv1"] = _f32(bv1[cs].reshape(1, 1024))
        pc["idx_fcq"] = (r * 1024 + np.arange(8, dtype=np.uint32)[None, :] * 128
                         + i128).astype(np.uint32)
        pc["idx_fckv"] = (r * 1408 + np.arange(11, dtype=np.uint32)[None, :] * 128
                          + i128).astype(np.uint32)
        pc["idx_fkr"] = (r * 64 + np.arange(64, dtype=np.uint32)[:, None]
                         ).astype(np.uint32)
        pc["idx_h"] = ((8 * r + np.arange(8, dtype=np.uint32)[None, :]) * 128
                       + i128).astype(np.uint32)
        pc["idx_t"] = (r * 2048 + np.arange(16, dtype=np.uint32)[None, :] * 128
                       + i128).astype(np.uint32)
        pc["idx_c"] = (2 * (np.arange(16, dtype=np.uint32)[None, :] * 128 + i128)
                       + r).astype(np.uint32)
        cores.append(pc)
    return sh, cores


def em_AB(p, t, C, ar, rw):
    xsb = ar.bf16(128, 16 * TC)
    for tt in range(4):
        for k in range(16):
            p.op("gpsimd", lambda e, k=k, tt=tt: e.dma_start(
                out=xsb[:, k * TC + tt * 512: k * TC + (tt + 1) * 512],
                in_=t["x_f"][k * 128:(k + 1) * 128, tt * 512:(tt + 1) * 512]),
                writes=[f"x{k}_{tt}"], dma=True)
    a0 = ar.off
    wsb = ar.bf16(128, 16 * QP)
    craw = ar.f32(128, 8 * 512)
    sqs = ar.bf16(128, 512)
    outsb = ar.bf16(128, 8 * 512)
    rowbuf = rw.f32(3 * 512)
    for k in range(16):
        p.op("sync", lambda e, k=k: e.dma_start(
            out=wsb[:, k * QP:(k + 1) * QP], in_=t["wq_down"][k * 128:(k + 1) * 128, :]),
            writes=[f"A:w{k}"], dma=True)
    NT = TC // 512

    def epiA(m):
        bank = m % 4
        cp = p.op("vector", lambda e, m=m, bank=bank: e.tensor_copy(
            craw[:, m * 512:(m + 1) * 512], p.pb[bank][:, :]),
            reads=[f"pb{bank}"], writes=[f"A:craw{m}"])
        p.op("scalar", lambda e, bank=bank: e.activation(
            sqs[:, :], p.pb[bank][:, :], AF.Square),
            reads=[f"pb{bank}"], writes=["A:sqs"], after=[cp])
        p.op("tensor", lambda e, m=m: e.matmul(
            p.pb[4][0:1, :], C["onc"][:, 0:1], sqs[:, :],
            start=(m == 0), stop=(m == 7)),
            reads=["A:sqs", "onc"],
            writes=["pb4"] if m in (0, 7) else [])

    for tt in range(NT):
        t0 = tt * 512
        for m in range(8):
            bank = m % 4
            for k in range(16):
                p.op("tensor", lambda e, k=k, m=m, t0=t0, bank=bank: e.matmul(
                    p.pb[bank][:, :],
                    wsb[:, k * QP + m * 128: k * QP + (m + 1) * 128],
                    xsb[:, k * TC + t0: k * TC + t0 + 512],
                    start=(k == 0), stop=(k == 15)),
                    reads=[f"x{k}_{tt}", f"A:w{k}"],
                    writes=[f"pb{bank}"] if k in (0, 15) else [])
            if m > 0:
                epiA(m - 1)
        epiA(7)
        p.op("vector", lambda e: e.tensor_scalar(
            rowbuf[0:1, 0:512], p.pb[4][0:1, :], 1.0 / QP, EPS, ALU.mult, ALU.add),
            reads=["pb4"], writes=["A:row0"])
        p.op("scalar", lambda e: e.activation(
            rowbuf[0:1, 512:1024], rowbuf[0:1, 0:512], AF.Ln),
            reads=["A:row0"], writes=["A:row1"])
        p.op("scalar", lambda e: e.activation(
            rowbuf[0:1, 1024:1536], rowbuf[0:1, 512:1024], AF.Exp, scale=-0.5),
            reads=["A:row1"], writes=["A:row2"])
        p.op("tensor", lambda e: e.matmul(
            p.pb[5][:, :], C["onr"][0:1, 0:128], rowbuf[0:1, 1024:1536],
            start=True, stop=True),
            reads=["A:row2", "onr"], writes=["pb5"])
        for m in range(8):
            p.op("vector", lambda e, m=m: e.scalar_tensor_tensor(
                outsb[:, m * 512:(m + 1) * 512],
                craw[:, m * 512:(m + 1) * 512],
                C["gq"][:, m:m + 1], p.pb[5][:, :], ALU.mult, ALU.mult),
                reads=[f"A:craw{m}", "gq", "pb5"], writes=[f"A:out{m}"])
            p.op("gpsimd", lambda e, m=m, t0=t0: e.indirect_dma_start(
                t["cq_sh"][:, :],
                IndirectOffsetOnAxis(ap=C["idx_fcq"][:, m:m + 1], axis=0),
                outsb[:, m * 512:(m + 1) * 512], None, element_offset=t0),
                reads=[f"A:out{m}", "idx"], dma=True)
    p.fence()
    ar.reset(a0)
    wsb = ar.bf16(128, 16 * KVPP)
    wrs = ar.bf16(128, 16 * 64)
    css = ar.f32(64, 2048)
    scs = ar.f32(64, 2048)
    craw = ar.f32(128, 11 * 512)
    kraw = ar.f32(64, 2048)
    ksc1 = ar.f32(64, 1024)
    ksc2 = ar.f32(64, 1024)
    kout = ar.bf16(64, 2048)
    sqs = ar.bf16(128, 512)
    outsb = ar.bf16(128, 11 * 512)
    for k in range(16):
        p.op("sync", lambda e, k=k: e.dma_start(
            out=wsb[:, k * KVPP:(k + 1) * KVPP],
            in_=t["wkv_lora"][k * 128:(k + 1) * 128, :]),
            writes=[f"B:w{k}"], dma=True)
        p.op("sync", lambda e, k=k: e.dma_start(
            out=wrs[:, k * 64:(k + 1) * 64],
            in_=t["wkv_rope"][k * 128:(k + 1) * 128, :]),
            writes=[f"B:wr{k}"], dma=True)
    p.op("sync", lambda e: e.dma_start(out=css[:, :], in_=t["cs_own"]),
         writes=["B:cs"], dma=True)
    p.op("sync", lambda e: e.dma_start(out=scs[:, :], in_=t["sc_own"]),
         writes=["B:sc"], dma=True)
    rowbuf = rw.f32(3 * 512)
    def epiB(m):
        bank = m % 4
        cp = p.op("vector", lambda e, m=m, bank=bank: e.tensor_copy(
            craw[:, m * 512:(m + 1) * 512], p.pb[bank][:, :]),
            reads=[f"pb{bank}"], writes=[f"B:craw{m}"])
        p.op("scalar", lambda e, bank=bank: e.activation(
            sqs[:, :], p.pb[bank][:, :], AF.Square),
            reads=[f"pb{bank}"], writes=["B:sqs"], after=[cp])
        p.op("tensor", lambda e, m=m: e.matmul(
            p.pb[4][0:1, :], C["onc"][:, 0:1], sqs[:, :],
            start=(m == 0), stop=(m == 10)),
            reads=["B:sqs", "onc"],
            writes=["pb4"] if m in (0, 10) else [])

    for tt in range(NT):
        t0 = tt * 512
        for m in range(11):
            bank = m % 4
            for k in range(16):
                p.op("tensor", lambda e, k=k, m=m, t0=t0, bank=bank: e.matmul(
                    p.pb[bank][:, :],
                    wsb[:, k * KVPP + m * 128: k * KVPP + (m + 1) * 128],
                    xsb[:, k * TC + t0: k * TC + t0 + 512],
                    start=(k == 0), stop=(k == 15)),
                    reads=[f"x{k}_{tt}", f"B:w{k}"],
                    writes=[f"pb{bank}"] if k in (0, 15) else [])
            if m > 0:
                epiB(m - 1)
        for k in range(16):
            p.op("tensor", lambda e, k=k, t0=t0: e.matmul(
                p.pb[5][0:64, :],
                wrs[:, k * 64:(k + 1) * 64],
                xsb[:, k * TC + t0: k * TC + t0 + 512],
                start=(k == 0), stop=(k == 15)),
                reads=[f"x{k}_{tt}", f"B:wr{k}"],
                writes=["pb5"] if k in (0, 15) else [])
        epiB(10)
        p.op("vector", lambda e, t0=t0: e.tensor_copy(
            kraw[:, t0:t0 + 512], p.pb[5][0:64, :]),
            reads=["pb5"], writes=["B:kraw"])
        p.op("vector", lambda e: e.tensor_scalar(
            rowbuf[0:1, 0:512], p.pb[4][0:1, :], 1.0 / KVP, EPS, ALU.mult, ALU.add),
            reads=["pb4"], writes=["B:row0"])
        p.op("scalar", lambda e: e.activation(
            rowbuf[0:1, 512:1024], rowbuf[0:1, 0:512], AF.Ln),
            reads=["B:row0"], writes=["B:row1"])
        p.op("scalar", lambda e: e.activation(
            rowbuf[0:1, 1024:1536], rowbuf[0:1, 512:1024], AF.Exp, scale=-0.5),
            reads=["B:row1"], writes=["B:row2"])
        p.op("tensor", lambda e: e.matmul(
            p.pb[6][:, :], C["onr"][0:1, 0:128], rowbuf[0:1, 1024:1536],
            start=True, stop=True),
            reads=["B:row2", "onr"], writes=["pb6"])
        for m in range(11):
            p.op("vector", lambda e, m=m: e.scalar_tensor_tensor(
                outsb[:, m * 512:(m + 1) * 512],
                craw[:, m * 512:(m + 1) * 512],
                C["gkv"][:, m:m + 1], p.pb[6][:, :], ALU.mult, ALU.mult),
                reads=[f"B:craw{m}", "gkv", "pb6"], writes=[f"B:out{m}"])
            p.op("gpsimd", lambda e, m=m, t0=t0: e.indirect_dma_start(
                t["ckv_sh"][:, :],
                IndirectOffsetOnAxis(ap=C["idx_fckv"][:, m:m + 1], axis=0),
                outsb[:, m * 512:(m + 1) * 512], None, element_offset=t0),
                reads=[f"B:out{m}", "idx"], dma=True)
    for hh in range(2):
        c0 = hh * 1024
        cs_ = slice(c0, c0 + 1024)
        p.op("vector", lambda e, cs_=cs_: e.tensor_mul(
            ksc1[0:32, :], kraw[0:32, cs_], css[0:32, cs_]),
            reads=["B:kraw", "B:cs"], writes=["B:ksc1a"])
        p.op("vector", lambda e, cs_=cs_: e.tensor_mul(
            ksc1[32:64, :], kraw[32:64, cs_], css[32:64, cs_]),
            reads=["B:kraw", "B:cs"], writes=["B:ksc1b"])
        p.op("vector", lambda e, cs_=cs_: e.tensor_mul(
            ksc2[0:32, :], kraw[0:32, cs_], scs[0:32, cs_]),
            reads=["B:kraw", "B:sc"], writes=["B:ksc2a"])
        p.op("vector", lambda e, cs_=cs_: e.tensor_mul(
            ksc2[32:64, :], kraw[32:64, cs_], scs[32:64, cs_]),
            reads=["B:kraw", "B:sc"], writes=["B:ksc2b"])
        p.op("vector", lambda e, cs_=cs_: e.tensor_copy(kraw[0:32, cs_], ksc1[32:64, :]),
             reads=["B:ksc1b"], writes=["B:kraw"])
        p.op("vector", lambda e, cs_=cs_: e.tensor_sub(
            kout[0:32, cs_], ksc1[0:32, :], kraw[0:32, cs_]),
            reads=["B:ksc1a", "B:kraw"], writes=[f"B:ne{hh}"])
        p.op("vector", lambda e, cs_=cs_: e.tensor_copy(kraw[32:64, cs_], ksc2[0:32, :]),
             reads=["B:ksc2a"], writes=["B:kraw"])
        p.op("vector", lambda e, cs_=cs_: e.tensor_add(
            kout[32:64, cs_], ksc2[32:64, :], kraw[32:64, cs_]),
            reads=["B:ksc2b", "B:kraw"], writes=[f"B:no{hh}"])
    p.op("gpsimd", lambda e: e.indirect_dma_start(
        t["kr_sh"][:, :],
        IndirectOffsetOnAxis(ap=C["idx_fkr"][0:64, 0:1], axis=0),
        kout[0:64, :], None),
        reads=["B:ne0", "B:no0", "B:ne1", "B:no1", "idx"], dma=True)


def em_C(p, t, C, ar, rw):
    csb = ar.bf16(128, 8 * S)
    wsb = ar.bf16(128, 8 * QP)
    c4 = ar.f32(128, S)
    s4 = ar.f32(128, S)
    rraw = ar.f32(128, 4 * 512)
    rsc1 = ar.f32(128, 512)
    rsc2 = ar.f32(128, 512)
    outsb = ar.bf16(128, 8 * 512)
    for k in range(8):
        for pp in range(2):
            p.op("sync", lambda e, k=k, pp=pp: e.dma_start(
                out=csb[:, k * S + pp * 2048: k * S + pp * 2048 + 2048],
                in_=t["cq_sh"][pp * 1024 + k * 128: pp * 1024 + (k + 1) * 128, :]),
                writes=[f"C:c{k}"], dma=True)
        p.op("sync", lambda e, k=k: e.dma_start(
            out=wsb[:, k * QP:(k + 1) * QP],
            in_=t["wq_up"][k * 128:(k + 1) * 128, :]),
            writes=[f"C:w{k}"], dma=True)
    p.op("sync", lambda e: e.dma_start(out=c4[:, :], in_=t["cos4"]),
         writes=["C:cos"], dma=True)
    p.op("sync", lambda e: e.dma_start(out=s4[:, :], in_=t["sin4"]),
         writes=["C:sin"], dma=True)
    for tt in range(S // 512):
        t0 = tt * 512
        for m in range(8):
            bank = m % 4
            for k in range(8):
                p.op("tensor", lambda e, k=k, m=m, t0=t0, bank=bank: e.matmul(
                    p.pb[bank][:, :],
                    wsb[:, k * QP + m * 128: k * QP + (m + 1) * 128],
                    csb[:, k * S + t0: k * S + t0 + 512],
                    start=(k == 0), stop=(k == 7)),
                    reads=[f"C:c{k}", f"C:w{k}"],
                    writes=[f"pb{bank}"] if k in (0, 7) else [])
            if m < 4:
                p.op("vector", lambda e, m=m, bank=bank: e.tensor_copy(
                    outsb[:, m * 512:(m + 1) * 512], p.pb[bank][:, :]),
                    reads=[f"pb{bank}"], writes=[f"C:out{m}"])
            else:
                p.op("vector", lambda e, m=m, bank=bank: e.tensor_copy(
                    rraw[:, (m - 4) * 512:(m - 3) * 512], p.pb[bank][:, :]),
                    reads=[f"pb{bank}"], writes=[f"C:rr{m - 4}"])
        for g in range(2):
            ek, ok = f"C:rr{g}", f"C:rr{2 + g}"
            E = rraw[:, g * 512:(g + 1) * 512]
            O = rraw[:, (2 + g) * 512:(3 + g) * 512]
            p.op("vector", lambda e, E=E, t0=t0: e.tensor_mul(
                rsc1[:, :], E, c4[:, t0:t0 + 512]),
                reads=[ek, "C:cos"], writes=["C:rs1"])
            p.op("vector", lambda e, O=O, t0=t0: e.tensor_mul(
                rsc2[:, :], O, s4[:, t0:t0 + 512]),
                reads=[ok, "C:sin"], writes=["C:rs2"])
            p.op("vector", lambda e, g=g: e.tensor_sub(
                outsb[:, (4 + g) * 512:(5 + g) * 512], rsc1[:, :], rsc2[:, :]),
                reads=["C:rs1", "C:rs2"], writes=[f"C:out{4 + g}"])
            p.op("vector", lambda e, O=O, t0=t0: e.tensor_mul(
                rsc1[:, :], O, c4[:, t0:t0 + 512]),
                reads=[ok, "C:cos"], writes=["C:rs1"])
            p.op("vector", lambda e, E=E, t0=t0: e.tensor_mul(
                rsc2[:, :], E, s4[:, t0:t0 + 512]),
                reads=[ek, "C:sin"], writes=["C:rs2"])
            p.op("vector", lambda e, g=g: e.tensor_add(
                outsb[:, (6 + g) * 512:(7 + g) * 512], rsc1[:, :], rsc2[:, :]),
                reads=["C:rs1", "C:rs2"], writes=[f"C:out{6 + g}"])
        for m in range(8):
            p.op("sync", lambda e, m=m, t0=t0: e.dma_start(
                out=t["q0"][m * 128:(m + 1) * 128, t0:t0 + 512],
                in_=outsb[:, m * 512:(m + 1) * 512]),
                reads=[f"C:out{m}"], dma=True)


def em_D(p, t, C, ar, rw):
    csb = ar.bf16(128, 11 * S)
    wks = ar.bf16(128, 11 * 512)
    wvs = ar.bf16(128, 11 * 1024)
    outsb = ar.bf16(128, 4 * 512)
    vosb = ar.bf16(128, 2 * 512)
    for k in range(11):
        for pp in range(2):
            p.op("sync", lambda e, k=k, pp=pp: e.dma_start(
                out=csb[:, k * S + pp * 2048: k * S + pp * 2048 + 2048],
                in_=t["ckv_sh"][pp * 1408 + k * 128: pp * 1408 + (k + 1) * 128, :]),
                writes=[f"D:c{k}"], dma=True)
        p.op("sync", lambda e, k=k: e.dma_start(
            out=wks[:, k * 512:(k + 1) * 512],
            in_=t["wkv_up_k"][k * 128:(k + 1) * 128, :]),
            writes=[f"D:wk{k}"], dma=True)
        p.op("sync", lambda e, k=k: e.dma_start(
            out=wvs[:, k * 1024:(k + 1) * 1024],
            in_=t["wkv_up_v"][k * 128:(k + 1) * 128, :]),
            writes=[f"D:wv{k}"], dma=True)
    for tt in range(S // 512):
        t0 = tt * 512
        for m in range(4):
            bank = m % 4
            for k in range(11):
                p.op("tensor", lambda e, k=k, m=m, t0=t0, bank=bank: e.matmul(
                    p.pb[bank][:, :],
                    wks[:, k * 512 + m * 128: k * 512 + (m + 1) * 128],
                    csb[:, k * S + t0: k * S + t0 + 512],
                    start=(k == 0), stop=(k == 10)),
                    reads=[f"D:c{k}", f"D:wk{k}"],
                    writes=[f"pb{bank}"] if k in (0, 10) else [])
            p.op("vector", lambda e, m=m, bank=bank: e.tensor_copy(
                outsb[:, m * 512:(m + 1) * 512], p.pb[bank][:, :]),
                reads=[f"pb{bank}"], writes=[f"D:out{m}"])
            p.op("sync", lambda e, m=m, t0=t0: e.dma_start(
                out=t["knope"][m * 128:(m + 1) * 128, t0:t0 + 512],
                in_=outsb[:, m * 512:(m + 1) * 512]),
                reads=[f"D:out{m}"], dma=True)
    for ttile in range(S // 128):
        for v4 in range(2):
            bank = 4 + (2 * ttile + v4) % 4
            for k in range(11):
                p.op("tensor", lambda e, k=k, ttile=ttile, v4=v4, bank=bank: e.matmul(
                    p.pb[bank][:, :],
                    csb[:, k * S + ttile * 128: k * S + (ttile + 1) * 128],
                    wvs[:, k * 1024 + v4 * 512: k * 1024 + (v4 + 1) * 512],
                    start=(k == 0), stop=(k == 10)),
                    reads=[f"D:c{k}", f"D:wv{k}"],
                    writes=[f"pb{bank}"] if k in (0, 10) else [])
            p.op("vector", lambda e, v4=v4, bank=bank: e.tensor_copy(
                vosb[:, v4 * 512:(v4 + 1) * 512], p.pb[bank][:, :]),
                reads=[f"pb{bank}"], writes=[f"D:vo{v4}"])
            p.op("sync", lambda e, ttile=ttile, v4=v4: e.dma_start(
                out=t["vt"][ttile * 128:(ttile + 1) * 128, v4 * 512:(v4 + 1) * 512],
                in_=vosb[:, v4 * 512:(v4 + 1) * 512]),
                reads=[f"D:vo{v4}"], dma=True)


def _bound_c(p, q_key, q_sb, sq_sb, onc, srow, slot, pf):
    for c8 in range(8):
        sb0 = (c8 % 2) * 512
        bk = 6 + c8 % 2
        p.op("scalar", lambda e, c8=c8, sb0=sb0: e.activation(
            sq_sb[:, sb0:sb0 + 512], q_sb[:, c8 * 512:(c8 + 1) * 512], AF.Square),
            reads=[q_key], writes=[f"{pf}:bsq{c8 % 2}"])
        p.op("tensor", lambda e, sb0=sb0, bk=bk: e.matmul(
            p.pb[bk][0:1, 0:512], onc[:, 0:1], sq_sb[:, sb0:sb0 + 512],
            start=True, stop=True),
            reads=[f"{pf}:bsq{c8 % 2}", "onc"], writes=[f"pb{bk}"])
        p.op("vector", lambda e, c8=c8, slot=slot, bk=bk: e.reduce_max(
            srow[0:1, 16 + slot * 8 + c8:17 + slot * 8 + c8],
            p.pb[bk][0:1, 0:512], axis=AX),
            reads=[f"pb{bk}"], writes=[f"{pf}:m8_{slot}_{c8}"])
    p.op("vector", lambda e, slot=slot: e.reduce_max(
        srow[0:1, slot:slot + 1],
        srow[0:1, 16 + slot * 8:16 + slot * 8 + 8], axis=AX),
        reads=[f"{pf}:m8_{slot}_{c8}" for c8 in range(8)], writes=[f"{pf}:mx{slot}"])


def _negc_from_bounds(p, C, srow, negc_col, pf, pr=0):
    p.op("vector", lambda e: e.tensor_mul(
        srow[0:1, 2:3], srow[0:1, 0:1], srow[0:1, 1:2]),
        reads=[f"{pf}:mx0", f"{pf}:mx1"], writes=[f"{pf}:prod"])
    p.op("scalar", lambda e: e.activation(
        srow[0:1, 3:4], srow[0:1, 2:3], AF.Ln),
        reads=[f"{pf}:prod"], writes=[f"{pf}:lnp"])
    p.op("scalar", lambda e: e.activation(
        srow[0:1, 4:5], srow[0:1, 3:4], AF.Exp, scale=0.5),
        reads=[f"{pf}:lnp"], writes=[f"{pf}:sqp"])
    p.op("vector", lambda e: e.tensor_scalar_mul(
        srow[0:1, 5:6], srow[0:1, 4:5], -SCALE),
        reads=[f"{pf}:sqp"], writes=[f"{pf}:negc_row"])
    p.op("tensor", lambda e: e.matmul(
        p.pb[6][:, 0:1], C["onr"][0:1, 0:128], srow[0:1, 5:6],
        start=True, stop=True),
        reads=[f"{pf}:negc_row", "onr"], writes=["pb6"])
    p.op("vector", lambda e: e.tensor_copy(negc_col[:, 0:1], p.pb[6][:, 0:1]),
         reads=["pb6"], writes=[f"{pf}:negc{pr}"])


def em_E(p, t, C, ar, rw):
    q_sb = ar.bf16(128, 2 * S)
    k_sb = ar.bf16(128, 2 * S)
    vt_sb = ar.bf16(128, 2 * S)
    msk = ar.f32(128, 2048)
    sq_sb = ar.bf16(128, 2 * 512)
    negc = ar.f32(128, 2)
    probs = ar.bf16(128, 2 * 512)
    pacc = ar.f32(128, 2 * 512)
    rbc = ar.f32(128, 512)
    ctxo = ar.bf16(128, 512)
    srow = rw.f32(80)
    rrow = rw.f32(512)
    p.op("sync", lambda e: e.dma_start(out=msk[:, :], in_=t["maskE"]),
         writes=["E:msk"], dma=True)

    def loads(j):
        pr = j % 2
        o = pr * S
        nof, evf, odf = q_head_rows8(j)
        p.op("sync", lambda e, nof=nof, o=o: e.dma_start(
            out=q_sb[0:64, o:o + S], in_=t["q0"][nof:nof + 64, :]),
            writes=[f"E:q{pr}"], dma=True)
        p.op("sync", lambda e, evf=evf, o=o: e.dma_start(
            out=q_sb[64:96, o:o + S], in_=t["q0"][evf:evf + 32, :]),
            writes=[f"E:q{pr}"], dma=True)
        p.op("sync", lambda e, odf=odf, o=o: e.dma_start(
            out=q_sb[96:128, o:o + S], in_=t["q0"][odf:odf + 32, :]),
            writes=[f"E:q{pr}"], dma=True)
        p.op("sync", lambda e, j=j, o=o: e.dma_start(
            out=k_sb[0:64, o:o + S], in_=t["knope"][j * 64:(j + 1) * 64, :]),
            writes=[f"E:k{pr}"], dma=True)
        for pp in range(2):
            p.op("sync", lambda e, pp=pp, o=o: e.dma_start(
                out=k_sb[64:96, o + pp * 2048: o + (pp + 1) * 2048],
                in_=t["kr_sh"][pp * 64: pp * 64 + 32, :]),
                writes=[f"E:k{pr}"], dma=True)
            p.op("sync", lambda e, pp=pp, o=o: e.dma_start(
                out=k_sb[96:128, o + pp * 2048: o + (pp + 1) * 2048],
                in_=t["kr_sh"][pp * 64 + 32: pp * 64 + 64, :]),
                writes=[f"E:k{pr}"], dma=True)
        for g in range(8):
            p.op("sync", lambda e, j=j, g=g, o=o: e.dma_start(
                out=vt_sb[:, o + g * 512: o + (g + 1) * 512].rearrange(
                    "p (a d) -> p a d", d=128),
                in_=t["vt"][g * 512:(g + 1) * 512, j * 128:(j + 1) * 128].rearrange(
                    "(a p) d -> p a d", p=128)),
                writes=[f"E:v{pr}"], dma=True)

    loads(0)
    for j in range(8):
        pr = j % 2
        o = pr * S
        if j < 7:
            loads(j + 1)
        _bound_c(p, f"E:q{pr}", q_sb[:, o:o + S], sq_sb, C["onc"], srow, 0, "E")
        _bound_c(p, f"E:k{pr}", k_sb[:, o:o + S], sq_sb, C["onc"], srow, 1, "E")
        _negc_from_bounds(p, C, srow, negc[:, pr:pr + 1], "E", pr)

        def finkb(sl, kb, nkb, dn, pv):
            b0 = kb % 2
            jj = kb - 4 * sl
            if jj >= 0:
                p.op("vector", lambda e, jj=jj, b0=b0: e.tensor_add(
                    p.pb[b0][:, :], p.pb[b0][:, :], msk[:, jj * 512:(jj + 1) * 512]),
                    reads=[f"pb{b0}", "E:msk"], writes=[f"pb{b0}"])
            p.op("scalar", lambda e, b0=b0, pr=pr: e.activation(
                probs[:, b0 * 512:(b0 + 1) * 512], p.pb[b0][:, :], AF.Exp,
                bias=negc[:, pr:pr + 1], scale=SCALE),
                reads=[f"pb{b0}", f"E:negc{pr}"], writes=[f"E:probs{b0}"])
            sp = sl % 2
            if kb == 0:
                p.op("vector", lambda e, b0=b0, sp=sp: e.tensor_copy(
                    pacc[:, sp * 512:(sp + 1) * 512],
                    probs[:, b0 * 512:(b0 + 1) * 512]),
                    reads=[f"E:probs{b0}"], writes=[f"E:pacc{sp}"])
            else:
                p.op("vector", lambda e, b0=b0, sp=sp: e.tensor_add(
                    pacc[:, sp * 512:(sp + 1) * 512],
                    pacc[:, sp * 512:(sp + 1) * 512],
                    probs[:, b0 * 512:(b0 + 1) * 512]),
                    reads=[f"E:probs{b0}", f"E:pacc{sp}"], writes=[f"E:pacc{sp}"])
            p.op("tensor", lambda e, kb=kb, b0=b0, nkb=nkb, pv=pv, o=o: e.matmul(
                p.pb[pv][:, :], vt_sb[:, o + kb * 128: o + (kb + 1) * 128],
                probs[:, b0 * 512:(b0 + 1) * 512],
                start=(kb == 0), stop=(kb == nkb - 1)),
                reads=[f"E:probs{b0}", f"E:v{pr}"],
                writes=[f"pb{pv}"] if kb in (0, nkb - 1) else [])
            if kb == nkb - 1:
                p.op("tensor", lambda e, dn=dn, sp=sp: e.matmul(
                    p.pb[dn][0:1, :], C["oncf"][:, 0:1],
                    pacc[:, sp * 512:(sp + 1) * 512],
                    start=True, stop=True),
                    reads=[f"E:pacc{sp}", "oncf"], writes=[f"pb{dn}"])

        def slepi(sl, dn, pv):
            p.op("vector", lambda e, dn=dn: e.reciprocal(rrow[0:1, :], p.pb[dn][0:1, :]),
                 reads=[f"pb{dn}"], writes=["E:rrow"])
            p.op("tensor", lambda e, dn=dn: e.matmul(
                p.pb[dn][:, :], C["onr"][0:1, 0:128], rrow[0:1, :],
                start=True, stop=True),
                reads=["E:rrow", "onr"], writes=[f"pb{dn}"])
            p.op("vector", lambda e, dn=dn: e.tensor_copy(rbc[:, :], p.pb[dn][:, :]),
                 reads=[f"pb{dn}"], writes=["E:rbc"])
            p.op("vector", lambda e, pv=pv: e.tensor_mul(ctxo[:, :], p.pb[pv][:, :], rbc[:, :]),
                 reads=[f"pb{pv}", "E:rbc"], writes=["E:ctxo"])
            p.op("gpsimd", lambda e, j=j, sl=sl: e.indirect_dma_start(
                t["ctx0_sh"][:, :],
                IndirectOffsetOnAxis(ap=C["idx_h"][:, j:j + 1], axis=0),
                ctxo[:, :], None, element_offset=sl * 512),
                reads=["E:ctxo", "idx"], dma=True)

        prev = None
        for sl in range(8):
            nkb = 4 * sl + 4
            dn = 4 + (sl % 2) * 2
            pv = 5 + (sl % 2) * 2
            for kb in range(nkb):
                b0 = kb % 2
                p.op("tensor", lambda e, sl=sl, kb=kb, b0=b0, o=o: e.matmul(
                    p.pb[b0][:, :],
                    k_sb[:, o + kb * 128: o + (kb + 1) * 128],
                    q_sb[:, o + sl * 512: o + (sl + 1) * 512],
                    start=True, stop=True),
                    reads=[f"E:k{pr}", f"E:q{pr}"], writes=[f"pb{b0}"])
                if kb == 2 and prev is not None:
                    slepi(*prev)
                if kb > 0:
                    finkb(sl, kb - 1, nkb, dn, pv)
            finkb(sl, nkb - 1, nkb, dn, pv)
            prev = (sl, dn, pv)
        slepi(*prev)


def _ln_apply(p, C, pf, u1, nchunk, dim, outfn, srow, t1, outs):
    A_ = srow[0:1, 0:512]
    Bm = srow[0:1, 512:1024]
    Cc = srow[0:1, 1024:1536]
    p.op("vector", lambda e: e.tensor_scalar(A_, p.pb[6][0:1, :], 1.0 / dim, None, ALU.mult),
         reads=["pb6"], writes=[f"{pf}:lnA"])
    p.op("vector", lambda e: e.tensor_mul(Bm, A_, A_), reads=[f"{pf}:lnA"],
         writes=[f"{pf}:lnB"])
    p.op("vector", lambda e: e.tensor_scalar(Cc, p.pb[7][0:1, :], 1.0 / dim, EPS,
                                             ALU.mult, ALU.add),
         reads=["pb7"], writes=[f"{pf}:lnC"])
    p.op("vector", lambda e: e.tensor_sub(Cc, Cc, Bm),
         reads=[f"{pf}:lnB", f"{pf}:lnC"], writes=[f"{pf}:lnC"])
    p.op("scalar", lambda e: e.activation(Cc, Cc, AF.Ln),
         reads=[f"{pf}:lnC"], writes=[f"{pf}:lnC"])
    p.op("scalar", lambda e: e.activation(Cc, Cc, AF.Exp, scale=-0.5),
         reads=[f"{pf}:lnC"], writes=[f"{pf}:lnC"])
    p.op("vector", lambda e: e.tensor_mul(Bm, A_, Cc),
         reads=[f"{pf}:lnA", f"{pf}:lnC"], writes=[f"{pf}:lnB"])
    p.op("vector", lambda e: e.tensor_scalar_mul(Bm, Bm, -1.0),
         reads=[f"{pf}:lnB"], writes=[f"{pf}:lnB"])
    p.op("tensor", lambda e: e.matmul(
        p.pb[4][:, :], C["onr"][0:1, 0:128], Cc, start=True, stop=True),
        reads=[f"{pf}:lnC", "onr"], writes=["pb4"])
    p.op("tensor", lambda e: e.matmul(
        p.pb[5][:, :], C["onr"][0:1, 0:128], Bm, start=True, stop=True),
        reads=[f"{pf}:lnB", "onr"], writes=["pb5"])
    for m in range(nchunk):
        p.op("vector", lambda e, m=m: e.tensor_mul(
            t1[:, :], u1[:, m * 512:(m + 1) * 512], p.pb[4][:, :]),
            reads=[f"{pf}:u1_{m}", "pb4"], writes=[f"{pf}:t1"])
        p.op("vector", lambda e: e.tensor_add(outs[:, :], t1[:, :], p.pb[5][:, :]),
             reads=[f"{pf}:t1", "pb5"], writes=[f"{pf}:outs"])
        outfn(m, outs)


def em_F(p, t, C, ar, rw):
    ctx0v = t["ctx0_sh"].rearrange("h (p w) -> (h p) w", p=2)
    csb = ar.bf16(128, 16 * TC)
    wsb = ar.bf16(128, 16 * D)
    xc = ar.f32(128, 2 * 512)
    u1 = ar.f32(128, 16 * 512)
    sq = ar.bf16(128, 512)
    accx = ar.f32(128, 512)
    accsq = ar.f32(128, 512)
    t1 = ar.f32(128, 512)
    outs = ar.f32(128, 512)
    srow = rw.f32(3 * 512)
    for k in range(16):
        p.op("gpsimd", lambda e, k=k: e.indirect_dma_start(
            csb[:, k * TC:(k + 1) * TC], None,
            ctx0v[:, :], IndirectOffsetOnAxis(ap=C["idx_c"][:, k:k + 1], axis=0)),
            writes=[f"F:c{k}"], reads=["idx"], dma=True)
        p.op("sync", lambda e, k=k: e.dma_start(
            out=wsb[:, k * D:(k + 1) * D], in_=t["wo"][k * 128:(k + 1) * 128, :]),
            writes=[f"F:w{k}"], dma=True)
    def epiF(m):
        bank = m % 4
        xb = m % 2
        p.op("vector", lambda e, m=m, bank=bank, xb=xb: e.tensor_add(
            u1[:, m * 512:(m + 1) * 512], p.pb[bank][:, :],
            xc[:, xb * 512:(xb + 1) * 512]),
            reads=[f"pb{bank}", f"F:xc{xb}"], writes=[f"F:u1_{m}"])
        if m == 0:
            p.op("vector", lambda e: e.tensor_copy(accx[:, :], u1[:, 0:512]),
                 reads=["F:u1_0"], writes=["F:accx"])
            p.op("scalar", lambda e: e.activation(
                accsq[:, :], u1[:, 0:512], AF.Square),
                reads=["F:u1_0"], writes=["F:accsq"])
        else:
            p.op("vector", lambda e, m=m: e.tensor_add(
                accx[:, :], accx[:, :], u1[:, m * 512:(m + 1) * 512]),
                reads=[f"F:u1_{m}", "F:accx"], writes=["F:accx"])
            p.op("scalar", lambda e, m=m: e.activation(
                sq[:, :], u1[:, m * 512:(m + 1) * 512], AF.Square),
                reads=[f"F:u1_{m}"], writes=["F:sq"])
            p.op("vector", lambda e: e.tensor_add(accsq[:, :], accsq[:, :], sq[:, :]),
                 reads=["F:sq", "F:accsq"], writes=["F:accsq"])
        if m == 15:
            p.op("tensor", lambda e: e.matmul(
                p.pb[6][0:1, :], C["oncf"][:, 0:1], accx[:, :],
                start=True, stop=True),
                reads=["F:accx", "oncf"], writes=["pb6"])
            p.op("tensor", lambda e: e.matmul(
                p.pb[7][0:1, :], C["oncf"][:, 0:1], accsq[:, :],
                start=True, stop=True),
                reads=["F:accsq", "oncf"], writes=["pb7"])

    for tt in range(TC // 512):
        t0 = tt * 512
        for m in range(16):
            bank = m % 4
            xb = m % 2
            p.op("sync", lambda e, m=m, t0=t0, xb=xb: e.dma_start(
                out=xc[:, xb * 512:(xb + 1) * 512],
                in_=t["x_f"][m * 128:(m + 1) * 128, t0:t0 + 512]),
                writes=[f"F:xc{xb}"], dma=True)
            for k in range(16):
                p.op("tensor", lambda e, k=k, m=m, t0=t0, bank=bank: e.matmul(
                    p.pb[bank][:, :],
                    wsb[:, k * D + m * 128: k * D + (m + 1) * 128],
                    csb[:, k * TC + t0: k * TC + t0 + 512],
                    start=(k == 0), stop=(k == 15)),
                    reads=[f"F:c{k}", f"F:w{k}"],
                    writes=[f"pb{bank}"] if k in (0, 15) else [])
            if m > 0:
                epiF(m - 1)
        epiF(15)

        def outfn(m, o, t0=t0):
            p.op("sync", lambda e, m=m, t0=t0, o=o: e.dma_start(
                out=t["x1"][m * 128:(m + 1) * 128, t0:t0 + 512], in_=o[:, :]),
                reads=["F:outs"], dma=True)
        _ln_apply(p, C, "F", u1, 16, D, outfn, srow, t1, outs)


def em_G(p, t, C, ar, rw, L, xin, xout, scat):
    pf = f"G{L}"
    TH = 512
    xb_sb = ar.bf16(128, 2 * 16 * TH)
    gw = ar.bf16(128, 2 * 2048)
    uw = ar.bf16(128, 2 * 2048)
    dw = ar.bf16(128, 2 * 5632)
    gssb = ar.f32(128, 44)
    ussb = ar.f32(128, 44)
    dssb = ar.f32(128, 16)
    g_sb = ar.f32(128, 2 * 512)
    u_sb = ar.f32(128, 2 * 512)
    hdn = ar.bf16(128, 2 * 44 * TH)
    xc = ar.f32(128, 2 * 512)
    u2 = ar.f32(128, 2 * 512)
    u2t = ar.f32(128, 2 * 512)
    sq = ar.bf16(128, 512)
    accx = ar.f32(128, 512)
    accsq = ar.f32(128, 512)
    rsb = ar.f32(128, 512)
    bsb2 = ar.f32(128, 512)
    outs = ar.f32(128, 512)
    outs_bf = ar.bf16(128, 512)
    srow = rw.f32(3 * 512)
    p.op("sync", lambda e: e.dma_start(out=gssb[:, :], in_=t[f"{L}_gs"]),
         writes=[f"{pf}:gs"], dma=True)
    p.op("sync", lambda e: e.dma_start(out=ussb[:, :], in_=t[f"{L}_us"]),
         writes=[f"{pf}:us"], dma=True)
    p.op("sync", lambda e: e.dma_start(out=dssb[:, :], in_=t[f"{L}_ds"]),
         writes=[f"{pf}:ds"], dma=True)

    def load_xb(q):
        qp = q % 2
        h0 = q * TH
        for k in range(16):
            p.op("gpsimd", lambda e, k=k, h0=h0, qp=qp: e.dma_start(
                out=xb_sb[:, qp * 16 * TH + k * TH: qp * 16 * TH + (k + 1) * TH],
                in_=xin[k * 128:(k + 1) * 128, h0:h0 + TH]),
                writes=[f"{pf}:xb{qp}_{k}"], dma=True)

    def gate_up(q, i):
        qp = q % 2
        wsl = i % 2
        bg = i % 2
        bu = 2 + i % 2
        p.op("scalar", lambda e, i=i, wsl=wsl: e.dma_start(
            out=gw[:, wsl * 2048:(wsl + 1) * 2048], in_=t[f"{L}_gws"][i]),
            writes=[f"{pf}:gw{wsl}"], dma=True)
        p.op("gpsimd", lambda e, i=i, wsl=wsl: e.dma_start(
            out=uw[:, wsl * 2048:(wsl + 1) * 2048], in_=t[f"{L}_uws"][i]),
            writes=[f"{pf}:uw{wsl}"], dma=True)
        for k in range(16):
            p.op("tensor", lambda e, k=k, wsl=wsl, qp=qp, bg=bg: e.matmul(
                p.pb[bg][:, :],
                gw[:, wsl * 2048 + k * 128: wsl * 2048 + (k + 1) * 128],
                xb_sb[:, qp * 16 * TH + k * TH: qp * 16 * TH + (k + 1) * TH],
                start=(k == 0), stop=(k == 15)),
                reads=[f"{pf}:gw{wsl}", f"{pf}:xb{qp}_{k}"],
                writes=[f"pb{bg}"] if k in (0, 15) else [])
        for k in range(16):
            p.op("tensor", lambda e, k=k, wsl=wsl, qp=qp, bu=bu: e.matmul(
                p.pb[bu][:, :],
                uw[:, wsl * 2048 + k * 128: wsl * 2048 + (k + 1) * 128],
                xb_sb[:, qp * 16 * TH + k * TH: qp * 16 * TH + (k + 1) * TH],
                start=(k == 0), stop=(k == 15)),
                reads=[f"{pf}:uw{wsl}", f"{pf}:xb{qp}_{k}"],
                writes=[f"pb{bu}"] if k in (0, 15) else [])
        p.op("scalar", lambda e, i=i, bg=bg: e.activation(
            g_sb[:, bg * 512:(bg + 1) * 512], p.pb[bg][:, :], AF.Silu,
            scale=gssb[:, i:i + 1]),
            reads=[f"pb{bg}", f"{pf}:gs"], writes=[f"{pf}:g{bg}"])
        p.op("vector", lambda e, i=i, bg=bg, bu=bu: e.tensor_scalar_mul(
            u_sb[:, bg * 512:(bg + 1) * 512], p.pb[bu][:, :],
            ussb[:, i:i + 1]),
            reads=[f"pb{bu}", f"{pf}:us"], writes=[f"{pf}:u{bg}"])
        p.op("vector", lambda e, i=i, qp=qp, bg=bg: e.tensor_mul(
            hdn[:, qp * 44 * TH + i * TH: qp * 44 * TH + (i + 1) * TH],
            g_sb[:, bg * 512:(bg + 1) * 512], u_sb[:, bg * 512:(bg + 1) * 512]),
            reads=[f"{pf}:g{bg}", f"{pf}:u{bg}"], writes=[f"{pf}:hdn{qp}_{i}"])

    def down_f(q, f):
        qp = q % 2
        h0 = q * TH
        wsl = f % 2
        bk = 4 + f % 2
        p.op("sync", lambda e, f=f, wsl=wsl: e.dma_start(
            out=dw[:, wsl * 5632:(wsl + 1) * 5632], in_=t[f"{L}_dws"][f]),
            writes=[f"{pf}:dw{wsl}"], dma=True)
        p.op("sync", lambda e, f=f, h0=h0, wsl=wsl: e.dma_start(
            out=xc[:, wsl * 512:(wsl + 1) * 512],
            in_=xin[f * 128:(f + 1) * 128, h0:h0 + TH]),
            writes=[f"{pf}:xc{wsl}"], dma=True)
        for ik in range(44):
            p.op("tensor", lambda e, ik=ik, wsl=wsl, qp=qp, bk=bk: e.matmul(
                p.pb[bk][:, :],
                dw[:, wsl * 5632 + ik * 128: wsl * 5632 + (ik + 1) * 128],
                hdn[:, qp * 44 * TH + ik * TH: qp * 44 * TH + (ik + 1) * TH],
                start=(ik == 0), stop=(ik == 43)),
                reads=[f"{pf}:dw{wsl}", f"{pf}:hdn{qp}_{ik}"],
                writes=[f"pb{bk}"] if ik in (0, 43) else [])

    def down_epi(q, f):
        h0 = q * TH
        wsl = f % 2
        bk = 4 + f % 2
        p.op("vector", lambda e, f=f, wsl=wsl, bk=bk: e.scalar_tensor_tensor(
            u2[:, wsl * 512:(wsl + 1) * 512], p.pb[bk][:, :],
            dssb[:, f:f + 1], xc[:, wsl * 512:(wsl + 1) * 512],
            ALU.mult, ALU.add),
            reads=[f"pb{bk}", f"{pf}:ds", f"{pf}:xc{wsl}"],
            writes=[f"{pf}:u2_{wsl}"])
        p.op("gpsimd", lambda e, f=f, h0=h0, wsl=wsl: e.dma_start(
            out=t["u2d"][f * 128:(f + 1) * 128, h0:h0 + TH],
            in_=u2[:, wsl * 512:(wsl + 1) * 512]),
            reads=[f"{pf}:u2_{wsl}"], writes=[f"{pf}:u2d_{q % 2}_{f}"], dma=True)
        if f == 0:
            p.op("vector", lambda e, wsl=wsl: e.tensor_copy(
                accx[:, :], u2[:, wsl * 512:(wsl + 1) * 512]),
                reads=[f"{pf}:u2_{wsl}"], writes=[f"{pf}:accx"])
            p.op("scalar", lambda e, wsl=wsl: e.activation(
                accsq[:, :], u2[:, wsl * 512:(wsl + 1) * 512], AF.Square),
                reads=[f"{pf}:u2_{wsl}"], writes=[f"{pf}:accsq"])
        else:
            p.op("vector", lambda e, wsl=wsl: e.tensor_add(
                accx[:, :], accx[:, :], u2[:, wsl * 512:(wsl + 1) * 512]),
                reads=[f"{pf}:u2_{wsl}", f"{pf}:accx"], writes=[f"{pf}:accx"])
            p.op("scalar", lambda e, wsl=wsl: e.activation(
                sq[:, :], u2[:, wsl * 512:(wsl + 1) * 512], AF.Square),
                reads=[f"{pf}:u2_{wsl}"], writes=[f"{pf}:sq"])
            p.op("vector", lambda e: e.tensor_add(accsq[:, :], accsq[:, :], sq[:, :]),
                 reads=[f"{pf}:sq", f"{pf}:accsq"], writes=[f"{pf}:accsq"])
        if f == 15:
            p.op("tensor", lambda e: e.matmul(
                p.pb[6][0:1, :], C["oncf"][:, 0:1], accx[:, :],
                start=True, stop=True),
                reads=[f"{pf}:accx", "oncf"], writes=["pb6"])
            p.op("tensor", lambda e: e.matmul(
                p.pb[7][0:1, :], C["oncf"][:, 0:1], accsq[:, :],
                start=True, stop=True),
                reads=[f"{pf}:accsq", "oncf"], writes=["pb7"])

    def rows_part(q):
        A_ = srow[0:1, 0:512]
        Bm = srow[0:1, 512:1024]
        Cc = srow[0:1, 1024:1536]
        p.op("vector", lambda e, A_=A_: e.tensor_scalar(
            A_, p.pb[6][0:1, :], 1.0 / D, None, ALU.mult),
            reads=["pb6"], writes=[f"{pf}:lnA"])
        p.op("vector", lambda e, A_=A_, Bm=Bm: e.tensor_mul(Bm, A_, A_),
             reads=[f"{pf}:lnA"], writes=[f"{pf}:lnB"])
        p.op("vector", lambda e, Cc=Cc: e.tensor_scalar(
            Cc, p.pb[7][0:1, :], 1.0 / D, EPS, ALU.mult, ALU.add),
            reads=["pb7"], writes=[f"{pf}:lnC"])
        p.op("vector", lambda e, Cc=Cc, Bm=Bm: e.tensor_sub(Cc, Cc, Bm),
             reads=[f"{pf}:lnB", f"{pf}:lnC"], writes=[f"{pf}:lnC"])
        p.op("scalar", lambda e, Cc=Cc: e.activation(Cc, Cc, AF.Ln),
             reads=[f"{pf}:lnC"], writes=[f"{pf}:lnC"])
        p.op("scalar", lambda e, Cc=Cc: e.activation(Cc, Cc, AF.Exp, scale=-0.5),
             reads=[f"{pf}:lnC"], writes=[f"{pf}:lnC"])
        p.op("vector", lambda e, A_=A_, Bm=Bm, Cc=Cc: e.tensor_mul(Bm, A_, Cc),
             reads=[f"{pf}:lnA", f"{pf}:lnC"], writes=[f"{pf}:lnB"])
        p.op("vector", lambda e, Bm=Bm: e.tensor_scalar_mul(Bm, Bm, -1.0),
             reads=[f"{pf}:lnB"], writes=[f"{pf}:lnB"])
        p.op("tensor", lambda e, Cc=Cc: e.matmul(
            p.pb[6][:, :], C["onr"][0:1, 0:128], Cc, start=True, stop=True),
            reads=[f"{pf}:lnC", "onr"], writes=["pb6"])
        p.op("tensor", lambda e, Bm=Bm: e.matmul(
            p.pb[7][:, :], C["onr"][0:1, 0:128], Bm, start=True, stop=True),
            reads=[f"{pf}:lnB", "onr"], writes=["pb7"])
        p.op("vector", lambda e: e.tensor_copy(rsb[:, :], p.pb[6][:, :]),
             reads=["pb6"], writes=[f"{pf}:rsb"])
        p.op("vector", lambda e: e.tensor_copy(bsb2[:, :], p.pb[7][:, :]),
             reads=["pb7"], writes=[f"{pf}:bsb2"])

    def norm_iter(q, f):
        h0 = q * TH
        wsl = f % 2
        p.op("sync", lambda e, f=f, h0=h0, wsl=wsl: e.dma_start(
            out=u2t[:, wsl * 512:(wsl + 1) * 512],
            in_=t["u2d"][f * 128:(f + 1) * 128, h0:h0 + TH]),
            reads=[f"{pf}:u2d_{q % 2}_{f}"], writes=[f"{pf}:u2t{wsl}"], dma=True)
        p.op("vector", lambda e, wsl=wsl: e.tensor_mul(
            outs[:, :], u2t[:, wsl * 512:(wsl + 1) * 512], rsb[:, :]),
            reads=[f"{pf}:u2t{wsl}", f"{pf}:rsb"], writes=[f"{pf}:outs"])
        p.op("vector", lambda e: e.tensor_add(outs[:, :], outs[:, :], bsb2[:, :]),
             reads=[f"{pf}:outs", f"{pf}:bsb2"], writes=[f"{pf}:outs"])
        p.op("sync", lambda e, f=f, h0=h0: e.dma_start(
            out=xout[f * 128:(f + 1) * 128, h0:h0 + TH],
            in_=outs[:, :]),
            reads=[f"{pf}:outs"], dma=True)
        if scat:
            p.op("vector", lambda e: e.tensor_copy(outs_bf[:, :], outs[:, :]),
                 reads=[f"{pf}:outs"], writes=[f"{pf}:obf"])
            p.op("gpsimd", lambda e, f=f, h0=h0: e.indirect_dma_start(
                t["x2_sh"][:, :],
                IndirectOffsetOnAxis(ap=C["idx_t"][:, f:f + 1], axis=0),
                outs_bf[:, :], None, element_offset=h0),
                reads=[f"{pf}:obf", "idx"], dma=True)

    NQ = TC // TH
    load_xb(0)
    for q in range(NQ):
        if q + 1 < NQ:
            load_xb(q + 1)
        nxt_down = 0
        nxt_norm = 0
        for i in range(44):
            gate_up(q, i)
            if q > 0:
                want = (i + 1) * 16 // 44
                while nxt_down < want:
                    down_f(q - 1, nxt_down)
                    if nxt_down > 0:
                        down_epi(q - 1, nxt_down - 1)
                    nxt_down += 1
            if q > 1:
                while nxt_norm < (i + 1) * 16 // 44:
                    norm_iter(q - 2, nxt_norm)
                    nxt_norm += 1
        if q > 0:
            while nxt_down < 16:
                down_f(q - 1, nxt_down)
                if nxt_down > 0:
                    down_epi(q - 1, nxt_down - 1)
                nxt_down += 1
            down_epi(q - 1, 15)
            rows_part(q - 1)
        if q > 1:
            while nxt_norm < 16:
                norm_iter(q - 2, nxt_norm)
                nxt_norm += 1
    for f in range(16):
        down_f(NQ - 1, f)
        if f > 0:
            down_epi(NQ - 1, f - 1)
        norm_iter(NQ - 2, f)
    down_epi(NQ - 1, 15)
    rows_part(NQ - 1)
    for f in range(16):
        norm_iter(NQ - 1, f)


def em_H(p, t, C, ar, rw):
    xsb = ar.bf16(128, 16 * S)
    wsb = ar.bf16(128, 16 * 1024)
    outsb = ar.bf16(128, 4 * 512)
    bpp = ar.f32(128, 16)
    bsb = rw.f32(3 * 1024)
    on5 = rw.f32(512)
    p.op("sync", lambda e: e.dma_start(out=on5[0:1, :], in_=t["ones512"]),
         writes=["H:on5"], dma=True)
    for k in range(16):
        for pp in range(2):
            p.op("sync", lambda e, k=k, pp=pp: e.dma_start(
                out=xsb[:, k * S + pp * 2048: k * S + pp * 2048 + 2048],
                in_=t["x2_sh"][pp * 2048 + k * 128: pp * 2048 + (k + 1) * 128, :]),
                writes=[f"H:x{k}"], dma=True)
    for nm, boff in (("bq1", 0), ("bk1", 1024), ("bv1", 2048)):
        p.op("sync", lambda e, nm=nm, boff=boff: e.dma_start(
            out=bsb[0:1, boff:boff + 1024], in_=t[nm]),
            writes=[f"H:{nm}"], dma=True)
    p.op("sync", lambda e: e.dma_start(out=bpp[:, 0:8], in_=t["bqpp"]),
         writes=["H:bpp"], dma=True)
    p.op("sync", lambda e: e.dma_start(out=bpp[:, 8:16], in_=t["bkpp"]),
         writes=["H:bpp"], dma=True)
    for wnm, bo, outd in (("wq1", 0, "q1"), ("wk1", 8, "k1")):
        for k in range(16):
            p.op("sync", lambda e, k=k, wnm=wnm: e.dma_start(
                out=wsb[:, k * 1024:(k + 1) * 1024],
                in_=t[wnm][k * 128:(k + 1) * 128, :]),
                writes=[f"H:w{k}"], dma=True)
        for tt in range(S // 512):
            t0 = tt * 512
            for m in range(8):
                bank = m % 4
                for k in range(16):
                    p.op("tensor", lambda e, k=k, m=m, t0=t0, bank=bank: e.matmul(
                        p.pb[bank][:, :],
                        wsb[:, k * 1024 + m * 128: k * 1024 + (m + 1) * 128],
                        xsb[:, k * S + t0: k * S + t0 + 512],
                        start=(k == 0), stop=(k == 15)),
                        reads=[f"H:x{k}", f"H:w{k}"],
                        writes=[f"pb{bank}"] if k in (0, 15) else [])
                p.op("vector", lambda e, m=m, bank=bank, bo=bo: e.tensor_scalar(
                    outsb[:, bank * 512:(bank + 1) * 512], p.pb[bank][:, :],
                    bpp[:, bo + m:bo + m + 1], None, ALU.add),
                    reads=[f"pb{bank}", "H:bpp"], writes=[f"H:o{bank}"])
                p.op("sync", lambda e, m=m, t0=t0, bank=bank, outd=outd: e.dma_start(
                    out=t[outd][m * 128:(m + 1) * 128, t0:t0 + 512],
                    in_=outsb[:, bank * 512:(bank + 1) * 512]),
                    reads=[f"H:o{bank}"], dma=True)
    for wnm, brow, outd in (("wv1", 2048, "v1t"), ("wq1", 0, "q1t")):
        for k in range(16):
            p.op("sync", lambda e, k=k, wnm=wnm: e.dma_start(
                out=wsb[:, k * 1024:(k + 1) * 1024],
                in_=t[wnm][k * 128:(k + 1) * 128, :]),
                writes=[f"H:w{k}"], dma=True)
        for ttile in range(S // 128):
            for v4 in range(2):
                bank = 4 + (2 * ttile + v4) % 4
                p.op("tensor", lambda e, v4=v4, bank=bank, brow=brow: e.matmul(
                    p.pb[bank][:, :], C["onr"][0:1, 0:128],
                    bsb[0:1, brow + v4 * 512: brow + (v4 + 1) * 512],
                    start=True, stop=False),
                    reads=["H:bv1", "H:bq1", "onr"], writes=[f"pb{bank}"])
                for k in range(16):
                    p.op("tensor", lambda e, k=k, ttile=ttile, v4=v4, bank=bank: e.matmul(
                        p.pb[bank][:, :],
                        xsb[:, k * S + ttile * 128: k * S + (ttile + 1) * 128],
                        wsb[:, k * 1024 + v4 * 512: k * 1024 + (v4 + 1) * 512],
                        start=False, stop=(k == 15)),
                        reads=[f"H:x{k}", f"H:w{k}"],
                        writes=[f"pb{bank}"] if k == 15 else [])
                p.op("vector", lambda e, v4=v4, bank=bank: e.tensor_copy(
                    outsb[:, v4 * 512:(v4 + 1) * 512], p.pb[bank][:, :]),
                    reads=[f"pb{bank}"], writes=[f"H:o{v4}"])
                p.op("sync", lambda e, ttile=ttile, v4=v4, outd=outd: e.dma_start(
                    out=t[outd][ttile * 128:(ttile + 1) * 128, v4 * 512:(v4 + 1) * 512],
                    in_=outsb[:, v4 * 512:(v4 + 1) * 512]),
                    reads=[f"H:o{v4}"], dma=True)


def em_I(p, t, C, ar, rw):
    q_sb = ar.bf16(128, 2 * S)
    k_sb = ar.bf16(128, 2 * S)
    vt_sb = ar.bf16(128, 2 * S)
    qt_all = ar.bf16(128, 2 * S)
    msk = ar.f32(128, 384)
    iorow = ar.f32(128, 64)
    iopp = ar.f32(64, 1)
    ident = ar.bf16(128, 128)
    idf8 = ar.f32(8, 8)
    sq_sb = ar.bf16(128, 2 * 512)
    negc = ar.f32(128, 1)
    na = ar.f32(8, S)
    nb = ar.f32(8, S)
    mx8 = ar.f32(8, 8)
    rkall = ar.f32(128, 256)
    probs = ar.bf16(128, 384)
    lctx = ar.bf16(128, S)
    P_sb = ar.bf16(128, 128)
    qgT_sb = ar.bf16(64, 128)
    qg_sb = ar.bf16(128, 64)
    gp_sb = ar.bf16(128, 64)
    rg_pp = ar.f32(64, 1)
    gctx_sb = ar.bf16(64, 128)
    cb_sb = ar.f32(128, 2 * 512)
    pi_sb = ar.bf16(64, 2 * 512)
    t2_sb = ar.f32(128, 2 * 512)
    ctxo = ar.bf16(128, 2 * 512)
    nrow = ar.f32(1, S)
    rloc = ar.f32(1, S)
    mra = ar.f32(1, S)
    srow = rw.f32(64)
    rgrow = rw.f32(64)
    rr_sb = rw.f32(512)
    mr_sb = rw.f32(512)
    cbrow = rw.f32(1024)
    pbb6 = p.pb[6].bitcast(BF16)
    for (tl, dsrc, key) in [(msk, "maskL", "I:msk"), (iorow, "io_row", "I:iorow"),
                            (iopp, "io_pp", "I:iopp"), (ident, "identd", "I:ident"),
                            (idf8, "ident8f", "I:id8")]:
        p.op("sync", lambda e, tl=tl, dsrc=dsrc: e.dma_start(out=tl[:, :], in_=t[dsrc]),
             writes=[key], dma=True)
    for j in range(8):
        p.op("sync", lambda e, j=j: e.dma_start(
            out=q_sb[:, 0:S], in_=t["q1"][j * 128:(j + 1) * 128, :]),
            writes=["I:q0"], dma=True)
        for c8 in range(8):
            sb0 = (c8 % 2) * 512
            bk = c8 % 2
            p.op("scalar", lambda e, c8=c8, sb0=sb0: e.activation(
                sq_sb[:, sb0:sb0 + 512], q_sb[:, c8 * 512:(c8 + 1) * 512], AF.Square),
                reads=["I:q0"], writes=[f"I:sq{c8 % 2}"])
            p.op("tensor", lambda e, sb0=sb0, bk=bk: e.matmul(
                p.pb[bk][0:1, :], C["onc"][:, 0:1], sq_sb[:, sb0:sb0 + 512],
                start=True, stop=True),
                reads=[f"I:sq{c8 % 2}", "onc"], writes=[f"pb{bk}"])
            p.op("vector", lambda e, c8=c8, bk=bk: e.tensor_copy(
                nrow[0:1, c8 * 512:(c8 + 1) * 512], p.pb[bk][0:1, :]),
                reads=[f"pb{bk}"], writes=["I:nrow"])
        p.op("sync", lambda e, j=j: e.dma_start(
            out=t["rankm"][j:j + 1, :], in_=nrow[0:1, :]),
            reads=["I:nrow"], writes=["I:na_d"], dma=True)
    p.op("sync", lambda e: e.dma_start(out=na[:, :], in_=t["rankm"]),
         reads=["I:na_d"], writes=["I:na"], dma=True)
    bufs = [na, nb]
    for rnd in range(8):
        cur, nxt = bufs[rnd % 2], bufs[(rnd + 1) % 2]
        ck, nk = ("I:na" if rnd % 2 == 0 else "I:nb"), ("I:nb" if rnd % 2 == 0 else "I:na")
        p.op("vector", lambda e, cur=cur: e.max(out=mx8[:, :], in_=cur[:, :]),
             reads=[ck], writes=["I:mx8"])
        if rnd == 7:
            p.op("vector", lambda e: e.memset(mx8[:, 6:8], -1.0),
                 reads=[], writes=["I:mx8"])
        p.op("vector", lambda e, cur=cur, nxt=nxt: e.match_replace(
            out=nxt[:, :], in_to_replace=mx8[:, :], in_values=cur[:, :],
            imm_value=-1.0),
            reads=[ck, "I:mx8"], writes=[nk])
    p.op("vector", lambda e: e.tensor_scalar(
        nb[:, :], na[:, :], -0.5, None, ALU.is_le),
        reads=["I:na"], writes=["I:nb"])
    p.op("vector", lambda e: e.memset(nb[:, 0:1], 1.0), reads=[], writes=["I:nb"])
    p.op("vector", lambda e: e.memset(nb[:, S - 1:S], 1.0), reads=[], writes=["I:nb"])
    p.op("vector", lambda e: e.tensor_tensor_scan(
        na[:, :], nb[:, :], nb[:, :], 0.0, ALU.add, ALU.bypass),
        reads=["I:nb"], writes=["I:na"])
    p.op("vector", lambda e: e.tensor_sub(na[:, :], na[:, :], nb[:, :]),
         reads=["I:na", "I:nb"], writes=["I:na"])
    p.op("sync", lambda e: e.dma_start(out=t["maskm"], in_=nb[:, :]),
         reads=["I:nb"], writes=["I:maskmD"], dma=True)
    p.op("vector", lambda e: e.tensor_scalar(
        nb[:, :], nb[:, :], -1e9, 1e9, ALU.mult, ALU.add),
        reads=["I:nb"], writes=["I:nb"])
    p.op("vector", lambda e: e.tensor_add(na[:, :], na[:, :], nb[:, :]),
         reads=["I:na", "I:nb"], writes=["I:na"])
    p.op("sync", lambda e: e.dma_start(out=t["rankm"], in_=na[:, :]),
         reads=["I:na"], writes=["I:rankmD"], dma=True)
    for c in range(32):
        p.op("tensor", lambda e, c=c: e.transpose(
            p.pb[6][:, 0:8], na[0:8, c * 128:(c + 1) * 128], idf8[0:8, 0:8]),
            reads=["I:na", "I:id8"], writes=["pb6"])
        p.op("vector", lambda e, c=c: e.tensor_copy(
            rkall[:, c * 8:(c + 1) * 8], p.pb[6][:, 0:8]),
            reads=["pb6"], writes=["I:rkall"])
    NB = 32

    def loads(j):
        pr = j % 2
        o = pr * S
        p.op("sync", lambda e, j=j, o=o: e.dma_start(
            out=q_sb[:, o:o + S], in_=t["q1"][j * 128:(j + 1) * 128, :]),
            writes=[f"I:q{pr}"], dma=True)
        p.op("sync", lambda e, j=j, o=o: e.dma_start(
            out=k_sb[:, o:o + S], in_=t["k1"][j * 128:(j + 1) * 128, :]),
            writes=[f"I:k{pr}"], dma=True)
        for g in range(8):
            p.op("sync", lambda e, j=j, g=g, o=o: e.dma_start(
                out=vt_sb[:, o + g * 512: o + (g + 1) * 512].rearrange(
                    "p (a d) -> p a d", d=128),
                in_=t["v1t"][g * 512:(g + 1) * 512, j * 128:(j + 1) * 128].rearrange(
                    "(a p) d -> p a d", p=128)),
                writes=[f"I:v{pr}"], dma=True)
            p.op("sync", lambda e, j=j, g=g, o=o: e.dma_start(
                out=qt_all[:, o + g * 512: o + (g + 1) * 512].rearrange(
                    "p (a d) -> p a d", d=128),
                in_=t["q1t"][g * 512:(g + 1) * 512, j * 128:(j + 1) * 128].rearrange(
                    "(a p) d -> p a d", p=128)),
                writes=[f"I:qt{pr}"], dma=True)

    loads(0)
    for j in range(8):
        pr = j % 2
        o = pr * S
        if j < 7:
            loads(j + 1)
        _bound_c(p, f"I:q{pr}", q_sb[:, o:o + S], sq_sb, C["onc"], srow, 0, "I")
        _bound_c(p, f"I:k{pr}", k_sb[:, o:o + S], sq_sb, C["onc"], srow, 1, "I")
        _negc_from_bounds(p, C, srow, negc[:, 0:1], "I")

        def qb0(tb):
            return 0 if tb % 2 == 0 else 7

        def epi_local(tb):
            bqk = qb0(tb)
            qlo = max(0, (tb - 1) * 128)
            qhi = min(S, (tb + 2) * 128)
            Nq = qhi - qlo
            mlo = 128 if tb == 0 else 0
            del qhi
            p.op("vector", lambda e, Nq=Nq, mlo=mlo, bqk=bqk: e.tensor_add(
                p.pb[bqk][:, 0:Nq], p.pb[bqk][:, 0:Nq], msk[:, mlo:mlo + Nq]),
                reads=[f"pb{bqk}", "I:msk"], writes=[f"pb{bqk}"])
            p.op("scalar", lambda e, Nq=Nq, bqk=bqk: e.activation(
                probs[:, 0:Nq], p.pb[bqk][:, 0:Nq], AF.Exp,
                bias=negc[:, 0:1], scale=SCALE),
                reads=[f"pb{bqk}", "I:negc0"], writes=["I:probs"])
            for cg in (-1, 0, 1):
                qb = tb + cg
                if qb < 0 or qb >= NB:
                    continue
                col = qb * 128 - qlo
                first = (tb == max(0, qb - 1))
                last = (tb == min(NB - 1, qb + 1))
                cbank = 2 if qb % 3 == 0 else (3 if qb % 3 == 1 else 6)
                dbank = 1 if qb % 3 == 0 else (4 if qb % 3 == 1 else 5)
                p.op("tensor", lambda e, col=col, dbank=dbank, first=first,
                     last=last: e.matmul(
                    p.pb[dbank][0:1, 0:128], C["onc"][:, 0:1], probs[:, col:col + 128],
                    start=first, stop=last),
                    reads=["I:probs", "onc"],
                    writes=[f"pb{dbank}"] if (first or last) else [])
                p.op("tensor", lambda e, tb=tb, col=col, cbank=cbank, first=first,
                     last=last, o=o: e.matmul(
                    p.pb[cbank][:, 0:128], vt_sb[:, o + tb * 128: o + (tb + 1) * 128],
                    probs[:, col:col + 128], start=first, stop=last),
                    reads=["I:probs", f"I:v{pr}"],
                    writes=[f"pb{cbank}"] if (first or last) else [])
            done = [tb - 1] if tb < NB - 1 else [tb - 1, tb]
            for qb in done:
                if qb < 0:
                    continue
                cbank = 2 if qb % 3 == 0 else (3 if qb % 3 == 1 else 6)
                dbank = 1 if qb % 3 == 0 else (4 if qb % 3 == 1 else 5)
                p.op("vector", lambda e, qb=qb, dbank=dbank: e.reciprocal(
                    rloc[0:1, qb * 128:(qb + 1) * 128], p.pb[dbank][0:1, 0:128]),
                    reads=[f"pb{dbank}"], writes=["I:rloc"])
                p.op("vector", lambda e, qb=qb, cbank=cbank: e.tensor_copy(
                    lctx[:, qb * 128:(qb + 1) * 128], p.pb[cbank][:, 0:128]),
                    reads=[f"pb{cbank}"], writes=["I:lctx"])

        for tb in range(NB):
            qlo = max(0, (tb - 1) * 128)
            qhi = min(S, (tb + 2) * 128)
            Nq = qhi - qlo
            bqk = qb0(tb)
            p.op("tensor", lambda e, tb=tb, qlo=qlo, Nq=Nq, bqk=bqk, o=o: e.matmul(
                p.pb[bqk][:, 0:Nq], k_sb[:, o + tb * 128: o + (tb + 1) * 128],
                q_sb[:, o + qlo: o + qlo + Nq], start=True, stop=True),
                reads=[f"I:k{pr}", f"I:q{pr}"], writes=[f"pb{bqk}"])
            if tb > 0:
                epi_local(tb - 1)
        epi_local(NB - 1)
        for c in range(32):
            pp = c % 2
            p.op("vector", lambda e, c=c, pp=pp, j=j: e.tensor_scalar(
                P_sb[:, pp * 64:(pp + 1) * 64], iorow[:, :],
                rkall[:, c * 8 + j:c * 8 + j + 1], None, ALU.is_equal),
                reads=["I:iorow", "I:rkall"], writes=[f"I:P{pp}"])
            p.op("tensor", lambda e, c=c, pp=pp, o=o: e.matmul(
                p.pb[0][0:64, 0:128], P_sb[:, pp * 64:(pp + 1) * 64],
                qt_all[:, o + c * 128: o + (c + 1) * 128],
                start=(c == 0), stop=(c == 31)),
                reads=[f"I:P{pp}", f"I:qt{pr}"],
                writes=["pb0"] if c in (0, 31) else [])
        p.op("vector", lambda e: e.tensor_copy(qgT_sb[:, :], p.pb[0][0:64, 0:128]),
             reads=["pb0"], writes=["I:qgT"])
        p.op("tensor", lambda e: e.transpose(
            pbb6[:, 0:64], qgT_sb[:, :], ident[0:64, 0:64]),
            reads=["I:qgT", "I:ident"], writes=["pb6"])
        p.op("vector", lambda e: e.tensor_copy(qg_sb[:, :], pbb6[:, 0:64]),
             reads=["pb6"], writes=["I:qg"])
        for kb in range(32):
            p.op("tensor", lambda e, kb=kb, o=o: e.matmul(
                p.pb[1][:, 0:64], k_sb[:, o + kb * 128: o + (kb + 1) * 128], qg_sb[:, :],
                start=True, stop=True),
                reads=[f"I:k{pr}", "I:qg"], writes=["pb1"])
            p.op("scalar", lambda e: e.activation(
                gp_sb[:, :], p.pb[1][:, 0:64], AF.Exp,
                bias=negc[:, 0:1], scale=SCALE),
                reads=["pb1", "I:negc0"], writes=["I:gp"])
            p.op("tensor", lambda e, kb=kb: e.matmul(
                p.pb[2][0:1, 0:64], C["onc"][:, 0:1], gp_sb[:, :],
                start=(kb == 0), stop=(kb == 31)),
                reads=["I:gp", "onc"],
                writes=["pb2"] if kb in (0, 31) else [])
            p.op("tensor", lambda e, kb=kb, o=o: e.matmul(
                p.pb[3][0:64, 0:128], gp_sb[:, :], vt_sb[:, o + kb * 128: o + (kb + 1) * 128],
                start=(kb == 0), stop=(kb == 31)),
                reads=["I:gp", f"I:v{pr}"],
                writes=["pb3"] if kb in (0, 31) else [])
        p.op("vector", lambda e: e.reciprocal(rgrow[0:1, 0:64], p.pb[2][0:1, 0:64]),
             reads=["pb2"], writes=["I:rgrow"])
        p.op("tensor", lambda e: e.matmul(
            p.pb[5][0:64, 0:1], rgrow[0:1, 0:64], C["onr"][0:1, 0:1],
            start=True, stop=True),
            reads=["I:rgrow", "onr"], writes=["pb5"])
        p.op("vector", lambda e: e.tensor_copy(rg_pp[:, 0:1], p.pb[5][0:64, 0:1]),
             reads=["pb5"], writes=["I:rg"])
        p.op("vector", lambda e: e.tensor_scalar_mul(
            gctx_sb[:, :], p.pb[3][0:64, 0:128], rg_pp[:, 0:1]),
            reads=["pb3", "I:rg"], writes=["I:gctx"])
        p.op("sync", lambda e, j=j: e.dma_start(
            out=nrow[0:1, :], in_=t["rankm"][j:j + 1, :]),
            reads=["I:rankmD"], writes=["I:nrow"], dma=True)
        p.op("sync", lambda e, j=j: e.dma_start(
            out=mra[0:1, :], in_=t["maskm"][j:j + 1, :]),
            reads=["I:maskmD"], writes=["I:mra"], dma=True)
        for c8 in range(8):
            cp = c8 % 2
            co = cp * 512
            bb5 = 5 if cp == 0 else 2
            bb7 = 7 if cp == 0 else 1
            bb4 = 4 if cp == 0 else 3
            p.op("vector", lambda e, c8=c8, co=co, cp=cp: e.tensor_scalar(
                cbrow[0:1, co:co + 512], mra[0:1, c8 * 512:(c8 + 1) * 512], -1.0, 1.0,
                ALU.mult, ALU.add),
                reads=["I:mra"], writes=[f"I:cbrow{cp}"])
            p.op("vector", lambda e, c8=c8, co=co, cp=cp: e.tensor_mul(
                cbrow[0:1, co:co + 512], cbrow[0:1, co:co + 512],
                rloc[0:1, c8 * 512:(c8 + 1) * 512]),
                reads=[f"I:cbrow{cp}", "I:rloc"], writes=[f"I:cbrow{cp}"])
            p.op("tensor", lambda e, co=co, cp=cp, bb5=bb5: e.matmul(
                p.pb[bb5][:, :], C["onr"][0:1, 0:128], cbrow[0:1, co:co + 512],
                start=True, stop=True),
                reads=[f"I:cbrow{cp}", "onr"], writes=[f"pb{bb5}"])
            p.op("vector", lambda e, co=co, cp=cp, bb5=bb5: e.tensor_copy(
                cb_sb[:, co:co + 512], p.pb[bb5][:, :]),
                reads=[f"pb{bb5}"], writes=[f"I:cb{cp}"])
            p.op("tensor", lambda e, c8=c8, bb7=bb7: e.matmul(
                p.pb[bb7][0:64, :], C["onr"][0:1, 0:64],
                nrow[0:1, c8 * 512:(c8 + 1) * 512],
                start=True, stop=True),
                reads=["I:nrow", "onr"], writes=[f"pb{bb7}"])
            p.op("vector", lambda e, co=co, cp=cp, bb7=bb7: e.tensor_scalar(
                pi_sb[:, co:co + 512], p.pb[bb7][0:64, :], iopp[:, 0:1], None,
                ALU.is_equal),
                reads=[f"pb{bb7}", "I:iopp"], writes=[f"I:pi{cp}"])
            p.op("tensor", lambda e, co=co, cp=cp, bb4=bb4: e.matmul(
                p.pb[bb4][:, :], gctx_sb[:, :], pi_sb[:, co:co + 512],
                start=True, stop=True),
                reads=["I:gctx", f"I:pi{cp}"], writes=[f"pb{bb4}"])
            p.op("vector", lambda e, c8=c8, co=co, cp=cp: e.tensor_mul(
                t2_sb[:, co:co + 512], lctx[:, c8 * 512:(c8 + 1) * 512],
                cb_sb[:, co:co + 512]),
                reads=["I:lctx", f"I:cb{cp}"], writes=[f"I:t2{cp}"])
            p.op("vector", lambda e, co=co, cp=cp, bb4=bb4: e.tensor_add(
                ctxo[:, co:co + 512], t2_sb[:, co:co + 512], p.pb[bb4][:, :]),
                reads=[f"I:t2{cp}", f"pb{bb4}"], writes=[f"I:ctxo{cp}"])
            p.op("gpsimd", lambda e, j=j, c8=c8, co=co, cp=cp: e.indirect_dma_start(
                t["ctx1_sh"][:, :],
                IndirectOffsetOnAxis(ap=C["idx_h"][:, j:j + 1], axis=0),
                ctxo[:, co:co + 512], None, element_offset=c8 * 512),
                reads=[f"I:ctxo{cp}", "idx"], dma=True)


def em_J(p, t, C, ar, rw):
    ctx1v = t["ctx1_sh"].rearrange("h (p w) -> (h p) w", p=2)
    a1sb = ar.bf16(128, 16 * TC)
    xc = ar.f32(128, 2 * 512)
    u1 = ar.f32(128, 16 * 512)
    sq = ar.bf16(128, 512)
    accx = ar.f32(128, 512)
    accsq = ar.f32(128, 512)
    t1 = ar.f32(128, 512)
    outs = ar.f32(128, 512)
    srow = rw.f32(3 * 512)
    for k in range(16):
        p.op("gpsimd", lambda e, k=k: e.indirect_dma_start(
            a1sb[:, k * TC:(k + 1) * TC], None,
            ctx1v[:, :], IndirectOffsetOnAxis(ap=C["idx_c"][:, k:k + 1], axis=0)),
            writes=[f"J:a{k}"], reads=["idx"], dma=True)
    def epiJ(m):
        if m == 0:
            p.op("vector", lambda e: e.tensor_copy(accx[:, :], u1[:, 0:512]),
                 reads=["J:u1_0"], writes=["J:accx"])
            p.op("scalar", lambda e: e.activation(
                accsq[:, :], u1[:, 0:512], AF.Square),
                reads=["J:u1_0"], writes=["J:accsq"])
        else:
            p.op("vector", lambda e, m=m: e.tensor_add(
                accx[:, :], accx[:, :], u1[:, m * 512:(m + 1) * 512]),
                reads=[f"J:u1_{m}", "J:accx"], writes=["J:accx"])
            p.op("scalar", lambda e, m=m: e.activation(
                sq[:, :], u1[:, m * 512:(m + 1) * 512], AF.Square),
                reads=[f"J:u1_{m}"], writes=["J:sq"])
            p.op("vector", lambda e: e.tensor_add(accsq[:, :], accsq[:, :], sq[:, :]),
                 reads=["J:sq", "J:accsq"], writes=["J:accsq"])
        if m == 15:
            p.op("tensor", lambda e: e.matmul(
                p.pb[6][0:1, :], C["oncf"][:, 0:1], accx[:, :],
                start=True, stop=True),
                reads=["J:accx", "oncf"], writes=["pb6"])
            p.op("tensor", lambda e: e.matmul(
                p.pb[7][0:1, :], C["oncf"][:, 0:1], accsq[:, :],
                start=True, stop=True),
                reads=["J:accsq", "oncf"], writes=["pb7"])

    for tt in range(TC // 512):
        t0 = tt * 512
        for m in range(16):
            xb = m % 2
            p.op("sync", lambda e, m=m, t0=t0, xb=xb: e.dma_start(
                out=xc[:, xb * 512:(xb + 1) * 512],
                in_=t["x2"][m * 128:(m + 1) * 128, t0:t0 + 512]),
                writes=[f"J:xc{xb}"], dma=True)
            p.op("vector", lambda e, m=m, t0=t0: e.tensor_copy(
                u1[:, m * 512:(m + 1) * 512],
                a1sb[:, m * TC + t0: m * TC + t0 + 512]),
                reads=[f"J:a{m}"], writes=[f"J:u1_{m}"])
            p.op("vector", lambda e, m=m, xb=xb: e.tensor_add(
                u1[:, m * 512:(m + 1) * 512],
                u1[:, m * 512:(m + 1) * 512],
                xc[:, xb * 512:(xb + 1) * 512]),
                reads=[f"J:u1_{m}", f"J:xc{xb}"], writes=[f"J:u1_{m}"])
            if m > 0:
                epiJ(m - 1)
        epiJ(15)

        def outfn(m, o, t0=t0):
            p.op("sync", lambda e, m=m, t0=t0, o=o: e.dma_start(
                out=t["x3"][m * 128:(m + 1) * 128, t0:t0 + 512], in_=o[:, :]),
                reads=["J:outs"], dma=True)
        _ln_apply(p, C, "J", u1, 16, D, outfn, srow, t1, outs)


PHASES_ALL = ("AB", "C", "D", "E", "F", "G0", "H", "I", "J", "G1")


def build_fused(upto="G1", dbg=False):
    nc = bass.Bass("TRN2", target_bir_lowering=False)
    phases = PHASES_ALL[:PHASES_ALL.index(upto) + 1]
    t = {}

    def ein(name, shape, dt=F32):
        t[name] = nc.dram_tensor(name, shape, dt, kind="ExternalInput").ap()

    ein("x_f", [D, TC]); ein("wq_down", [D, QP], BF16); ein("gq", [128, 8])
    ein("wkv_lora", [D, KVPP], BF16); ein("wkv_rope", [D, 64], BF16); ein("gkv", [128, 11])
    ein("wq_up", [QP, 1024], BF16); ein("wkv_up_k", [KVPP, 512], BF16)
    ein("wkv_up_v", [KVPP, 1024], BF16)
    ein("wo", [D, D], BF16)
    ein("cs_own", [64, TC]); ein("sc_own", [64, TC])
    ein("cos4", [128, S]); ein("sin4", [128, S])
    for L in ("L0", "L1"):
        ein(f"{L}_gws", [44, 128, 2048], BF16); ein(f"{L}_uws", [44, 128, 2048], BF16)
        ein(f"{L}_dws", [16, 128, 5632], BF16)
        ein(f"{L}_gs", [128, 44]); ein(f"{L}_us", [128, 44]); ein(f"{L}_ds", [128, 16])
    ein("wq1", [D, 1024], BF16); ein("wk1", [D, 1024], BF16)
    ein("wv1", [D, 1024], BF16)
    ein("bq1", [1, 1024]); ein("bk1", [1, 1024]); ein("bv1", [1, 1024])
    ein("bqpp", [128, 8]); ein("bkpp", [128, 8])
    ein("maskE", [128, 2048]); ein("maskL", [128, 384])
    ein("io_row", [128, 64]); ein("io_pp", [64, 1]); ein("identd", [128, 128], BF16)
    ein("ident8f", [8, 8])
    ein("ones_colb", [128, 1], BF16); ein("ones_colf", [128, 1])
    ein("ones_row", [1, 128]); ein("ones512", [1, 512])
    ein("idx_fcq", [128, 8], U32); ein("idx_fckv", [128, 11], U32)
    ein("idx_fkr", [64, 1], U32); ein("idx_h", [128, 8], U32)
    ein("idx_t", [128, 16], U32); ein("idx_c", [128, 16], U32)

    def sh_t(name, shape, dt=BF16):
        t[name] = nc.dram_tensor(name, shape, dt, kind="Internal",
                                 addr_space="Shared").ap()

    sh_t("cq_sh", [2 * 1024, 2048]); sh_t("ckv_sh", [2 * 1408, 2048])
    sh_t("kr_sh", [2 * 64, 2048])
    sh_t("ctx0_sh", [2048, S]); sh_t("x2_sh", [2 * 2048, 2048])
    sh_t("ctx1_sh", [2048, S])

    ikind = "ExternalOutput" if dbg else "Internal"

    def loc(name, shape, dt, kind):
        t[name] = nc.dram_tensor(name, shape, dt, kind=kind).ap()

    loc("q0", [1024, S], BF16, ikind)
    loc("knope", [512, S], BF16, ikind)
    loc("vt", [S, 1024], BF16, ikind)
    loc("x1", [D, TC], F32, ikind)
    loc("x2", [D, TC], F32, ikind)
    loc("x3", [D, TC], F32, ikind)
    loc("q1", [1024, S], BF16, ikind)
    loc("k1", [1024, S], BF16, ikind)
    loc("v1t", [S, 1024], BF16, ikind)
    loc("q1t", [S, 1024], BF16, "Internal")
    loc("u2d", [D, TC], F32, "Internal")
    loc("rankm", [8, S], F32, "Internal")
    loc("maskm", [8, S], F32, "Internal")
    loc("bar_in", [1, 128], F32, "Internal")
    for i in range(4):
        loc(f"bar_out{i}", [1, 128], F32, "Internal")
    t["xout"] = nc.dram_tensor("xout", [D, TC], F32, kind="ExternalOutput").ap()

    with ExitStack() as ctx:
        C = {}
        C["onc"] = ctx.enter_context(nc.sbuf_tensor("c_onc", [128, 1], BF16))
        C["oncf"] = ctx.enter_context(nc.sbuf_tensor("c_oncf", [128, 1], F32))
        C["onr"] = ctx.enter_context(nc.sbuf_tensor("c_onr", [1, 128], F32))
        C["gq"] = ctx.enter_context(nc.sbuf_tensor("c_gq", [128, 8], F32))
        C["gkv"] = ctx.enter_context(nc.sbuf_tensor("c_gkv", [128, 11], F32))
        C["idx_fcq"] = ctx.enter_context(nc.sbuf_tensor("c_ifcq", [128, 8], U32))
        C["idx_fckv"] = ctx.enter_context(nc.sbuf_tensor("c_ifckv", [128, 11], U32))
        C["idx_fkr"] = ctx.enter_context(nc.sbuf_tensor("c_ifkr", [64, 1], U32))
        C["idx_h"] = ctx.enter_context(nc.sbuf_tensor("c_ih", [128, 8], U32))
        C["idx_t"] = ctx.enter_context(nc.sbuf_tensor("c_it", [128, 16], U32))
        C["idx_c"] = ctx.enter_context(nc.sbuf_tensor("c_ic", [128, 16], U32))
        ar = Arena(nc, ctx, 48744)
        rw = Rows(nc, ctx, 4096)
        p = Prog(nc, ctx)
        p.pb = [ctx.enter_context(nc.psum_tensor(f"pb{i}", [128, 512], F32))
                for i in range(8)]

        for nm in ("onc", "oncf", "onr", "gq", "gkv",
                   "idx_fcq", "idx_fckv", "idx_fkr", "idx_h", "idx_t", "idx_c"):
            src = {"onc": "ones_colb", "oncf": "ones_colf", "onr": "ones_row"}.get(nm, nm)
            key = "idx" if nm.startswith("idx") else nm
            p.op("sync", lambda e, nm=nm, src=src: e.dma_start(out=C[nm][:, :], in_=t[src]),
                 writes=[key], dma=True)
        p.op("sync", lambda e: e.dma_start(out=t["bar_in"], in_=C["onr"][0:1, :]),
             reads=["onr"], writes=["barin"], dma=True)

        def barrier(i):
            p.fence()
            p.cc(lambda e, i=i: e.collective_compute(
                "AllReduce", mybir.AluOpType.add, replica_groups=PAIRS,
                ins=[t["bar_in"].opt()], outs=[t[f"bar_out{i}"].opt()]))
            p.fence()

        for ph in phases:
            ar.reset(0)
            rw.reset()
            if ph == "AB":
                em_AB(p, t, C, ar, rw)
                barrier(0)
            elif ph == "C":
                em_C(p, t, C, ar, rw)
                p.fence()
            elif ph == "D":
                em_D(p, t, C, ar, rw)
                p.fence()
            elif ph == "E":
                em_E(p, t, C, ar, rw)
                barrier(1)
            elif ph == "F":
                em_F(p, t, C, ar, rw)
                p.fence()
            elif ph == "G0":
                em_G(p, t, C, ar, rw, "L0", t["x1"], t["x2"], scat=True)
                barrier(2)
            elif ph == "H":
                em_H(p, t, C, ar, rw)
                p.fence()
            elif ph == "I":
                em_I(p, t, C, ar, rw)
                barrier(3)
            elif ph == "J":
                em_J(p, t, C, ar, rw)
                p.fence()
            elif ph == "G1":
                em_G(p, t, C, ar, rw, "L1", t["x3"], t["xout"], scat=False)
        p.emit()
    return nc


def kernel(**ins):
    shd, cores = stage_inputs(ins)
    nc = build_fused()
    maps = []
    for c in range(NCORES):
        m = dict(shd)
        m.update(cores[c])
        maps.append(m)
    res = run_bass_kernel_spmd(nc, maps, core_ids=list(range(NCORES)))
    out = np.zeros((B, S, D), np.float32)
    for c in range(NCORES):
        b, r = c // 2, c % 2
        xo = np.asarray(res.results[c]["xout"])
        out[b, r * 2048:(r + 1) * 2048, :] = xo.T
    return out


# version 14
# speedup vs baseline: 1.0302x; 1.0302x over previous
import numpy as np
import ml_dtypes
import concourse.bass as bass
import concourse.mybir as mybir
from concourse.bass import IndirectOffsetOnAxis
from concourse.bass_utils import run_bass_kernel_spmd
from contextlib import ExitStack

F32 = mybir.dt.float32
BF16 = mybir.dt.bfloat16
U32 = mybir.dt.uint32
AX = mybir.AxisListType.X
ALU = mybir.AluOpType
AF = mybir.ActivationFunctionType
BF = ml_dtypes.bfloat16

NCORES = 8
B, S, D = 4, 4096, 2048
H, DH = 16, 128
QP = 1024
KVP = 1365
KVPP = 1408
INTER = 5632
EPS = 1e-5
SCALE = 1.0 / float(np.sqrt(np.float32(128.0)))
NEG = -1e9
T = B * S
TC = 2048
PAIRS = [[0, 1], [2, 3], [4, 5], [6, 7]]

ENGS = ("sync", "gpsimd", "tensor", "vector", "scalar")
DMA_POOL = 8


class Prog:

    def __init__(self, nc, ctx):
        self.nc = nc
        self.ops = []
        self.last_writer = {}
        self.readers = {}
        self.eng_count = {e: 0 for e in ENGS}
        self.sems = {e: ctx.enter_context(nc.semaphore(f"se_{e}")) for e in ENGS}
        self.dpool = {e: [ctx.enter_context(nc.semaphore(f"sd_{e}{i}"))
                          for i in range(DMA_POOL)] for e in ("sync", "gpsimd", "scalar")}
        self.dcount = {e: [0] * DMA_POOL for e in ("sync", "gpsimd", "scalar")}
        self.drr = {e: 0 for e in ("sync", "gpsimd", "scalar")}
        self.ccsems = [ctx.enter_context(nc.semaphore(f"scc{i}")) for i in range(8)]
        self.ncc = 0
        self.fence_deps = ()

    def op(self, engine, fn, reads=(), writes=(), dma=False, after=()):
        deps = set(after)
        deps.update(self.fence_deps)
        for k in reads:
            if k in self.last_writer:
                deps.add(self.last_writer[k])
            if k.startswith("pb"):
                for rd in self.readers.get(k, ()):
                    if self.ops[rd][0] != engine:
                        deps.add(rd)
        for k in writes:
            if k in self.last_writer:
                deps.add(self.last_writer[k])
            for rd in self.readers.get(k, ()):
                deps.add(rd)
        i = len(self.ops)
        if dma:
            idx = self.drr[engine] % DMA_POOL
            self.drr[engine] += 1
            before = self.dcount[engine][idx]
            self.dcount[engine][idx] = before + 16
            info = ("dma", idx, before)
            pos = 0
        else:
            info = None
            self.eng_count[engine] += 1
            pos = self.eng_count[engine]
        self.ops.append((engine, fn, deps, info, pos))
        for k in reads:
            self.readers.setdefault(k, []).append(i)
        for k in writes:
            self.last_writer[k] = i
            self.readers[k] = []
        return i

    def cc(self, fn):
        sem = self.ccsems[self.ncc]
        self.ncc += 1
        deps = set(self.fence_deps)
        i = len(self.ops)
        self.ops.append(("gpsimd", fn, deps, ("cc", sem), 0))
        return i

    def fence(self):
        deps = []
        last_eng = {}
        last_dma = {}
        last_cc = []
        for i, (e, fn, d, info, pos) in enumerate(self.ops):
            if info is None:
                last_eng[e] = i
            elif info[0] == "dma":
                last_dma[(e, info[1])] = i
            else:
                last_cc.append(i)
        deps.extend(last_eng.values())
        deps.extend(last_dma.values())
        deps.extend(last_cc[-4:])
        self.fence_deps = tuple(deps)

    def emit(self):
        nc = self.nc
        with nc.Block() as block:
            for ename in ENGS:
                self._emit_engine(block, ename)

    def _emit_engine(self, block, ename):
        prog = self

        def body(eng):
            wm = {}

            def wait(sem, val):
                if val > wm.get(id(sem), 0):
                    eng.wait_ge(sem, val)
                    wm[id(sem)] = val

            def do_waits(deps):
                for d in sorted(deps):
                    de, dfn, ddeps, dinfo, dpos = prog.ops[d]
                    if dinfo is not None and dinfo[0] == "dma":
                        wait(prog.dpool[de][dinfo[1]], dinfo[2] + 16)
                    elif dinfo is not None and dinfo[0] == "cc":
                        wait(dinfo[1], 1)
                    else:
                        if de == ename and ename == "tensor":
                            continue
                        wait(prog.sems[de], dpos)

            for i, (e, fn, deps, info, pos) in enumerate(prog.ops):
                if e != ename:
                    continue
                do_waits(deps)
                if info is not None and info[0] == "dma":
                    idx, before = info[1], info[2]
                    if before > 0:
                        wait(prog.dpool[ename][idx], before)
                    inst = fn(eng)
                    inst.then_inc(prog.dpool[ename][idx], 16)
                elif info is not None and info[0] == "cc":
                    inst = fn(eng)
                    inst.then_inc(info[1])
                else:
                    inst = fn(eng)
                    inst.then_inc(prog.sems[ename], 1)
            if ename == "sync":
                for q, pool in prog.dpool.items():
                    for idx, sem in enumerate(pool):
                        if prog.dcount[q][idx] > 0:
                            wait(sem, prog.dcount[q][idx])
                for i in range(prog.ncc):
                    wait(prog.ccsems[i], 1)

        getattr(block, ename)(body)


class Arena:

    def __init__(self, nc, ctx, cols_f32):
        self.t = ctx.enter_context(nc.sbuf_tensor("arena", [128, cols_f32], F32))
        self.cap = cols_f32
        self.off = 0
        self.peak = 0

    def reset(self, to=0):
        self.off = to

    def f32(self, parts, cols):
        o = self.off
        self.off += cols
        self.peak = max(self.peak, self.off)
        assert self.off <= self.cap, f"arena overflow {self.off}>{self.cap}"
        return self.t[0:parts, o:o + cols]

    def bf16(self, parts, cols):
        cf = (cols + 1) // 2
        o = self.off
        self.off += cf
        self.peak = max(self.peak, self.off)
        assert self.off <= self.cap, f"arena overflow {self.off}>{self.cap}"
        return self.t.bitcast(BF16)[0:parts, 2 * o:2 * o + cols]


class Rows:

    def __init__(self, nc, ctx, cols):
        self.t = ctx.enter_context(nc.sbuf_tensor("rows", [1, cols], F32))
        self.cap = cols
        self.off = 0

    def reset(self):
        self.off = 0

    def f32(self, cols):
        o = self.off
        self.off += cols
        assert self.off <= self.cap, "rows overflow"
        return self.t[0:1, o:o + cols]


def _f32(a):
    return np.ascontiguousarray(np.asarray(a, dtype=np.float32))


def _bf(a):
    return np.ascontiguousarray(np.asarray(a, dtype=np.float32).astype(BF))


def _rope_tables_np():
    freqs = (1.0 / (np.float32(10000.0) **
                    (np.arange(0, 128, 2, dtype=np.float32) / np.float32(128.0))))
    pos = np.arange(S, dtype=np.float32)
    emb = pos[:, None] * freqs[None, :32]
    return np.cos(emb).astype(np.float32).T.copy(), \
        np.sin(emb).astype(np.float32).T.copy()


def q_pack_perm8(r):
    perm = np.zeros(1024, np.int64)
    for c in range(4):
        for half in range(2):
            j = 2 * c + half
            h = 8 * r + j
            perm[c * 128 + half * 64: c * 128 + half * 64 + 64] = \
                h * 128 + np.arange(64)
    for g in range(2):
        for hh in range(4):
            j = 4 * g + hh
            h = 8 * r + j
            perm[(4 + g) * 128 + hh * 32:(4 + g) * 128 + hh * 32 + 32] = \
                h * 128 + 64 + 2 * np.arange(32)
    for g in range(2):
        for hh in range(4):
            j = 4 * g + hh
            h = 8 * r + j
            perm[(6 + g) * 128 + hh * 32:(6 + g) * 128 + hh * 32 + 32] = \
                h * 128 + 64 + 2 * np.arange(32) + 1
    return perm


def q_head_rows8(j):
    return ((j // 2) * 128 + (j % 2) * 64,
            (4 + j // 4) * 128 + (j % 4) * 32,
            (6 + j // 4) * 128 + (j % 4) * 32)


def stage_inputs(ins):
    sh = {}
    x = ins["L0_attn_x"]
    xf = _f32(x.reshape(T, D).T)

    sh["wq_down"] = _bf(ins["L0_attn_Wq_down"])
    wkv = np.asarray(ins["L0_attn_Wkv_down"], np.float32)
    lora = np.zeros((D, KVPP), np.float32)
    lora[:, :KVP] = wkv[:, :KVP]
    sh["wkv_lora"] = _bf(lora)
    wr = wkv[:, KVP:]
    sh["wkv_rope"] = _bf(np.concatenate([wr[:, 0::2], wr[:, 1::2]], axis=1))

    sh["gq"] = _f32(np.asarray(ins["L0_attn_gq"]).reshape(8, 128).T)
    gkv = np.zeros(KVPP, np.float32)
    gkv[:KVP] = np.asarray(ins["L0_attn_gkv"])
    sh["gkv"] = _f32(gkv.reshape(11, 128).T)
    sh["wo"] = _bf(ins["L0_attn_Wo"])

    cosT, sinT = _rope_tables_np()
    sh["cos4"] = _f32(np.tile(cosT, (4, 1)))
    sh["sin4"] = _f32(np.tile(sinT, (4, 1)))

    wkvu = np.asarray(ins["L0_attn_Wkv_up"], np.float32)
    wkvu_p = np.zeros((KVPP, 3072), np.float32)
    wkvu_p[:KVP] = wkvu
    per_head = wkvu_p.reshape(KVPP, H, 192)

    for L in ("L0", "L1"):
        gw = np.asarray(ins[f"{L}_ffn_gate_w"], np.float32)
        uw = np.asarray(ins[f"{L}_ffn_up_w"], np.float32)
        dw = np.asarray(ins[f"{L}_ffn_down_w"], np.float32)
        sh[f"{L}_gws"] = _bf(gw.reshape(44, 128, 16, 128).transpose(0, 3, 2, 1)
                              .reshape(44, 128, 2048))
        sh[f"{L}_uws"] = _bf(uw.reshape(44, 128, 16, 128).transpose(0, 3, 2, 1)
                              .reshape(44, 128, 2048))
        sh[f"{L}_dws"] = _bf(dw.reshape(16, 128, 44, 128).transpose(0, 3, 2, 1)
                              .reshape(16, 128, 5632))
        sh[f"{L}_gs"] = _f32(np.asarray(ins[f"{L}_ffn_gate_s"]).reshape(44, 128).T)
        sh[f"{L}_us"] = _f32(np.asarray(ins[f"{L}_ffn_up_s"]).reshape(44, 128).T)
        sh[f"{L}_ds"] = _f32(np.asarray(ins[f"{L}_ffn_down_s"]).reshape(16, 128).T)

    sh["ones_colb"] = np.ones((128, 1), BF)
    sh["ones_colf"] = np.ones((128, 1), np.float32)
    sh["ones_row"] = np.ones((1, 128), np.float32)
    sh["ones512"] = np.ones((1, 512), np.float32)
    sh["identd"] = np.eye(128).astype(BF)
    sh["ident8f"] = np.eye(8).astype(np.float32)
    kk2 = np.arange(128)[:, None]
    qq2 = np.arange(512)[None, :]
    me = np.zeros((128, 4, 512), np.float32)
    for jj in range(4):
        me[:, jj, :] = np.where(jj * 128 + kk2 > qq2, NEG, 0.0)
    sh["maskE"] = _f32(me.reshape(128, 2048))
    ml = np.zeros((128, 384), np.float32)
    ml[64:128, 0:128] = NEG
    ml[0:64, 256:384] = NEG
    sh["maskL"] = _f32(ml)
    sh["io_row"] = _f32(np.tile(np.arange(64, dtype=np.float32)[None, :], (128, 1)))
    sh["io_pp"] = _f32(np.arange(64, dtype=np.float32).reshape(64, 1))

    wq1 = np.asarray(ins["L1_attn_Wq"], np.float32)
    wk1 = np.asarray(ins["L1_attn_Wk"], np.float32)
    wv1 = np.asarray(ins["L1_attn_Wv"], np.float32)
    bq1 = np.asarray(ins["L1_attn_bq"], np.float32)
    bk1 = np.asarray(ins["L1_attn_bk"], np.float32)
    bv1 = np.asarray(ins["L1_attn_bv"], np.float32)

    i128 = np.arange(128, dtype=np.uint32)[:, None]
    cores = []
    for c in range(NCORES):
        b, r = c // 2, c % 2
        pc = {}
        pc["x_f"] = _f32(xf[:, c * TC:(c + 1) * TC])
        pc["wq_up"] = _bf(np.asarray(ins["L0_attn_Wq_up"], np.float32)[:, q_pack_perm8(r)])
        hs = slice(8 * r, 8 * r + 8)
        pc["wkv_up_k"] = _bf(per_head[:, hs, :64].reshape(KVPP, 512))
        pc["wkv_up_v"] = _bf(per_head[:, hs, 64:].reshape(KVPP, 1024))
        ps = slice(r * 2048, r * 2048 + 2048)
        pc["cs_own"] = _f32(np.concatenate([cosT[:, ps], sinT[:, ps]], 0))
        pc["sc_own"] = _f32(np.concatenate([sinT[:, ps], cosT[:, ps]], 0))
        cs = slice(1024 * r, 1024 * r + 1024)
        pc["wq1"] = _bf(wq1[:, cs])
        pc["wk1"] = _bf(wk1[:, cs])
        pc["wv1"] = _bf(wv1[:, cs])
        pc["bq1"] = _f32(bq1[cs].reshape(1, 1024))
        pc["bqpp"] = _f32(bq1[cs].reshape(8, 128).T)
        pc["bkpp"] = _f32(bk1[cs].reshape(8, 128).T)
        pc["bk1"] = _f32(bk1[cs].reshape(1, 1024))
        pc["bv1"] = _f32(bv1[cs].reshape(1, 1024))
        pc["idx_fcq"] = (r * 1024 + np.arange(8, dtype=np.uint32)[None, :] * 128
                         + i128).astype(np.uint32)
        pc["idx_fckv"] = (r * 1408 + np.arange(11, dtype=np.uint32)[None, :] * 128
                          + i128).astype(np.uint32)
        pc["idx_fkr"] = (r * 64 + np.arange(64, dtype=np.uint32)[:, None]
                         ).astype(np.uint32)
        pc["idx_h"] = ((8 * r + np.arange(8, dtype=np.uint32)[None, :]) * 128
                       + i128).astype(np.uint32)
        pc["idx_t"] = (r * 2048 + np.arange(16, dtype=np.uint32)[None, :] * 128
                       + i128).astype(np.uint32)
        pc["idx_c"] = (2 * (np.arange(16, dtype=np.uint32)[None, :] * 128 + i128)
                       + r).astype(np.uint32)
        cores.append(pc)
    return sh, cores


def em_AB(p, t, C, ar, rw):
    xsb = ar.bf16(128, 16 * TC)
    for tt in range(4):
        for k in range(16):
            p.op("gpsimd", lambda e, k=k, tt=tt: e.dma_start(
                out=xsb[:, k * TC + tt * 512: k * TC + (tt + 1) * 512],
                in_=t["x_f"][k * 128:(k + 1) * 128, tt * 512:(tt + 1) * 512]),
                writes=[f"x{k}_{tt}"], dma=True)
    a0 = ar.off
    wsb = ar.bf16(128, 16 * QP)
    craw = ar.f32(128, 8 * 512)
    sqs = ar.bf16(128, 512)
    outsb = ar.bf16(128, 8 * 512)
    rowbuf = rw.f32(3 * 512)
    for k in range(16):
        p.op("sync", lambda e, k=k: e.dma_start(
            out=wsb[:, k * QP:(k + 1) * QP], in_=t["wq_down"][k * 128:(k + 1) * 128, :]),
            writes=[f"A:w{k}"], dma=True)
    NT = TC // 512

    def epiA(m):
        bank = m % 4
        cp = p.op("vector", lambda e, m=m, bank=bank: e.tensor_copy(
            craw[:, m * 512:(m + 1) * 512], p.pb[bank][:, :]),
            reads=[f"pb{bank}"], writes=[f"A:craw{m}"])
        p.op("scalar", lambda e, bank=bank: e.activation(
            sqs[:, :], p.pb[bank][:, :], AF.Square),
            reads=[f"pb{bank}"], writes=["A:sqs"], after=[cp])
        p.op("tensor", lambda e, m=m: e.matmul(
            p.pb[4][0:1, :], C["onc"][:, 0:1], sqs[:, :],
            start=(m == 0), stop=(m == 7)),
            reads=["A:sqs", "onc"],
            writes=["pb4"] if m in (0, 7) else [])

    for tt in range(NT):
        t0 = tt * 512
        for m in range(8):
            bank = m % 4
            for k in range(16):
                p.op("tensor", lambda e, k=k, m=m, t0=t0, bank=bank: e.matmul(
                    p.pb[bank][:, :],
                    wsb[:, k * QP + m * 128: k * QP + (m + 1) * 128],
                    xsb[:, k * TC + t0: k * TC + t0 + 512],
                    start=(k == 0), stop=(k == 15)),
                    reads=[f"x{k}_{tt}", f"A:w{k}"],
                    writes=[f"pb{bank}"] if k in (0, 15) else [])
            if m > 0:
                epiA(m - 1)
        epiA(7)
        p.op("vector", lambda e: e.tensor_scalar(
            rowbuf[0:1, 0:512], p.pb[4][0:1, :], 1.0 / QP, EPS, ALU.mult, ALU.add),
            reads=["pb4"], writes=["A:row0"])
        p.op("scalar", lambda e: e.activation(
            rowbuf[0:1, 512:1024], rowbuf[0:1, 0:512], AF.Ln),
            reads=["A:row0"], writes=["A:row1"])
        p.op("scalar", lambda e: e.activation(
            rowbuf[0:1, 1024:1536], rowbuf[0:1, 512:1024], AF.Exp, scale=-0.5),
            reads=["A:row1"], writes=["A:row2"])
        p.op("tensor", lambda e: e.matmul(
            p.pb[5][:, :], C["onr"][0:1, 0:128], rowbuf[0:1, 1024:1536],
            start=True, stop=True),
            reads=["A:row2", "onr"], writes=["pb5"])
        for m in range(8):
            p.op("vector", lambda e, m=m: e.scalar_tensor_tensor(
                outsb[:, m * 512:(m + 1) * 512],
                craw[:, m * 512:(m + 1) * 512],
                C["gq"][:, m:m + 1], p.pb[5][:, :], ALU.mult, ALU.mult),
                reads=[f"A:craw{m}", "gq", "pb5"], writes=[f"A:out{m}"])
            p.op("gpsimd", lambda e, m=m, t0=t0: e.indirect_dma_start(
                t["cq_sh"][:, :],
                IndirectOffsetOnAxis(ap=C["idx_fcq"][:, m:m + 1], axis=0),
                outsb[:, m * 512:(m + 1) * 512], None, element_offset=t0),
                reads=[f"A:out{m}", "idx"], dma=True)
    p.fence()
    ar.reset(a0)
    wsb = ar.bf16(128, 16 * KVPP)
    wrs = ar.bf16(128, 16 * 64)
    css = ar.f32(64, 2048)
    scs = ar.f32(64, 2048)
    craw = ar.f32(128, 11 * 512)
    kraw = ar.f32(64, 2048)
    ksc1 = ar.f32(64, 1024)
    ksc2 = ar.f32(64, 1024)
    kout = ar.bf16(64, 2048)
    sqs = ar.bf16(128, 512)
    outsb = ar.bf16(128, 11 * 512)
    for k in range(16):
        p.op("sync", lambda e, k=k: e.dma_start(
            out=wsb[:, k * KVPP:(k + 1) * KVPP],
            in_=t["wkv_lora"][k * 128:(k + 1) * 128, :]),
            writes=[f"B:w{k}"], dma=True)
        p.op("sync", lambda e, k=k: e.dma_start(
            out=wrs[:, k * 64:(k + 1) * 64],
            in_=t["wkv_rope"][k * 128:(k + 1) * 128, :]),
            writes=[f"B:wr{k}"], dma=True)
    p.op("sync", lambda e: e.dma_start(out=css[:, :], in_=t["cs_own"]),
         writes=["B:cs"], dma=True)
    p.op("sync", lambda e: e.dma_start(out=scs[:, :], in_=t["sc_own"]),
         writes=["B:sc"], dma=True)
    rowbuf = rw.f32(3 * 512)
    def epiB(m):
        bank = m % 4
        cp = p.op("vector", lambda e, m=m, bank=bank: e.tensor_copy(
            craw[:, m * 512:(m + 1) * 512], p.pb[bank][:, :]),
            reads=[f"pb{bank}"], writes=[f"B:craw{m}"])
        p.op("scalar", lambda e, bank=bank: e.activation(
            sqs[:, :], p.pb[bank][:, :], AF.Square),
            reads=[f"pb{bank}"], writes=["B:sqs"], after=[cp])
        p.op("tensor", lambda e, m=m: e.matmul(
            p.pb[4][0:1, :], C["onc"][:, 0:1], sqs[:, :],
            start=(m == 0), stop=(m == 10)),
            reads=["B:sqs", "onc"],
            writes=["pb4"] if m in (0, 10) else [])

    for tt in range(NT):
        t0 = tt * 512
        for m in range(11):
            bank = m % 4
            for k in range(16):
                p.op("tensor", lambda e, k=k, m=m, t0=t0, bank=bank: e.matmul(
                    p.pb[bank][:, :],
                    wsb[:, k * KVPP + m * 128: k * KVPP + (m + 1) * 128],
                    xsb[:, k * TC + t0: k * TC + t0 + 512],
                    start=(k == 0), stop=(k == 15)),
                    reads=[f"x{k}_{tt}", f"B:w{k}"],
                    writes=[f"pb{bank}"] if k in (0, 15) else [])
            if m > 0:
                epiB(m - 1)
        for k in range(16):
            p.op("tensor", lambda e, k=k, t0=t0: e.matmul(
                p.pb[5][0:64, :],
                wrs[:, k * 64:(k + 1) * 64],
                xsb[:, k * TC + t0: k * TC + t0 + 512],
                start=(k == 0), stop=(k == 15)),
                reads=[f"x{k}_{tt}", f"B:wr{k}"],
                writes=["pb5"] if k in (0, 15) else [])
        epiB(10)
        p.op("vector", lambda e, t0=t0: e.tensor_copy(
            kraw[:, t0:t0 + 512], p.pb[5][0:64, :]),
            reads=["pb5"], writes=["B:kraw"])
        p.op("vector", lambda e: e.tensor_scalar(
            rowbuf[0:1, 0:512], p.pb[4][0:1, :], 1.0 / KVP, EPS, ALU.mult, ALU.add),
            reads=["pb4"], writes=["B:row0"])
        p.op("scalar", lambda e: e.activation(
            rowbuf[0:1, 512:1024], rowbuf[0:1, 0:512], AF.Ln),
            reads=["B:row0"], writes=["B:row1"])
        p.op("scalar", lambda e: e.activation(
            rowbuf[0:1, 1024:1536], rowbuf[0:1, 512:1024], AF.Exp, scale=-0.5),
            reads=["B:row1"], writes=["B:row2"])
        p.op("tensor", lambda e: e.matmul(
            p.pb[6][:, :], C["onr"][0:1, 0:128], rowbuf[0:1, 1024:1536],
            start=True, stop=True),
            reads=["B:row2", "onr"], writes=["pb6"])
        for m in range(11):
            p.op("vector", lambda e, m=m: e.scalar_tensor_tensor(
                outsb[:, m * 512:(m + 1) * 512],
                craw[:, m * 512:(m + 1) * 512],
                C["gkv"][:, m:m + 1], p.pb[6][:, :], ALU.mult, ALU.mult),
                reads=[f"B:craw{m}", "gkv", "pb6"], writes=[f"B:out{m}"])
            p.op("gpsimd", lambda e, m=m, t0=t0: e.indirect_dma_start(
                t["ckv_sh"][:, :],
                IndirectOffsetOnAxis(ap=C["idx_fckv"][:, m:m + 1], axis=0),
                outsb[:, m * 512:(m + 1) * 512], None, element_offset=t0),
                reads=[f"B:out{m}", "idx"], dma=True)
    for hh in range(2):
        c0 = hh * 1024
        cs_ = slice(c0, c0 + 1024)
        p.op("vector", lambda e, cs_=cs_: e.tensor_mul(
            ksc1[0:32, :], kraw[0:32, cs_], css[0:32, cs_]),
            reads=["B:kraw", "B:cs"], writes=["B:ksc1a"])
        p.op("vector", lambda e, cs_=cs_: e.tensor_mul(
            ksc1[32:64, :], kraw[32:64, cs_], css[32:64, cs_]),
            reads=["B:kraw", "B:cs"], writes=["B:ksc1b"])
        p.op("vector", lambda e, cs_=cs_: e.tensor_mul(
            ksc2[0:32, :], kraw[0:32, cs_], scs[0:32, cs_]),
            reads=["B:kraw", "B:sc"], writes=["B:ksc2a"])
        p.op("vector", lambda e, cs_=cs_: e.tensor_mul(
            ksc2[32:64, :], kraw[32:64, cs_], scs[32:64, cs_]),
            reads=["B:kraw", "B:sc"], writes=["B:ksc2b"])
        p.op("vector", lambda e, cs_=cs_: e.tensor_copy(kraw[0:32, cs_], ksc1[32:64, :]),
             reads=["B:ksc1b"], writes=["B:kraw"])
        p.op("vector", lambda e, cs_=cs_: e.tensor_sub(
            kout[0:32, cs_], ksc1[0:32, :], kraw[0:32, cs_]),
            reads=["B:ksc1a", "B:kraw"], writes=[f"B:ne{hh}"])
        p.op("vector", lambda e, cs_=cs_: e.tensor_copy(kraw[32:64, cs_], ksc2[0:32, :]),
             reads=["B:ksc2a"], writes=["B:kraw"])
        p.op("vector", lambda e, cs_=cs_: e.tensor_add(
            kout[32:64, cs_], ksc2[32:64, :], kraw[32:64, cs_]),
            reads=["B:ksc2b", "B:kraw"], writes=[f"B:no{hh}"])
    p.op("gpsimd", lambda e: e.indirect_dma_start(
        t["kr_sh"][:, :],
        IndirectOffsetOnAxis(ap=C["idx_fkr"][0:64, 0:1], axis=0),
        kout[0:64, :], None),
        reads=["B:ne0", "B:no0", "B:ne1", "B:no1", "idx"], dma=True)


def em_C(p, t, C, ar, rw):
    csb = ar.bf16(128, 8 * S)
    wsb = ar.bf16(128, 8 * QP)
    c4 = ar.f32(128, S)
    s4 = ar.f32(128, S)
    rraw = ar.f32(128, 4 * 512)
    rsc1 = ar.f32(128, 512)
    rsc2 = ar.f32(128, 512)
    outsb = ar.bf16(128, 8 * 512)
    for k in range(8):
        for pp in range(2):
            p.op("sync", lambda e, k=k, pp=pp: e.dma_start(
                out=csb[:, k * S + pp * 2048: k * S + pp * 2048 + 2048],
                in_=t["cq_sh"][pp * 1024 + k * 128: pp * 1024 + (k + 1) * 128, :]),
                writes=[f"C:c{k}"], dma=True)
        p.op("sync", lambda e, k=k: e.dma_start(
            out=wsb[:, k * QP:(k + 1) * QP],
            in_=t["wq_up"][k * 128:(k + 1) * 128, :]),
            writes=[f"C:w{k}"], dma=True)
    p.op("sync", lambda e: e.dma_start(out=c4[:, :], in_=t["cos4"]),
         writes=["C:cos"], dma=True)
    p.op("sync", lambda e: e.dma_start(out=s4[:, :], in_=t["sin4"]),
         writes=["C:sin"], dma=True)
    for tt in range(S // 512):
        t0 = tt * 512
        for m in range(8):
            bank = m % 4
            for k in range(8):
                p.op("tensor", lambda e, k=k, m=m, t0=t0, bank=bank: e.matmul(
                    p.pb[bank][:, :],
                    wsb[:, k * QP + m * 128: k * QP + (m + 1) * 128],
                    csb[:, k * S + t0: k * S + t0 + 512],
                    start=(k == 0), stop=(k == 7)),
                    reads=[f"C:c{k}", f"C:w{k}"],
                    writes=[f"pb{bank}"] if k in (0, 7) else [])
            if m < 4:
                p.op("vector", lambda e, m=m, bank=bank: e.tensor_copy(
                    outsb[:, m * 512:(m + 1) * 512], p.pb[bank][:, :]),
                    reads=[f"pb{bank}"], writes=[f"C:out{m}"])
            else:
                p.op("vector", lambda e, m=m, bank=bank: e.tensor_copy(
                    rraw[:, (m - 4) * 512:(m - 3) * 512], p.pb[bank][:, :]),
                    reads=[f"pb{bank}"], writes=[f"C:rr{m - 4}"])
        for g in range(2):
            ek, ok = f"C:rr{g}", f"C:rr{2 + g}"
            E = rraw[:, g * 512:(g + 1) * 512]
            O = rraw[:, (2 + g) * 512:(3 + g) * 512]
            p.op("vector", lambda e, E=E, t0=t0: e.tensor_mul(
                rsc1[:, :], E, c4[:, t0:t0 + 512]),
                reads=[ek, "C:cos"], writes=["C:rs1"])
            p.op("vector", lambda e, O=O, t0=t0: e.tensor_mul(
                rsc2[:, :], O, s4[:, t0:t0 + 512]),
                reads=[ok, "C:sin"], writes=["C:rs2"])
            p.op("vector", lambda e, g=g: e.tensor_sub(
                outsb[:, (4 + g) * 512:(5 + g) * 512], rsc1[:, :], rsc2[:, :]),
                reads=["C:rs1", "C:rs2"], writes=[f"C:out{4 + g}"])
            p.op("vector", lambda e, O=O, t0=t0: e.tensor_mul(
                rsc1[:, :], O, c4[:, t0:t0 + 512]),
                reads=[ok, "C:cos"], writes=["C:rs1"])
            p.op("vector", lambda e, E=E, t0=t0: e.tensor_mul(
                rsc2[:, :], E, s4[:, t0:t0 + 512]),
                reads=[ek, "C:sin"], writes=["C:rs2"])
            p.op("vector", lambda e, g=g: e.tensor_add(
                outsb[:, (6 + g) * 512:(7 + g) * 512], rsc1[:, :], rsc2[:, :]),
                reads=["C:rs1", "C:rs2"], writes=[f"C:out{6 + g}"])
        for m in range(8):
            p.op("sync", lambda e, m=m, t0=t0: e.dma_start(
                out=t["q0"][m * 128:(m + 1) * 128, t0:t0 + 512],
                in_=outsb[:, m * 512:(m + 1) * 512]),
                reads=[f"C:out{m}"], dma=True)


def em_D(p, t, C, ar, rw):
    csb = ar.bf16(128, 11 * S)
    wks = ar.bf16(128, 11 * 512)
    wvs = ar.bf16(128, 11 * 1024)
    outsb = ar.bf16(128, 4 * 512)
    vosb = ar.bf16(128, 2 * 512)
    for k in range(11):
        for pp in range(2):
            p.op("sync", lambda e, k=k, pp=pp: e.dma_start(
                out=csb[:, k * S + pp * 2048: k * S + pp * 2048 + 2048],
                in_=t["ckv_sh"][pp * 1408 + k * 128: pp * 1408 + (k + 1) * 128, :]),
                writes=[f"D:c{k}"], dma=True)
        p.op("sync", lambda e, k=k: e.dma_start(
            out=wks[:, k * 512:(k + 1) * 512],
            in_=t["wkv_up_k"][k * 128:(k + 1) * 128, :]),
            writes=[f"D:wk{k}"], dma=True)
        p.op("sync", lambda e, k=k: e.dma_start(
            out=wvs[:, k * 1024:(k + 1) * 1024],
            in_=t["wkv_up_v"][k * 128:(k + 1) * 128, :]),
            writes=[f"D:wv{k}"], dma=True)
    for tt in range(S // 512):
        t0 = tt * 512
        for m in range(4):
            bank = m % 4
            for k in range(11):
                p.op("tensor", lambda e, k=k, m=m, t0=t0, bank=bank: e.matmul(
                    p.pb[bank][:, :],
                    wks[:, k * 512 + m * 128: k * 512 + (m + 1) * 128],
                    csb[:, k * S + t0: k * S + t0 + 512],
                    start=(k == 0), stop=(k == 10)),
                    reads=[f"D:c{k}", f"D:wk{k}"],
                    writes=[f"pb{bank}"] if k in (0, 10) else [])
            p.op("vector", lambda e, m=m, bank=bank: e.tensor_copy(
                outsb[:, m * 512:(m + 1) * 512], p.pb[bank][:, :]),
                reads=[f"pb{bank}"], writes=[f"D:out{m}"])
            p.op("sync", lambda e, m=m, t0=t0: e.dma_start(
                out=t["knope"][m * 128:(m + 1) * 128, t0:t0 + 512],
                in_=outsb[:, m * 512:(m + 1) * 512]),
                reads=[f"D:out{m}"], dma=True)
    for ttile in range(S // 128):
        for v4 in range(2):
            bank = 4 + (2 * ttile + v4) % 4
            for k in range(11):
                p.op("tensor", lambda e, k=k, ttile=ttile, v4=v4, bank=bank: e.matmul(
                    p.pb[bank][:, :],
                    csb[:, k * S + ttile * 128: k * S + (ttile + 1) * 128],
                    wvs[:, k * 1024 + v4 * 512: k * 1024 + (v4 + 1) * 512],
                    start=(k == 0), stop=(k == 10)),
                    reads=[f"D:c{k}", f"D:wv{k}"],
                    writes=[f"pb{bank}"] if k in (0, 10) else [])
            p.op("vector", lambda e, v4=v4, bank=bank: e.tensor_copy(
                vosb[:, v4 * 512:(v4 + 1) * 512], p.pb[bank][:, :]),
                reads=[f"pb{bank}"], writes=[f"D:vo{v4}"])
            p.op("sync", lambda e, ttile=ttile, v4=v4: e.dma_start(
                out=t["vt"][ttile * 128:(ttile + 1) * 128, v4 * 512:(v4 + 1) * 512],
                in_=vosb[:, v4 * 512:(v4 + 1) * 512]),
                reads=[f"D:vo{v4}"], dma=True)


def _bound_c(p, q_key, q_sb, sq_sb, onc, srow, slot, pf):
    for c8 in range(8):
        sb0 = (c8 % 2) * 512
        bk = 6 + c8 % 2
        p.op("scalar", lambda e, c8=c8, sb0=sb0: e.activation(
            sq_sb[:, sb0:sb0 + 512], q_sb[:, c8 * 512:(c8 + 1) * 512], AF.Square),
            reads=[q_key], writes=[f"{pf}:bsq{c8 % 2}"])
        p.op("tensor", lambda e, sb0=sb0, bk=bk: e.matmul(
            p.pb[bk][0:1, 0:512], onc[:, 0:1], sq_sb[:, sb0:sb0 + 512],
            start=True, stop=True),
            reads=[f"{pf}:bsq{c8 % 2}", "onc"], writes=[f"pb{bk}"])
        p.op("vector", lambda e, c8=c8, slot=slot, bk=bk: e.reduce_max(
            srow[0:1, 16 + slot * 8 + c8:17 + slot * 8 + c8],
            p.pb[bk][0:1, 0:512], axis=AX),
            reads=[f"pb{bk}"], writes=[f"{pf}:m8_{slot}_{c8}"])
    p.op("vector", lambda e, slot=slot: e.reduce_max(
        srow[0:1, slot:slot + 1],
        srow[0:1, 16 + slot * 8:16 + slot * 8 + 8], axis=AX),
        reads=[f"{pf}:m8_{slot}_{c8}" for c8 in range(8)], writes=[f"{pf}:mx{slot}"])


def _negc_from_bounds(p, C, srow, negc_col, pf, pr=0):
    p.op("vector", lambda e: e.tensor_mul(
        srow[0:1, 2:3], srow[0:1, 0:1], srow[0:1, 1:2]),
        reads=[f"{pf}:mx0", f"{pf}:mx1"], writes=[f"{pf}:prod"])
    p.op("scalar", lambda e: e.activation(
        srow[0:1, 3:4], srow[0:1, 2:3], AF.Ln),
        reads=[f"{pf}:prod"], writes=[f"{pf}:lnp"])
    p.op("scalar", lambda e: e.activation(
        srow[0:1, 4:5], srow[0:1, 3:4], AF.Exp, scale=0.5),
        reads=[f"{pf}:lnp"], writes=[f"{pf}:sqp"])
    p.op("vector", lambda e: e.tensor_scalar_mul(
        srow[0:1, 5:6], srow[0:1, 4:5], -SCALE),
        reads=[f"{pf}:sqp"], writes=[f"{pf}:negc_row"])
    p.op("tensor", lambda e: e.matmul(
        p.pb[6][:, 0:1], C["onr"][0:1, 0:128], srow[0:1, 5:6],
        start=True, stop=True),
        reads=[f"{pf}:negc_row", "onr"], writes=["pb6"])
    p.op("vector", lambda e: e.tensor_copy(negc_col[:, 0:1], p.pb[6][:, 0:1]),
         reads=["pb6"], writes=[f"{pf}:negc{pr}"])


def em_E(p, t, C, ar, rw):
    q_sb = ar.bf16(128, 2 * S)
    k_sb = ar.bf16(128, 2 * S)
    vt_sb = ar.bf16(128, 2 * S)
    msk = ar.f32(128, 2048)
    sq_sb = ar.bf16(128, 2 * 512)
    negc = ar.f32(128, 2)
    probs = ar.bf16(128, 4 * 512)
    pacc = ar.f32(128, 4 * 512)
    rbc = ar.f32(128, 512)
    ctxo = ar.bf16(128, 512)
    srow = rw.f32(80)
    rrow = rw.f32(512)
    p.op("sync", lambda e: e.dma_start(out=msk[:, :], in_=t["maskE"]),
         writes=["E:msk"], dma=True)

    def loads(j):
        pr = j % 2
        o = pr * S
        nof, evf, odf = q_head_rows8(j)
        p.op("sync", lambda e, nof=nof, o=o: e.dma_start(
            out=q_sb[0:64, o:o + S], in_=t["q0"][nof:nof + 64, :]),
            writes=[f"E:q{pr}"], dma=True)
        p.op("sync", lambda e, evf=evf, o=o: e.dma_start(
            out=q_sb[64:96, o:o + S], in_=t["q0"][evf:evf + 32, :]),
            writes=[f"E:q{pr}"], dma=True)
        p.op("sync", lambda e, odf=odf, o=o: e.dma_start(
            out=q_sb[96:128, o:o + S], in_=t["q0"][odf:odf + 32, :]),
            writes=[f"E:q{pr}"], dma=True)
        p.op("sync", lambda e, j=j, o=o: e.dma_start(
            out=k_sb[0:64, o:o + S], in_=t["knope"][j * 64:(j + 1) * 64, :]),
            writes=[f"E:k{pr}"], dma=True)
        for pp in range(2):
            p.op("sync", lambda e, pp=pp, o=o: e.dma_start(
                out=k_sb[64:96, o + pp * 2048: o + (pp + 1) * 2048],
                in_=t["kr_sh"][pp * 64: pp * 64 + 32, :]),
                writes=[f"E:k{pr}"], dma=True)
            p.op("sync", lambda e, pp=pp, o=o: e.dma_start(
                out=k_sb[96:128, o + pp * 2048: o + (pp + 1) * 2048],
                in_=t["kr_sh"][pp * 64 + 32: pp * 64 + 64, :]),
                writes=[f"E:k{pr}"], dma=True)
        for g in range(8):
            p.op("sync", lambda e, j=j, g=g, o=o: e.dma_start(
                out=vt_sb[:, o + g * 512: o + (g + 1) * 512].rearrange(
                    "p (a d) -> p a d", d=128),
                in_=t["vt"][g * 512:(g + 1) * 512, j * 128:(j + 1) * 128].rearrange(
                    "(a p) d -> p a d", p=128)),
                writes=[f"E:v{pr}"], dma=True)

    loads(0)
    for j in range(8):
        pr = j % 2
        o = pr * S
        if j < 7:
            loads(j + 1)
        _bound_c(p, f"E:q{pr}", q_sb[:, o:o + S], sq_sb, C["onc"], srow, 0, "E")
        _bound_c(p, f"E:k{pr}", k_sb[:, o:o + S], sq_sb, C["onc"], srow, 1, "E")
        _negc_from_bounds(p, C, srow, negc[:, pr:pr + 1], "E", pr)

        def finkb(sl, kb, nkb, dn, pv):
            b0 = kb % 2
            ps = kb % 4
            jj = kb - 4 * sl
            if jj >= 0:
                p.op("vector", lambda e, jj=jj, b0=b0: e.tensor_add(
                    p.pb[b0][:, :], p.pb[b0][:, :], msk[:, jj * 512:(jj + 1) * 512]),
                    reads=[f"pb{b0}", "E:msk"], writes=[f"pb{b0}"])
            p.op("scalar", lambda e, b0=b0, ps=ps, pr=pr: e.activation(
                probs[:, ps * 512:(ps + 1) * 512], p.pb[b0][:, :], AF.Exp,
                bias=negc[:, pr:pr + 1], scale=SCALE),
                reads=[f"pb{b0}", f"E:negc{pr}"], writes=[f"E:probs{ps}"])
            sp = sl % 2
            pc = 2 * sp + kb % 2
            if kb < 2:
                p.op("gpsimd", lambda e, ps=ps, pc=pc: e.tensor_copy(
                    pacc[:, pc * 512:(pc + 1) * 512],
                    probs[:, ps * 512:(ps + 1) * 512]),
                    reads=[f"E:probs{ps}"], writes=[f"E:pacc{pc}"])
            else:
                p.op("gpsimd", lambda e, ps=ps, pc=pc: e.tensor_add(
                    pacc[:, pc * 512:(pc + 1) * 512],
                    pacc[:, pc * 512:(pc + 1) * 512],
                    probs[:, ps * 512:(ps + 1) * 512]),
                    reads=[f"E:probs{ps}", f"E:pacc{pc}"], writes=[f"E:pacc{pc}"])
            p.op("tensor", lambda e, kb=kb, ps=ps, nkb=nkb, pv=pv, o=o: e.matmul(
                p.pb[pv][:, :], vt_sb[:, o + kb * 128: o + (kb + 1) * 128],
                probs[:, ps * 512:(ps + 1) * 512],
                start=(kb == 0), stop=(kb == nkb - 1)),
                reads=[f"E:probs{ps}", f"E:v{pr}"],
                writes=[f"pb{pv}"] if kb in (0, nkb - 1) else [])
            if kb == nkb - 1:
                p.op("tensor", lambda e, dn=dn, sp=sp: e.matmul(
                    p.pb[dn][0:1, :], C["oncf"][:, 0:1],
                    pacc[:, (2 * sp) * 512:(2 * sp + 1) * 512],
                    start=True, stop=False),
                    reads=[f"E:pacc{2 * sp}", "oncf"], writes=[f"pb{dn}"])
                p.op("tensor", lambda e, dn=dn, sp=sp: e.matmul(
                    p.pb[dn][0:1, :], C["oncf"][:, 0:1],
                    pacc[:, (2 * sp + 1) * 512:(2 * sp + 2) * 512],
                    start=False, stop=True),
                    reads=[f"E:pacc{2 * sp + 1}", "oncf"], writes=[f"pb{dn}"])

        def slepi(sl, dn, pv):
            p.op("vector", lambda e, dn=dn: e.reciprocal(rrow[0:1, :], p.pb[dn][0:1, :]),
                 reads=[f"pb{dn}"], writes=["E:rrow"])
            p.op("tensor", lambda e, dn=dn: e.matmul(
                p.pb[dn][:, :], C["onr"][0:1, 0:128], rrow[0:1, :],
                start=True, stop=True),
                reads=["E:rrow", "onr"], writes=[f"pb{dn}"])
            p.op("vector", lambda e, dn=dn: e.tensor_copy(rbc[:, :], p.pb[dn][:, :]),
                 reads=[f"pb{dn}"], writes=["E:rbc"])
            p.op("vector", lambda e, pv=pv: e.tensor_mul(ctxo[:, :], p.pb[pv][:, :], rbc[:, :]),
                 reads=[f"pb{pv}", "E:rbc"], writes=["E:ctxo"])
            p.op("gpsimd", lambda e, j=j, sl=sl: e.indirect_dma_start(
                t["ctx0_sh"][:, :],
                IndirectOffsetOnAxis(ap=C["idx_h"][:, j:j + 1], axis=0),
                ctxo[:, :], None, element_offset=sl * 512),
                reads=["E:ctxo", "idx"], dma=True)

        prev = None
        for sl in range(8):
            nkb = 4 * sl + 4
            dn = 4 + (sl % 2) * 2
            pv = 5 + (sl % 2) * 2
            for kb in range(nkb):
                b0 = kb % 2
                p.op("tensor", lambda e, sl=sl, kb=kb, b0=b0, o=o: e.matmul(
                    p.pb[b0][:, :],
                    k_sb[:, o + kb * 128: o + (kb + 1) * 128],
                    q_sb[:, o + sl * 512: o + (sl + 1) * 512],
                    start=True, stop=True),
                    reads=[f"E:k{pr}", f"E:q{pr}"], writes=[f"pb{b0}"])
                if kb == 2 and prev is not None:
                    slepi(*prev)
                if kb > 0:
                    finkb(sl, kb - 1, nkb, dn, pv)
            finkb(sl, nkb - 1, nkb, dn, pv)
            prev = (sl, dn, pv)
        slepi(*prev)


def _ln_apply(p, C, pf, u1, nchunk, dim, outfn, srow, t1, outs):
    A_ = srow[0:1, 0:512]
    Bm = srow[0:1, 512:1024]
    Cc = srow[0:1, 1024:1536]
    p.op("vector", lambda e: e.tensor_scalar(A_, p.pb[6][0:1, :], 1.0 / dim, None, ALU.mult),
         reads=["pb6"], writes=[f"{pf}:lnA"])
    p.op("vector", lambda e: e.tensor_mul(Bm, A_, A_), reads=[f"{pf}:lnA"],
         writes=[f"{pf}:lnB"])
    p.op("vector", lambda e: e.tensor_scalar(Cc, p.pb[7][0:1, :], 1.0 / dim, EPS,
                                             ALU.mult, ALU.add),
         reads=["pb7"], writes=[f"{pf}:lnC"])
    p.op("vector", lambda e: e.tensor_sub(Cc, Cc, Bm),
         reads=[f"{pf}:lnB", f"{pf}:lnC"], writes=[f"{pf}:lnC"])
    p.op("scalar", lambda e: e.activation(Cc, Cc, AF.Ln),
         reads=[f"{pf}:lnC"], writes=[f"{pf}:lnC"])
    p.op("scalar", lambda e: e.activation(Cc, Cc, AF.Exp, scale=-0.5),
         reads=[f"{pf}:lnC"], writes=[f"{pf}:lnC"])
    p.op("vector", lambda e: e.tensor_mul(Bm, A_, Cc),
         reads=[f"{pf}:lnA", f"{pf}:lnC"], writes=[f"{pf}:lnB"])
    p.op("vector", lambda e: e.tensor_scalar_mul(Bm, Bm, -1.0),
         reads=[f"{pf}:lnB"], writes=[f"{pf}:lnB"])
    p.op("tensor", lambda e: e.matmul(
        p.pb[4][:, :], C["onr"][0:1, 0:128], Cc, start=True, stop=True),
        reads=[f"{pf}:lnC", "onr"], writes=["pb4"])
    p.op("tensor", lambda e: e.matmul(
        p.pb[5][:, :], C["onr"][0:1, 0:128], Bm, start=True, stop=True),
        reads=[f"{pf}:lnB", "onr"], writes=["pb5"])
    for m in range(nchunk):
        p.op("vector", lambda e, m=m: e.tensor_mul(
            t1[:, :], u1[:, m * 512:(m + 1) * 512], p.pb[4][:, :]),
            reads=[f"{pf}:u1_{m}", "pb4"], writes=[f"{pf}:t1"])
        p.op("vector", lambda e: e.tensor_add(outs[:, :], t1[:, :], p.pb[5][:, :]),
             reads=[f"{pf}:t1", "pb5"], writes=[f"{pf}:outs"])
        outfn(m, outs)


def em_F(p, t, C, ar, rw):
    ctx0v = t["ctx0_sh"].rearrange("h (p w) -> (h p) w", p=2)
    csb = ar.bf16(128, 16 * TC)
    wsb = ar.bf16(128, 16 * D)
    xc = ar.f32(128, 2 * 512)
    u1 = ar.f32(128, 16 * 512)
    sq = ar.bf16(128, 512)
    accx = ar.f32(128, 512)
    accsq = ar.f32(128, 512)
    t1 = ar.f32(128, 512)
    outs = ar.f32(128, 512)
    srow = rw.f32(3 * 512)
    for k in range(16):
        p.op("gpsimd", lambda e, k=k: e.indirect_dma_start(
            csb[:, k * TC:(k + 1) * TC], None,
            ctx0v[:, :], IndirectOffsetOnAxis(ap=C["idx_c"][:, k:k + 1], axis=0)),
            writes=[f"F:c{k}"], reads=["idx"], dma=True)
        p.op("sync", lambda e, k=k: e.dma_start(
            out=wsb[:, k * D:(k + 1) * D], in_=t["wo"][k * 128:(k + 1) * 128, :]),
            writes=[f"F:w{k}"], dma=True)
    def epiF(m):
        bank = m % 4
        xb = m % 2
        p.op("vector", lambda e, m=m, bank=bank, xb=xb: e.tensor_add(
            u1[:, m * 512:(m + 1) * 512], p.pb[bank][:, :],
            xc[:, xb * 512:(xb + 1) * 512]),
            reads=[f"pb{bank}", f"F:xc{xb}"], writes=[f"F:u1_{m}"])
        if m == 0:
            p.op("gpsimd", lambda e: e.tensor_copy(accx[:, :], u1[:, 0:512]),
                 reads=["F:u1_0"], writes=["F:accx"])
            p.op("scalar", lambda e: e.activation(
                accsq[:, :], u1[:, 0:512], AF.Square),
                reads=["F:u1_0"], writes=["F:accsq"])
        else:
            p.op("gpsimd", lambda e, m=m: e.tensor_add(
                accx[:, :], accx[:, :], u1[:, m * 512:(m + 1) * 512]),
                reads=[f"F:u1_{m}", "F:accx"], writes=["F:accx"])
            p.op("scalar", lambda e, m=m: e.activation(
                sq[:, :], u1[:, m * 512:(m + 1) * 512], AF.Square),
                reads=[f"F:u1_{m}"], writes=["F:sq"])
            p.op("gpsimd", lambda e: e.tensor_add(accsq[:, :], accsq[:, :], sq[:, :]),
                 reads=["F:sq", "F:accsq"], writes=["F:accsq"])
        if m == 15:
            p.op("tensor", lambda e: e.matmul(
                p.pb[6][0:1, :], C["oncf"][:, 0:1], accx[:, :],
                start=True, stop=True),
                reads=["F:accx", "oncf"], writes=["pb6"])
            p.op("tensor", lambda e: e.matmul(
                p.pb[7][0:1, :], C["oncf"][:, 0:1], accsq[:, :],
                start=True, stop=True),
                reads=["F:accsq", "oncf"], writes=["pb7"])

    for tt in range(TC // 512):
        t0 = tt * 512
        for m in range(16):
            bank = m % 4
            xb = m % 2
            p.op("sync", lambda e, m=m, t0=t0, xb=xb: e.dma_start(
                out=xc[:, xb * 512:(xb + 1) * 512],
                in_=t["x_f"][m * 128:(m + 1) * 128, t0:t0 + 512]),
                writes=[f"F:xc{xb}"], dma=True)
            for k in range(16):
                p.op("tensor", lambda e, k=k, m=m, t0=t0, bank=bank: e.matmul(
                    p.pb[bank][:, :],
                    wsb[:, k * D + m * 128: k * D + (m + 1) * 128],
                    csb[:, k * TC + t0: k * TC + t0 + 512],
                    start=(k == 0), stop=(k == 15)),
                    reads=[f"F:c{k}", f"F:w{k}"],
                    writes=[f"pb{bank}"] if k in (0, 15) else [])
            if m > 0:
                epiF(m - 1)
        epiF(15)

        def outfn(m, o, t0=t0):
            p.op("sync", lambda e, m=m, t0=t0, o=o: e.dma_start(
                out=t["x1"][m * 128:(m + 1) * 128, t0:t0 + 512], in_=o[:, :]),
                reads=["F:outs"], dma=True)
        _ln_apply(p, C, "F", u1, 16, D, outfn, srow, t1, outs)


def em_G(p, t, C, ar, rw, L, xin, xout, scat):
    pf = f"G{L}"
    TH = 512
    xb_sb = ar.bf16(128, 2 * 16 * TH)
    gw = ar.bf16(128, 2 * 2048)
    uw = ar.bf16(128, 2 * 2048)
    dw = ar.bf16(128, 2 * 5632)
    gssb = ar.f32(128, 44)
    ussb = ar.f32(128, 44)
    dssb = ar.f32(128, 16)
    g_sb = ar.f32(128, 2 * 512)
    u_sb = ar.f32(128, 2 * 512)
    hdn = ar.bf16(128, 2 * 44 * TH)
    xc = ar.f32(128, 2 * 512)
    u2 = ar.f32(128, 2 * 512)
    u2t = ar.f32(128, 2 * 512)
    sq = ar.bf16(128, 512)
    accx = ar.f32(128, 512)
    accsq = ar.f32(128, 512)
    rsb = ar.f32(128, 512)
    bsb2 = ar.f32(128, 512)
    outs = ar.f32(128, 512)
    outs_bf = ar.bf16(128, 512)
    srow = rw.f32(3 * 512)
    p.op("sync", lambda e: e.dma_start(out=gssb[:, :], in_=t[f"{L}_gs"]),
         writes=[f"{pf}:gs"], dma=True)
    p.op("sync", lambda e: e.dma_start(out=ussb[:, :], in_=t[f"{L}_us"]),
         writes=[f"{pf}:us"], dma=True)
    p.op("sync", lambda e: e.dma_start(out=dssb[:, :], in_=t[f"{L}_ds"]),
         writes=[f"{pf}:ds"], dma=True)

    def load_xb(q):
        qp = q % 2
        h0 = q * TH
        for k in range(16):
            p.op("gpsimd", lambda e, k=k, h0=h0, qp=qp: e.dma_start(
                out=xb_sb[:, qp * 16 * TH + k * TH: qp * 16 * TH + (k + 1) * TH],
                in_=xin[k * 128:(k + 1) * 128, h0:h0 + TH]),
                writes=[f"{pf}:xb{qp}_{k}"], dma=True)

    def gate_up(q, i):
        qp = q % 2
        wsl = i % 2
        bg = i % 2
        bu = 2 + i % 2
        p.op("scalar", lambda e, i=i, wsl=wsl: e.dma_start(
            out=gw[:, wsl * 2048:(wsl + 1) * 2048], in_=t[f"{L}_gws"][i]),
            writes=[f"{pf}:gw{wsl}"], dma=True)
        p.op("gpsimd", lambda e, i=i, wsl=wsl: e.dma_start(
            out=uw[:, wsl * 2048:(wsl + 1) * 2048], in_=t[f"{L}_uws"][i]),
            writes=[f"{pf}:uw{wsl}"], dma=True)
        for k in range(16):
            p.op("tensor", lambda e, k=k, wsl=wsl, qp=qp, bg=bg: e.matmul(
                p.pb[bg][:, :],
                gw[:, wsl * 2048 + k * 128: wsl * 2048 + (k + 1) * 128],
                xb_sb[:, qp * 16 * TH + k * TH: qp * 16 * TH + (k + 1) * TH],
                start=(k == 0), stop=(k == 15)),
                reads=[f"{pf}:gw{wsl}", f"{pf}:xb{qp}_{k}"],
                writes=[f"pb{bg}"] if k in (0, 15) else [])
        for k in range(16):
            p.op("tensor", lambda e, k=k, wsl=wsl, qp=qp, bu=bu: e.matmul(
                p.pb[bu][:, :],
                uw[:, wsl * 2048 + k * 128: wsl * 2048 + (k + 1) * 128],
                xb_sb[:, qp * 16 * TH + k * TH: qp * 16 * TH + (k + 1) * TH],
                start=(k == 0), stop=(k == 15)),
                reads=[f"{pf}:uw{wsl}", f"{pf}:xb{qp}_{k}"],
                writes=[f"pb{bu}"] if k in (0, 15) else [])
        p.op("scalar", lambda e, i=i, bg=bg: e.activation(
            g_sb[:, bg * 512:(bg + 1) * 512], p.pb[bg][:, :], AF.Silu,
            scale=gssb[:, i:i + 1]),
            reads=[f"pb{bg}", f"{pf}:gs"], writes=[f"{pf}:g{bg}"])
        p.op("vector", lambda e, i=i, bg=bg, bu=bu: e.tensor_scalar_mul(
            u_sb[:, bg * 512:(bg + 1) * 512], p.pb[bu][:, :],
            ussb[:, i:i + 1]),
            reads=[f"pb{bu}", f"{pf}:us"], writes=[f"{pf}:u{bg}"])
        p.op("vector", lambda e, i=i, qp=qp, bg=bg: e.tensor_mul(
            hdn[:, qp * 44 * TH + i * TH: qp * 44 * TH + (i + 1) * TH],
            g_sb[:, bg * 512:(bg + 1) * 512], u_sb[:, bg * 512:(bg + 1) * 512]),
            reads=[f"{pf}:g{bg}", f"{pf}:u{bg}"], writes=[f"{pf}:hdn{qp}_{i}"])

    def down_f(q, f):
        qp = q % 2
        h0 = q * TH
        wsl = f % 2
        bk = 4 + f % 2
        p.op("sync", lambda e, f=f, wsl=wsl: e.dma_start(
            out=dw[:, wsl * 5632:(wsl + 1) * 5632], in_=t[f"{L}_dws"][f]),
            writes=[f"{pf}:dw{wsl}"], dma=True)
        p.op("sync", lambda e, f=f, h0=h0, wsl=wsl: e.dma_start(
            out=xc[:, wsl * 512:(wsl + 1) * 512],
            in_=xin[f * 128:(f + 1) * 128, h0:h0 + TH]),
            writes=[f"{pf}:xc{wsl}"], dma=True)
        for ik in range(44):
            p.op("tensor", lambda e, ik=ik, wsl=wsl, qp=qp, bk=bk: e.matmul(
                p.pb[bk][:, :],
                dw[:, wsl * 5632 + ik * 128: wsl * 5632 + (ik + 1) * 128],
                hdn[:, qp * 44 * TH + ik * TH: qp * 44 * TH + (ik + 1) * TH],
                start=(ik == 0), stop=(ik == 43)),
                reads=[f"{pf}:dw{wsl}", f"{pf}:hdn{qp}_{ik}"],
                writes=[f"pb{bk}"] if ik in (0, 43) else [])

    def down_epi(q, f):
        h0 = q * TH
        wsl = f % 2
        bk = 4 + f % 2
        p.op("vector", lambda e, f=f, wsl=wsl, bk=bk: e.scalar_tensor_tensor(
            u2[:, wsl * 512:(wsl + 1) * 512], p.pb[bk][:, :],
            dssb[:, f:f + 1], xc[:, wsl * 512:(wsl + 1) * 512],
            ALU.mult, ALU.add),
            reads=[f"pb{bk}", f"{pf}:ds", f"{pf}:xc{wsl}"],
            writes=[f"{pf}:u2_{wsl}"])
        p.op("gpsimd", lambda e, f=f, h0=h0, wsl=wsl: e.dma_start(
            out=t["u2d"][f * 128:(f + 1) * 128, h0:h0 + TH],
            in_=u2[:, wsl * 512:(wsl + 1) * 512]),
            reads=[f"{pf}:u2_{wsl}"], writes=[f"{pf}:u2d_{q % 2}_{f}"], dma=True)
        if f == 0:
            p.op("gpsimd", lambda e, wsl=wsl: e.tensor_copy(
                accx[:, :], u2[:, wsl * 512:(wsl + 1) * 512]),
                reads=[f"{pf}:u2_{wsl}"], writes=[f"{pf}:accx"])
            p.op("scalar", lambda e, wsl=wsl: e.activation(
                accsq[:, :], u2[:, wsl * 512:(wsl + 1) * 512], AF.Square),
                reads=[f"{pf}:u2_{wsl}"], writes=[f"{pf}:accsq"])
        else:
            p.op("gpsimd", lambda e, wsl=wsl: e.tensor_add(
                accx[:, :], accx[:, :], u2[:, wsl * 512:(wsl + 1) * 512]),
                reads=[f"{pf}:u2_{wsl}", f"{pf}:accx"], writes=[f"{pf}:accx"])
            p.op("scalar", lambda e, wsl=wsl: e.activation(
                sq[:, :], u2[:, wsl * 512:(wsl + 1) * 512], AF.Square),
                reads=[f"{pf}:u2_{wsl}"], writes=[f"{pf}:sq"])
            p.op("gpsimd", lambda e: e.tensor_add(accsq[:, :], accsq[:, :], sq[:, :]),
                 reads=[f"{pf}:sq", f"{pf}:accsq"], writes=[f"{pf}:accsq"])
        if f == 15:
            p.op("tensor", lambda e: e.matmul(
                p.pb[6][0:1, :], C["oncf"][:, 0:1], accx[:, :],
                start=True, stop=True),
                reads=[f"{pf}:accx", "oncf"], writes=["pb6"])
            p.op("tensor", lambda e: e.matmul(
                p.pb[7][0:1, :], C["oncf"][:, 0:1], accsq[:, :],
                start=True, stop=True),
                reads=[f"{pf}:accsq", "oncf"], writes=["pb7"])

    def rows_part(q):
        A_ = srow[0:1, 0:512]
        Bm = srow[0:1, 512:1024]
        Cc = srow[0:1, 1024:1536]
        p.op("vector", lambda e, A_=A_: e.tensor_scalar(
            A_, p.pb[6][0:1, :], 1.0 / D, None, ALU.mult),
            reads=["pb6"], writes=[f"{pf}:lnA"])
        p.op("vector", lambda e, A_=A_, Bm=Bm: e.tensor_mul(Bm, A_, A_),
             reads=[f"{pf}:lnA"], writes=[f"{pf}:lnB"])
        p.op("vector", lambda e, Cc=Cc: e.tensor_scalar(
            Cc, p.pb[7][0:1, :], 1.0 / D, EPS, ALU.mult, ALU.add),
            reads=["pb7"], writes=[f"{pf}:lnC"])
        p.op("vector", lambda e, Cc=Cc, Bm=Bm: e.tensor_sub(Cc, Cc, Bm),
             reads=[f"{pf}:lnB", f"{pf}:lnC"], writes=[f"{pf}:lnC"])
        p.op("scalar", lambda e, Cc=Cc: e.activation(Cc, Cc, AF.Ln),
             reads=[f"{pf}:lnC"], writes=[f"{pf}:lnC"])
        p.op("scalar", lambda e, Cc=Cc: e.activation(Cc, Cc, AF.Exp, scale=-0.5),
             reads=[f"{pf}:lnC"], writes=[f"{pf}:lnC"])
        p.op("vector", lambda e, A_=A_, Bm=Bm, Cc=Cc: e.tensor_mul(Bm, A_, Cc),
             reads=[f"{pf}:lnA", f"{pf}:lnC"], writes=[f"{pf}:lnB"])
        p.op("vector", lambda e, Bm=Bm: e.tensor_scalar_mul(Bm, Bm, -1.0),
             reads=[f"{pf}:lnB"], writes=[f"{pf}:lnB"])
        p.op("tensor", lambda e, Cc=Cc: e.matmul(
            p.pb[6][:, :], C["onr"][0:1, 0:128], Cc, start=True, stop=True),
            reads=[f"{pf}:lnC", "onr"], writes=["pb6"])
        p.op("tensor", lambda e, Bm=Bm: e.matmul(
            p.pb[7][:, :], C["onr"][0:1, 0:128], Bm, start=True, stop=True),
            reads=[f"{pf}:lnB", "onr"], writes=["pb7"])
        p.op("vector", lambda e: e.tensor_copy(rsb[:, :], p.pb[6][:, :]),
             reads=["pb6"], writes=[f"{pf}:rsb"])
        p.op("vector", lambda e: e.tensor_copy(bsb2[:, :], p.pb[7][:, :]),
             reads=["pb7"], writes=[f"{pf}:bsb2"])

    def norm_iter(q, f):
        h0 = q * TH
        wsl = f % 2
        p.op("sync", lambda e, f=f, h0=h0, wsl=wsl: e.dma_start(
            out=u2t[:, wsl * 512:(wsl + 1) * 512],
            in_=t["u2d"][f * 128:(f + 1) * 128, h0:h0 + TH]),
            reads=[f"{pf}:u2d_{q % 2}_{f}"], writes=[f"{pf}:u2t{wsl}"], dma=True)
        p.op("vector", lambda e, wsl=wsl: e.tensor_mul(
            outs[:, :], u2t[:, wsl * 512:(wsl + 1) * 512], rsb[:, :]),
            reads=[f"{pf}:u2t{wsl}", f"{pf}:rsb"], writes=[f"{pf}:outs"])
        p.op("vector", lambda e: e.tensor_add(outs[:, :], outs[:, :], bsb2[:, :]),
             reads=[f"{pf}:outs", f"{pf}:bsb2"], writes=[f"{pf}:outs"])
        p.op("sync", lambda e, f=f, h0=h0: e.dma_start(
            out=xout[f * 128:(f + 1) * 128, h0:h0 + TH],
            in_=outs[:, :]),
            reads=[f"{pf}:outs"], dma=True)
        if scat:
            p.op("vector", lambda e: e.tensor_copy(outs_bf[:, :], outs[:, :]),
                 reads=[f"{pf}:outs"], writes=[f"{pf}:obf"])
            p.op("gpsimd", lambda e, f=f, h0=h0: e.indirect_dma_start(
                t["x2_sh"][:, :],
                IndirectOffsetOnAxis(ap=C["idx_t"][:, f:f + 1], axis=0),
                outs_bf[:, :], None, element_offset=h0),
                reads=[f"{pf}:obf", "idx"], dma=True)

    NQ = TC // TH
    load_xb(0)
    for q in range(NQ):
        if q + 1 < NQ:
            load_xb(q + 1)
        nxt_down = 0
        nxt_norm = 0
        for i in range(44):
            gate_up(q, i)
            if q > 0:
                want = (i + 1) * 16 // 44
                while nxt_down < want:
                    down_f(q - 1, nxt_down)
                    if nxt_down > 0:
                        down_epi(q - 1, nxt_down - 1)
                    nxt_down += 1
            if q > 1:
                while nxt_norm < (i + 1) * 16 // 44:
                    norm_iter(q - 2, nxt_norm)
                    nxt_norm += 1
        if q > 0:
            while nxt_down < 16:
                down_f(q - 1, nxt_down)
                if nxt_down > 0:
                    down_epi(q - 1, nxt_down - 1)
                nxt_down += 1
            down_epi(q - 1, 15)
            rows_part(q - 1)
        if q > 1:
            while nxt_norm < 16:
                norm_iter(q - 2, nxt_norm)
                nxt_norm += 1
    for f in range(16):
        down_f(NQ - 1, f)
        if f > 0:
            down_epi(NQ - 1, f - 1)
        norm_iter(NQ - 2, f)
    down_epi(NQ - 1, 15)
    rows_part(NQ - 1)
    for f in range(16):
        norm_iter(NQ - 1, f)


def em_H(p, t, C, ar, rw):
    xsb = ar.bf16(128, 16 * S)
    wsb = ar.bf16(128, 16 * 1024)
    outsb = ar.bf16(128, 4 * 512)
    bpp = ar.f32(128, 16)
    bsb = rw.f32(3 * 1024)
    on5 = rw.f32(512)
    p.op("sync", lambda e: e.dma_start(out=on5[0:1, :], in_=t["ones512"]),
         writes=["H:on5"], dma=True)
    for k in range(16):
        for pp in range(2):
            p.op("sync", lambda e, k=k, pp=pp: e.dma_start(
                out=xsb[:, k * S + pp * 2048: k * S + pp * 2048 + 2048],
                in_=t["x2_sh"][pp * 2048 + k * 128: pp * 2048 + (k + 1) * 128, :]),
                writes=[f"H:x{k}"], dma=True)
    for nm, boff in (("bq1", 0), ("bk1", 1024), ("bv1", 2048)):
        p.op("sync", lambda e, nm=nm, boff=boff: e.dma_start(
            out=bsb[0:1, boff:boff + 1024], in_=t[nm]),
            writes=[f"H:{nm}"], dma=True)
    p.op("sync", lambda e: e.dma_start(out=bpp[:, 0:8], in_=t["bqpp"]),
         writes=["H:bpp"], dma=True)
    p.op("sync", lambda e: e.dma_start(out=bpp[:, 8:16], in_=t["bkpp"]),
         writes=["H:bpp"], dma=True)
    for wnm, bo, outd in (("wq1", 0, "q1"), ("wk1", 8, "k1")):
        for k in range(16):
            p.op("sync", lambda e, k=k, wnm=wnm: e.dma_start(
                out=wsb[:, k * 1024:(k + 1) * 1024],
                in_=t[wnm][k * 128:(k + 1) * 128, :]),
                writes=[f"H:w{k}"], dma=True)
        for tt in range(S // 512):
            t0 = tt * 512
            for m in range(8):
                bank = m % 4
                for k in range(16):
                    p.op("tensor", lambda e, k=k, m=m, t0=t0, bank=bank: e.matmul(
                        p.pb[bank][:, :],
                        wsb[:, k * 1024 + m * 128: k * 1024 + (m + 1) * 128],
                        xsb[:, k * S + t0: k * S + t0 + 512],
                        start=(k == 0), stop=(k == 15)),
                        reads=[f"H:x{k}", f"H:w{k}"],
                        writes=[f"pb{bank}"] if k in (0, 15) else [])
                p.op("vector", lambda e, m=m, bank=bank, bo=bo: e.tensor_scalar(
                    outsb[:, bank * 512:(bank + 1) * 512], p.pb[bank][:, :],
                    bpp[:, bo + m:bo + m + 1], None, ALU.add),
                    reads=[f"pb{bank}", "H:bpp"], writes=[f"H:o{bank}"])
                p.op("sync", lambda e, m=m, t0=t0, bank=bank, outd=outd: e.dma_start(
                    out=t[outd][m * 128:(m + 1) * 128, t0:t0 + 512],
                    in_=outsb[:, bank * 512:(bank + 1) * 512]),
                    reads=[f"H:o{bank}"], dma=True)
    for wnm, brow, outd in (("wv1", 2048, "v1t"), ("wq1", 0, "q1t")):
        for k in range(16):
            p.op("sync", lambda e, k=k, wnm=wnm: e.dma_start(
                out=wsb[:, k * 1024:(k + 1) * 1024],
                in_=t[wnm][k * 128:(k + 1) * 128, :]),
                writes=[f"H:w{k}"], dma=True)
        for ttile in range(S // 128):
            for v4 in range(2):
                bank = 4 + (2 * ttile + v4) % 4
                p.op("tensor", lambda e, v4=v4, bank=bank, brow=brow: e.matmul(
                    p.pb[bank][:, :], C["onr"][0:1, 0:128],
                    bsb[0:1, brow + v4 * 512: brow + (v4 + 1) * 512],
                    start=True, stop=False),
                    reads=["H:bv1", "H:bq1", "onr"], writes=[f"pb{bank}"])
                for k in range(16):
                    p.op("tensor", lambda e, k=k, ttile=ttile, v4=v4, bank=bank: e.matmul(
                        p.pb[bank][:, :],
                        xsb[:, k * S + ttile * 128: k * S + (ttile + 1) * 128],
                        wsb[:, k * 1024 + v4 * 512: k * 1024 + (v4 + 1) * 512],
                        start=False, stop=(k == 15)),
                        reads=[f"H:x{k}", f"H:w{k}"],
                        writes=[f"pb{bank}"] if k == 15 else [])
                p.op("vector", lambda e, v4=v4, bank=bank: e.tensor_copy(
                    outsb[:, v4 * 512:(v4 + 1) * 512], p.pb[bank][:, :]),
                    reads=[f"pb{bank}"], writes=[f"H:o{v4}"])
                p.op("sync", lambda e, ttile=ttile, v4=v4, outd=outd: e.dma_start(
                    out=t[outd][ttile * 128:(ttile + 1) * 128, v4 * 512:(v4 + 1) * 512],
                    in_=outsb[:, v4 * 512:(v4 + 1) * 512]),
                    reads=[f"H:o{v4}"], dma=True)


def em_I(p, t, C, ar, rw):
    q_sb = ar.bf16(128, 2 * S)
    k_sb = ar.bf16(128, 2 * S)
    vt_sb = ar.bf16(128, 2 * S)
    qt_all = ar.bf16(128, 2 * S)
    msk = ar.f32(128, 384)
    iorow = ar.f32(128, 64)
    iopp = ar.f32(64, 1)
    ident = ar.bf16(128, 128)
    idf8 = ar.f32(8, 8)
    sq_sb = ar.bf16(128, 2 * 512)
    negc = ar.f32(128, 1)
    na = ar.f32(8, S)
    nb = ar.f32(8, S)
    mx8 = ar.f32(8, 8)
    rkall = ar.f32(128, 256)
    probs = ar.bf16(128, 384)
    lctx = ar.bf16(128, S)
    P_sb = ar.bf16(128, 128)
    qgT_sb = ar.bf16(64, 128)
    qg_sb = ar.bf16(128, 64)
    gp_sb = ar.bf16(128, 64)
    rg_pp = ar.f32(64, 1)
    gctx_sb = ar.bf16(64, 128)
    cb_sb = ar.f32(128, 2 * 512)
    pi_sb = ar.bf16(64, 2 * 512)
    t2_sb = ar.f32(128, 2 * 512)
    ctxo = ar.bf16(128, 2 * 512)
    nrow = ar.f32(1, S)
    rloc = ar.f32(1, S)
    mra = ar.f32(1, S)
    srow = rw.f32(64)
    rgrow = rw.f32(64)
    rr_sb = rw.f32(512)
    mr_sb = rw.f32(512)
    cbrow = rw.f32(1024)
    pbb6 = p.pb[6].bitcast(BF16)
    for (tl, dsrc, key) in [(msk, "maskL", "I:msk"), (iorow, "io_row", "I:iorow"),
                            (iopp, "io_pp", "I:iopp"), (ident, "identd", "I:ident"),
                            (idf8, "ident8f", "I:id8")]:
        p.op("sync", lambda e, tl=tl, dsrc=dsrc: e.dma_start(out=tl[:, :], in_=t[dsrc]),
             writes=[key], dma=True)
    for j in range(8):
        p.op("sync", lambda e, j=j: e.dma_start(
            out=q_sb[:, 0:S], in_=t["q1"][j * 128:(j + 1) * 128, :]),
            writes=["I:q0"], dma=True)
        for c8 in range(8):
            sb0 = (c8 % 2) * 512
            bk = c8 % 2
            p.op("scalar", lambda e, c8=c8, sb0=sb0: e.activation(
                sq_sb[:, sb0:sb0 + 512], q_sb[:, c8 * 512:(c8 + 1) * 512], AF.Square),
                reads=["I:q0"], writes=[f"I:sq{c8 % 2}"])
            p.op("tensor", lambda e, sb0=sb0, bk=bk: e.matmul(
                p.pb[bk][0:1, :], C["onc"][:, 0:1], sq_sb[:, sb0:sb0 + 512],
                start=True, stop=True),
                reads=[f"I:sq{c8 % 2}", "onc"], writes=[f"pb{bk}"])
            p.op("vector", lambda e, c8=c8, bk=bk: e.tensor_copy(
                nrow[0:1, c8 * 512:(c8 + 1) * 512], p.pb[bk][0:1, :]),
                reads=[f"pb{bk}"], writes=["I:nrow"])
        p.op("sync", lambda e, j=j: e.dma_start(
            out=t["rankm"][j:j + 1, :], in_=nrow[0:1, :]),
            reads=["I:nrow"], writes=["I:na_d"], dma=True)
    p.op("sync", lambda e: e.dma_start(out=na[:, :], in_=t["rankm"]),
         reads=["I:na_d"], writes=["I:na"], dma=True)
    bufs = [na, nb]
    for rnd in range(8):
        cur, nxt = bufs[rnd % 2], bufs[(rnd + 1) % 2]
        ck, nk = ("I:na" if rnd % 2 == 0 else "I:nb"), ("I:nb" if rnd % 2 == 0 else "I:na")
        p.op("vector", lambda e, cur=cur: e.max(out=mx8[:, :], in_=cur[:, :]),
             reads=[ck], writes=["I:mx8"])
        if rnd == 7:
            p.op("vector", lambda e: e.memset(mx8[:, 6:8], -1.0),
                 reads=[], writes=["I:mx8"])
        p.op("vector", lambda e, cur=cur, nxt=nxt: e.match_replace(
            out=nxt[:, :], in_to_replace=mx8[:, :], in_values=cur[:, :],
            imm_value=-1.0),
            reads=[ck, "I:mx8"], writes=[nk])
    p.op("vector", lambda e: e.tensor_scalar(
        nb[:, :], na[:, :], -0.5, None, ALU.is_le),
        reads=["I:na"], writes=["I:nb"])
    p.op("vector", lambda e: e.memset(nb[:, 0:1], 1.0), reads=[], writes=["I:nb"])
    p.op("vector", lambda e: e.memset(nb[:, S - 1:S], 1.0), reads=[], writes=["I:nb"])
    p.op("vector", lambda e: e.tensor_tensor_scan(
        na[:, :], nb[:, :], nb[:, :], 0.0, ALU.add, ALU.bypass),
        reads=["I:nb"], writes=["I:na"])
    p.op("vector", lambda e: e.tensor_sub(na[:, :], na[:, :], nb[:, :]),
         reads=["I:na", "I:nb"], writes=["I:na"])
    p.op("sync", lambda e: e.dma_start(out=t["maskm"], in_=nb[:, :]),
         reads=["I:nb"], writes=["I:maskmD"], dma=True)
    p.op("vector", lambda e: e.tensor_scalar(
        nb[:, :], nb[:, :], -1e9, 1e9, ALU.mult, ALU.add),
        reads=["I:nb"], writes=["I:nb"])
    p.op("vector", lambda e: e.tensor_add(na[:, :], na[:, :], nb[:, :]),
         reads=["I:na", "I:nb"], writes=["I:na"])
    p.op("sync", lambda e: e.dma_start(out=t["rankm"], in_=na[:, :]),
         reads=["I:na"], writes=["I:rankmD"], dma=True)
    for c in range(32):
        p.op("tensor", lambda e, c=c: e.transpose(
            p.pb[6][:, 0:8], na[0:8, c * 128:(c + 1) * 128], idf8[0:8, 0:8]),
            reads=["I:na", "I:id8"], writes=["pb6"])
        p.op("vector", lambda e, c=c: e.tensor_copy(
            rkall[:, c * 8:(c + 1) * 8], p.pb[6][:, 0:8]),
            reads=["pb6"], writes=["I:rkall"])
    NB = 32

    def loads(j):
        pr = j % 2
        o = pr * S
        p.op("sync", lambda e, j=j, o=o: e.dma_start(
            out=q_sb[:, o:o + S], in_=t["q1"][j * 128:(j + 1) * 128, :]),
            writes=[f"I:q{pr}"], dma=True)
        p.op("sync", lambda e, j=j, o=o: e.dma_start(
            out=k_sb[:, o:o + S], in_=t["k1"][j * 128:(j + 1) * 128, :]),
            writes=[f"I:k{pr}"], dma=True)
        for g in range(8):
            p.op("sync", lambda e, j=j, g=g, o=o: e.dma_start(
                out=vt_sb[:, o + g * 512: o + (g + 1) * 512].rearrange(
                    "p (a d) -> p a d", d=128),
                in_=t["v1t"][g * 512:(g + 1) * 512, j * 128:(j + 1) * 128].rearrange(
                    "(a p) d -> p a d", p=128)),
                writes=[f"I:v{pr}"], dma=True)
            p.op("sync", lambda e, j=j, g=g, o=o: e.dma_start(
                out=qt_all[:, o + g * 512: o + (g + 1) * 512].rearrange(
                    "p (a d) -> p a d", d=128),
                in_=t["q1t"][g * 512:(g + 1) * 512, j * 128:(j + 1) * 128].rearrange(
                    "(a p) d -> p a d", p=128)),
                writes=[f"I:qt{pr}"], dma=True)

    loads(0)
    for j in range(8):
        pr = j % 2
        o = pr * S
        if j < 7:
            loads(j + 1)
        _bound_c(p, f"I:q{pr}", q_sb[:, o:o + S], sq_sb, C["onc"], srow, 0, "I")
        _bound_c(p, f"I:k{pr}", k_sb[:, o:o + S], sq_sb, C["onc"], srow, 1, "I")
        _negc_from_bounds(p, C, srow, negc[:, 0:1], "I")

        def qb0(tb):
            return 0 if tb % 2 == 0 else 7

        def epi_local(tb):
            bqk = qb0(tb)
            qlo = max(0, (tb - 1) * 128)
            qhi = min(S, (tb + 2) * 128)
            Nq = qhi - qlo
            mlo = 128 if tb == 0 else 0
            del qhi
            p.op("vector", lambda e, Nq=Nq, mlo=mlo, bqk=bqk: e.tensor_add(
                p.pb[bqk][:, 0:Nq], p.pb[bqk][:, 0:Nq], msk[:, mlo:mlo + Nq]),
                reads=[f"pb{bqk}", "I:msk"], writes=[f"pb{bqk}"])
            p.op("scalar", lambda e, Nq=Nq, bqk=bqk: e.activation(
                probs[:, 0:Nq], p.pb[bqk][:, 0:Nq], AF.Exp,
                bias=negc[:, 0:1], scale=SCALE),
                reads=[f"pb{bqk}", "I:negc0"], writes=["I:probs"])
            for cg in (-1, 0, 1):
                qb = tb + cg
                if qb < 0 or qb >= NB:
                    continue
                col = qb * 128 - qlo
                first = (tb == max(0, qb - 1))
                last = (tb == min(NB - 1, qb + 1))
                cbank = 2 if qb % 3 == 0 else (3 if qb % 3 == 1 else 6)
                dbank = 1 if qb % 3 == 0 else (4 if qb % 3 == 1 else 5)
                p.op("tensor", lambda e, col=col, dbank=dbank, first=first,
                     last=last: e.matmul(
                    p.pb[dbank][0:1, 0:128], C["onc"][:, 0:1], probs[:, col:col + 128],
                    start=first, stop=last),
                    reads=["I:probs", "onc"],
                    writes=[f"pb{dbank}"] if (first or last) else [])
                p.op("tensor", lambda e, tb=tb, col=col, cbank=cbank, first=first,
                     last=last, o=o: e.matmul(
                    p.pb[cbank][:, 0:128], vt_sb[:, o + tb * 128: o + (tb + 1) * 128],
                    probs[:, col:col + 128], start=first, stop=last),
                    reads=["I:probs", f"I:v{pr}"],
                    writes=[f"pb{cbank}"] if (first or last) else [])
            done = [tb - 1] if tb < NB - 1 else [tb - 1, tb]
            for qb in done:
                if qb < 0:
                    continue
                cbank = 2 if qb % 3 == 0 else (3 if qb % 3 == 1 else 6)
                dbank = 1 if qb % 3 == 0 else (4 if qb % 3 == 1 else 5)
                p.op("vector", lambda e, qb=qb, dbank=dbank: e.reciprocal(
                    rloc[0:1, qb * 128:(qb + 1) * 128], p.pb[dbank][0:1, 0:128]),
                    reads=[f"pb{dbank}"], writes=["I:rloc"])
                p.op("vector", lambda e, qb=qb, cbank=cbank: e.tensor_copy(
                    lctx[:, qb * 128:(qb + 1) * 128], p.pb[cbank][:, 0:128]),
                    reads=[f"pb{cbank}"], writes=["I:lctx"])

        for tb in range(NB):
            qlo = max(0, (tb - 1) * 128)
            qhi = min(S, (tb + 2) * 128)
            Nq = qhi - qlo
            bqk = qb0(tb)
            p.op("tensor", lambda e, tb=tb, qlo=qlo, Nq=Nq, bqk=bqk, o=o: e.matmul(
                p.pb[bqk][:, 0:Nq], k_sb[:, o + tb * 128: o + (tb + 1) * 128],
                q_sb[:, o + qlo: o + qlo + Nq], start=True, stop=True),
                reads=[f"I:k{pr}", f"I:q{pr}"], writes=[f"pb{bqk}"])
            if tb > 0:
                epi_local(tb - 1)
        epi_local(NB - 1)
        for c in range(32):
            pp = c % 2
            p.op("vector", lambda e, c=c, pp=pp, j=j: e.tensor_scalar(
                P_sb[:, pp * 64:(pp + 1) * 64], iorow[:, :],
                rkall[:, c * 8 + j:c * 8 + j + 1], None, ALU.is_equal),
                reads=["I:iorow", "I:rkall"], writes=[f"I:P{pp}"])
            p.op("tensor", lambda e, c=c, pp=pp, o=o: e.matmul(
                p.pb[0][0:64, 0:128], P_sb[:, pp * 64:(pp + 1) * 64],
                qt_all[:, o + c * 128: o + (c + 1) * 128],
                start=(c == 0), stop=(c == 31)),
                reads=[f"I:P{pp}", f"I:qt{pr}"],
                writes=["pb0"] if c in (0, 31) else [])
        p.op("vector", lambda e: e.tensor_copy(qgT_sb[:, :], p.pb[0][0:64, 0:128]),
             reads=["pb0"], writes=["I:qgT"])
        p.op("tensor", lambda e: e.transpose(
            pbb6[:, 0:64], qgT_sb[:, :], ident[0:64, 0:64]),
            reads=["I:qgT", "I:ident"], writes=["pb6"])
        p.op("vector", lambda e: e.tensor_copy(qg_sb[:, :], pbb6[:, 0:64]),
             reads=["pb6"], writes=["I:qg"])
        for kb in range(32):
            p.op("tensor", lambda e, kb=kb, o=o: e.matmul(
                p.pb[1][:, 0:64], k_sb[:, o + kb * 128: o + (kb + 1) * 128], qg_sb[:, :],
                start=True, stop=True),
                reads=[f"I:k{pr}", "I:qg"], writes=["pb1"])
            p.op("scalar", lambda e: e.activation(
                gp_sb[:, :], p.pb[1][:, 0:64], AF.Exp,
                bias=negc[:, 0:1], scale=SCALE),
                reads=["pb1", "I:negc0"], writes=["I:gp"])
            p.op("tensor", lambda e, kb=kb: e.matmul(
                p.pb[2][0:1, 0:64], C["onc"][:, 0:1], gp_sb[:, :],
                start=(kb == 0), stop=(kb == 31)),
                reads=["I:gp", "onc"],
                writes=["pb2"] if kb in (0, 31) else [])
            p.op("tensor", lambda e, kb=kb, o=o: e.matmul(
                p.pb[3][0:64, 0:128], gp_sb[:, :], vt_sb[:, o + kb * 128: o + (kb + 1) * 128],
                start=(kb == 0), stop=(kb == 31)),
                reads=["I:gp", f"I:v{pr}"],
                writes=["pb3"] if kb in (0, 31) else [])
        p.op("vector", lambda e: e.reciprocal(rgrow[0:1, 0:64], p.pb[2][0:1, 0:64]),
             reads=["pb2"], writes=["I:rgrow"])
        p.op("tensor", lambda e: e.matmul(
            p.pb[5][0:64, 0:1], rgrow[0:1, 0:64], C["onr"][0:1, 0:1],
            start=True, stop=True),
            reads=["I:rgrow", "onr"], writes=["pb5"])
        p.op("vector", lambda e: e.tensor_copy(rg_pp[:, 0:1], p.pb[5][0:64, 0:1]),
             reads=["pb5"], writes=["I:rg"])
        p.op("vector", lambda e: e.tensor_scalar_mul(
            gctx_sb[:, :], p.pb[3][0:64, 0:128], rg_pp[:, 0:1]),
            reads=["pb3", "I:rg"], writes=["I:gctx"])
        p.op("sync", lambda e, j=j: e.dma_start(
            out=nrow[0:1, :], in_=t["rankm"][j:j + 1, :]),
            reads=["I:rankmD"], writes=["I:nrow"], dma=True)
        p.op("sync", lambda e, j=j: e.dma_start(
            out=mra[0:1, :], in_=t["maskm"][j:j + 1, :]),
            reads=["I:maskmD"], writes=["I:mra"], dma=True)
        for c8 in range(8):
            cp = c8 % 2
            co = cp * 512
            bb5 = 5 if cp == 0 else 2
            bb7 = 7 if cp == 0 else 1
            bb4 = 4 if cp == 0 else 3
            p.op("vector", lambda e, c8=c8, co=co, cp=cp: e.tensor_scalar(
                cbrow[0:1, co:co + 512], mra[0:1, c8 * 512:(c8 + 1) * 512], -1.0, 1.0,
                ALU.mult, ALU.add),
                reads=["I:mra"], writes=[f"I:cbrow{cp}"])
            p.op("vector", lambda e, c8=c8, co=co, cp=cp: e.tensor_mul(
                cbrow[0:1, co:co + 512], cbrow[0:1, co:co + 512],
                rloc[0:1, c8 * 512:(c8 + 1) * 512]),
                reads=[f"I:cbrow{cp}", "I:rloc"], writes=[f"I:cbrow{cp}"])
            p.op("tensor", lambda e, co=co, cp=cp, bb5=bb5: e.matmul(
                p.pb[bb5][:, :], C["onr"][0:1, 0:128], cbrow[0:1, co:co + 512],
                start=True, stop=True),
                reads=[f"I:cbrow{cp}", "onr"], writes=[f"pb{bb5}"])
            p.op("vector", lambda e, co=co, cp=cp, bb5=bb5: e.tensor_copy(
                cb_sb[:, co:co + 512], p.pb[bb5][:, :]),
                reads=[f"pb{bb5}"], writes=[f"I:cb{cp}"])
            p.op("tensor", lambda e, c8=c8, bb7=bb7: e.matmul(
                p.pb[bb7][0:64, :], C["onr"][0:1, 0:64],
                nrow[0:1, c8 * 512:(c8 + 1) * 512],
                start=True, stop=True),
                reads=["I:nrow", "onr"], writes=[f"pb{bb7}"])
            p.op("vector", lambda e, co=co, cp=cp, bb7=bb7: e.tensor_scalar(
                pi_sb[:, co:co + 512], p.pb[bb7][0:64, :], iopp[:, 0:1], None,
                ALU.is_equal),
                reads=[f"pb{bb7}", "I:iopp"], writes=[f"I:pi{cp}"])
            p.op("tensor", lambda e, co=co, cp=cp, bb4=bb4: e.matmul(
                p.pb[bb4][:, :], gctx_sb[:, :], pi_sb[:, co:co + 512],
                start=True, stop=True),
                reads=["I:gctx", f"I:pi{cp}"], writes=[f"pb{bb4}"])
            p.op("vector", lambda e, c8=c8, co=co, cp=cp: e.tensor_mul(
                t2_sb[:, co:co + 512], lctx[:, c8 * 512:(c8 + 1) * 512],
                cb_sb[:, co:co + 512]),
                reads=["I:lctx", f"I:cb{cp}"], writes=[f"I:t2{cp}"])
            p.op("vector", lambda e, co=co, cp=cp, bb4=bb4: e.tensor_add(
                ctxo[:, co:co + 512], t2_sb[:, co:co + 512], p.pb[bb4][:, :]),
                reads=[f"I:t2{cp}", f"pb{bb4}"], writes=[f"I:ctxo{cp}"])
            p.op("gpsimd", lambda e, j=j, c8=c8, co=co, cp=cp: e.indirect_dma_start(
                t["ctx1_sh"][:, :],
                IndirectOffsetOnAxis(ap=C["idx_h"][:, j:j + 1], axis=0),
                ctxo[:, co:co + 512], None, element_offset=c8 * 512),
                reads=[f"I:ctxo{cp}", "idx"], dma=True)


def em_J(p, t, C, ar, rw):
    ctx1v = t["ctx1_sh"].rearrange("h (p w) -> (h p) w", p=2)
    a1sb = ar.bf16(128, 16 * TC)
    xc = ar.f32(128, 2 * 512)
    u1 = ar.f32(128, 16 * 512)
    sq = ar.bf16(128, 512)
    accx = ar.f32(128, 512)
    accsq = ar.f32(128, 512)
    t1 = ar.f32(128, 512)
    outs = ar.f32(128, 512)
    srow = rw.f32(3 * 512)
    for k in range(16):
        p.op("gpsimd", lambda e, k=k: e.indirect_dma_start(
            a1sb[:, k * TC:(k + 1) * TC], None,
            ctx1v[:, :], IndirectOffsetOnAxis(ap=C["idx_c"][:, k:k + 1], axis=0)),
            writes=[f"J:a{k}"], reads=["idx"], dma=True)
    def epiJ(m):
        if m == 0:
            p.op("gpsimd", lambda e: e.tensor_copy(accx[:, :], u1[:, 0:512]),
                 reads=["J:u1_0"], writes=["J:accx"])
            p.op("scalar", lambda e: e.activation(
                accsq[:, :], u1[:, 0:512], AF.Square),
                reads=["J:u1_0"], writes=["J:accsq"])
        else:
            p.op("gpsimd", lambda e, m=m: e.tensor_add(
                accx[:, :], accx[:, :], u1[:, m * 512:(m + 1) * 512]),
                reads=[f"J:u1_{m}", "J:accx"], writes=["J:accx"])
            p.op("scalar", lambda e, m=m: e.activation(
                sq[:, :], u1[:, m * 512:(m + 1) * 512], AF.Square),
                reads=[f"J:u1_{m}"], writes=["J:sq"])
            p.op("gpsimd", lambda e: e.tensor_add(accsq[:, :], accsq[:, :], sq[:, :]),
                 reads=["J:sq", "J:accsq"], writes=["J:accsq"])
        if m == 15:
            p.op("tensor", lambda e: e.matmul(
                p.pb[6][0:1, :], C["oncf"][:, 0:1], accx[:, :],
                start=True, stop=True),
                reads=["J:accx", "oncf"], writes=["pb6"])
            p.op("tensor", lambda e: e.matmul(
                p.pb[7][0:1, :], C["oncf"][:, 0:1], accsq[:, :],
                start=True, stop=True),
                reads=["J:accsq", "oncf"], writes=["pb7"])

    for tt in range(TC // 512):
        t0 = tt * 512
        for m in range(16):
            xb = m % 2
            p.op("sync", lambda e, m=m, t0=t0, xb=xb: e.dma_start(
                out=xc[:, xb * 512:(xb + 1) * 512],
                in_=t["x2"][m * 128:(m + 1) * 128, t0:t0 + 512]),
                writes=[f"J:xc{xb}"], dma=True)
            p.op("vector", lambda e, m=m, t0=t0: e.tensor_copy(
                u1[:, m * 512:(m + 1) * 512],
                a1sb[:, m * TC + t0: m * TC + t0 + 512]),
                reads=[f"J:a{m}"], writes=[f"J:u1_{m}"])
            p.op("vector", lambda e, m=m, xb=xb: e.tensor_add(
                u1[:, m * 512:(m + 1) * 512],
                u1[:, m * 512:(m + 1) * 512],
                xc[:, xb * 512:(xb + 1) * 512]),
                reads=[f"J:u1_{m}", f"J:xc{xb}"], writes=[f"J:u1_{m}"])
            if m > 0:
                epiJ(m - 1)
        epiJ(15)

        def outfn(m, o, t0=t0):
            p.op("sync", lambda e, m=m, t0=t0, o=o: e.dma_start(
                out=t["x3"][m * 128:(m + 1) * 128, t0:t0 + 512], in_=o[:, :]),
                reads=["J:outs"], dma=True)
        _ln_apply(p, C, "J", u1, 16, D, outfn, srow, t1, outs)


PHASES_ALL = ("AB", "C", "D", "E", "F", "G0", "H", "I", "J", "G1")


def build_fused(upto="G1", dbg=False):
    nc = bass.Bass("TRN2", target_bir_lowering=False)
    phases = PHASES_ALL[:PHASES_ALL.index(upto) + 1]
    t = {}

    def ein(name, shape, dt=F32):
        t[name] = nc.dram_tensor(name, shape, dt, kind="ExternalInput").ap()

    ein("x_f", [D, TC]); ein("wq_down", [D, QP], BF16); ein("gq", [128, 8])
    ein("wkv_lora", [D, KVPP], BF16); ein("wkv_rope", [D, 64], BF16); ein("gkv", [128, 11])
    ein("wq_up", [QP, 1024], BF16); ein("wkv_up_k", [KVPP, 512], BF16)
    ein("wkv_up_v", [KVPP, 1024], BF16)
    ein("wo", [D, D], BF16)
    ein("cs_own", [64, TC]); ein("sc_own", [64, TC])
    ein("cos4", [128, S]); ein("sin4", [128, S])
    for L in ("L0", "L1"):
        ein(f"{L}_gws", [44, 128, 2048], BF16); ein(f"{L}_uws", [44, 128, 2048], BF16)
        ein(f"{L}_dws", [16, 128, 5632], BF16)
        ein(f"{L}_gs", [128, 44]); ein(f"{L}_us", [128, 44]); ein(f"{L}_ds", [128, 16])
    ein("wq1", [D, 1024], BF16); ein("wk1", [D, 1024], BF16)
    ein("wv1", [D, 1024], BF16)
    ein("bq1", [1, 1024]); ein("bk1", [1, 1024]); ein("bv1", [1, 1024])
    ein("bqpp", [128, 8]); ein("bkpp", [128, 8])
    ein("maskE", [128, 2048]); ein("maskL", [128, 384])
    ein("io_row", [128, 64]); ein("io_pp", [64, 1]); ein("identd", [128, 128], BF16)
    ein("ident8f", [8, 8])
    ein("ones_colb", [128, 1], BF16); ein("ones_colf", [128, 1])
    ein("ones_row", [1, 128]); ein("ones512", [1, 512])
    ein("idx_fcq", [128, 8], U32); ein("idx_fckv", [128, 11], U32)
    ein("idx_fkr", [64, 1], U32); ein("idx_h", [128, 8], U32)
    ein("idx_t", [128, 16], U32); ein("idx_c", [128, 16], U32)

    def sh_t(name, shape, dt=BF16):
        t[name] = nc.dram_tensor(name, shape, dt, kind="Internal",
                                 addr_space="Shared").ap()

    sh_t("cq_sh", [2 * 1024, 2048]); sh_t("ckv_sh", [2 * 1408, 2048])
    sh_t("kr_sh", [2 * 64, 2048])
    sh_t("ctx0_sh", [2048, S]); sh_t("x2_sh", [2 * 2048, 2048])
    sh_t("ctx1_sh", [2048, S])

    ikind = "ExternalOutput" if dbg else "Internal"

    def loc(name, shape, dt, kind):
        t[name] = nc.dram_tensor(name, shape, dt, kind=kind).ap()

    loc("q0", [1024, S], BF16, ikind)
    loc("knope", [512, S], BF16, ikind)
    loc("vt", [S, 1024], BF16, ikind)
    loc("x1", [D, TC], F32, ikind)
    loc("x2", [D, TC], F32, ikind)
    loc("x3", [D, TC], F32, ikind)
    loc("q1", [1024, S], BF16, ikind)
    loc("k1", [1024, S], BF16, ikind)
    loc("v1t", [S, 1024], BF16, ikind)
    loc("q1t", [S, 1024], BF16, "Internal")
    loc("u2d", [D, TC], F32, "Internal")
    loc("rankm", [8, S], F32, "Internal")
    loc("maskm", [8, S], F32, "Internal")
    loc("bar_in", [1, 128], F32, "Internal")
    for i in range(4):
        loc(f"bar_out{i}", [1, 128], F32, "Internal")
    t["xout"] = nc.dram_tensor("xout", [D, TC], F32, kind="ExternalOutput").ap()

    with ExitStack() as ctx:
        C = {}
        C["onc"] = ctx.enter_context(nc.sbuf_tensor("c_onc", [128, 1], BF16))
        C["oncf"] = ctx.enter_context(nc.sbuf_tensor("c_oncf", [128, 1], F32))
        C["onr"] = ctx.enter_context(nc.sbuf_tensor("c_onr", [1, 128], F32))
        C["gq"] = ctx.enter_context(nc.sbuf_tensor("c_gq", [128, 8], F32))
        C["gkv"] = ctx.enter_context(nc.sbuf_tensor("c_gkv", [128, 11], F32))
        C["idx_fcq"] = ctx.enter_context(nc.sbuf_tensor("c_ifcq", [128, 8], U32))
        C["idx_fckv"] = ctx.enter_context(nc.sbuf_tensor("c_ifckv", [128, 11], U32))
        C["idx_fkr"] = ctx.enter_context(nc.sbuf_tensor("c_ifkr", [64, 1], U32))
        C["idx_h"] = ctx.enter_context(nc.sbuf_tensor("c_ih", [128, 8], U32))
        C["idx_t"] = ctx.enter_context(nc.sbuf_tensor("c_it", [128, 16], U32))
        C["idx_c"] = ctx.enter_context(nc.sbuf_tensor("c_ic", [128, 16], U32))
        ar = Arena(nc, ctx, 48744)
        rw = Rows(nc, ctx, 4096)
        p = Prog(nc, ctx)
        p.pb = [ctx.enter_context(nc.psum_tensor(f"pb{i}", [128, 512], F32))
                for i in range(8)]

        for nm in ("onc", "oncf", "onr", "gq", "gkv",
                   "idx_fcq", "idx_fckv", "idx_fkr", "idx_h", "idx_t", "idx_c"):
            src = {"onc": "ones_colb", "oncf": "ones_colf", "onr": "ones_row"}.get(nm, nm)
            key = "idx" if nm.startswith("idx") else nm
            p.op("sync", lambda e, nm=nm, src=src: e.dma_start(out=C[nm][:, :], in_=t[src]),
                 writes=[key], dma=True)
        p.op("sync", lambda e: e.dma_start(out=t["bar_in"], in_=C["onr"][0:1, :]),
             reads=["onr"], writes=["barin"], dma=True)

        def barrier(i):
            p.fence()
            p.cc(lambda e, i=i: e.collective_compute(
                "AllReduce", mybir.AluOpType.add, replica_groups=PAIRS,
                ins=[t["bar_in"].opt()], outs=[t[f"bar_out{i}"].opt()]))
            p.fence()

        for ph in phases:
            ar.reset(0)
            rw.reset()
            if ph == "AB":
                em_AB(p, t, C, ar, rw)
                barrier(0)
            elif ph == "C":
                em_C(p, t, C, ar, rw)
                p.fence()
            elif ph == "D":
                em_D(p, t, C, ar, rw)
                p.fence()
            elif ph == "E":
                em_E(p, t, C, ar, rw)
                barrier(1)
            elif ph == "F":
                em_F(p, t, C, ar, rw)
                p.fence()
            elif ph == "G0":
                em_G(p, t, C, ar, rw, "L0", t["x1"], t["x2"], scat=True)
                barrier(2)
            elif ph == "H":
                em_H(p, t, C, ar, rw)
                p.fence()
            elif ph == "I":
                em_I(p, t, C, ar, rw)
                barrier(3)
            elif ph == "J":
                em_J(p, t, C, ar, rw)
                p.fence()
            elif ph == "G1":
                em_G(p, t, C, ar, rw, "L1", t["x3"], t["xout"], scat=False)
        p.emit()
    return nc


def kernel(**ins):
    shd, cores = stage_inputs(ins)
    nc = build_fused()
    maps = []
    for c in range(NCORES):
        m = dict(shd)
        m.update(cores[c])
        maps.append(m)
    res = run_bass_kernel_spmd(nc, maps, core_ids=list(range(NCORES)))
    out = np.zeros((B, S, D), np.float32)
    for c in range(NCORES):
        b, r = c // 2, c % 2
        xo = np.asarray(res.results[c]["xout"])
        out[b, r * 2048:(r + 1) * 2048, :] = xo.T
    return out


# version 15 (final)
# speedup vs baseline: 1.0931x; 1.0610x over previous
import numpy as np
import ml_dtypes
import concourse.bass as bass
import concourse.mybir as mybir
from concourse.bass import IndirectOffsetOnAxis
from concourse.bass_utils import run_bass_kernel_spmd
from contextlib import ExitStack

F32 = mybir.dt.float32
BF16 = mybir.dt.bfloat16
U32 = mybir.dt.uint32
AX = mybir.AxisListType.X
ALU = mybir.AluOpType
AF = mybir.ActivationFunctionType
BF = ml_dtypes.bfloat16

NCORES = 8
B, S, D = 4, 4096, 2048
H, DH = 16, 128
QP = 1024
KVP = 1365
KVPP = 1408
INTER = 5632
EPS = 1e-5
SCALE = 1.0 / float(np.sqrt(np.float32(128.0)))
NEG = -1e9
T = B * S
TC = 2048
PAIRS = [[0, 1], [2, 3], [4, 5], [6, 7]]

ENGS = ("sync", "gpsimd", "tensor", "vector", "scalar")
DMA_POOL = 8


class Prog:

    def __init__(self, nc, ctx):
        self.nc = nc
        self.ops = []
        self.last_writer = {}
        self.readers = {}
        self.eng_count = {e: 0 for e in ENGS}
        self.sems = {e: ctx.enter_context(nc.semaphore(f"se_{e}")) for e in ENGS}
        self.dpool = {e: [ctx.enter_context(nc.semaphore(f"sd_{e}{i}"))
                          for i in range(DMA_POOL)] for e in ("sync", "gpsimd", "scalar")}
        self.dcount = {e: [0] * DMA_POOL for e in ("sync", "gpsimd", "scalar")}
        self.drr = {e: 0 for e in ("sync", "gpsimd", "scalar")}
        self.ccsems = [ctx.enter_context(nc.semaphore(f"scc{i}")) for i in range(8)]
        self.ncc = 0
        self.fence_deps = ()

    def op(self, engine, fn, reads=(), writes=(), dma=False, after=()):
        deps = set(after)
        deps.update(self.fence_deps)
        for k in reads:
            if k in self.last_writer:
                deps.add(self.last_writer[k])
            if k.startswith("pb"):
                for rd in self.readers.get(k, ()):
                    if self.ops[rd][0] != engine:
                        deps.add(rd)
        for k in writes:
            if k in self.last_writer:
                deps.add(self.last_writer[k])
            for rd in self.readers.get(k, ()):
                deps.add(rd)
        i = len(self.ops)
        if dma:
            idx = self.drr[engine] % DMA_POOL
            self.drr[engine] += 1
            before = self.dcount[engine][idx]
            self.dcount[engine][idx] = before + 16
            info = ("dma", idx, before)
            pos = 0
        else:
            info = None
            self.eng_count[engine] += 1
            pos = self.eng_count[engine]
        self.ops.append((engine, fn, deps, info, pos))
        for k in reads:
            self.readers.setdefault(k, []).append(i)
        for k in writes:
            self.last_writer[k] = i
            self.readers[k] = []
        return i

    def cc(self, fn):
        sem = self.ccsems[self.ncc]
        self.ncc += 1
        deps = set(self.fence_deps)
        i = len(self.ops)
        self.ops.append(("gpsimd", fn, deps, ("cc", sem), 0))
        return i

    def fence(self):
        deps = []
        last_eng = {}
        last_dma = {}
        last_cc = []
        for i, (e, fn, d, info, pos) in enumerate(self.ops):
            if info is None:
                last_eng[e] = i
            elif info[0] == "dma":
                last_dma[(e, info[1])] = i
            else:
                last_cc.append(i)
        deps.extend(last_eng.values())
        deps.extend(last_dma.values())
        deps.extend(last_cc[-4:])
        self.fence_deps = tuple(deps)

    def emit(self):
        nc = self.nc
        with nc.Block() as block:
            for ename in ENGS:
                self._emit_engine(block, ename)

    def _emit_engine(self, block, ename):
        prog = self

        def body(eng):
            wm = {}

            def wait(sem, val):
                if val > wm.get(id(sem), 0):
                    eng.wait_ge(sem, val)
                    wm[id(sem)] = val

            def do_waits(deps):
                for d in sorted(deps):
                    de, dfn, ddeps, dinfo, dpos = prog.ops[d]
                    if dinfo is not None and dinfo[0] == "dma":
                        wait(prog.dpool[de][dinfo[1]], dinfo[2] + 16)
                    elif dinfo is not None and dinfo[0] == "cc":
                        wait(dinfo[1], 1)
                    else:
                        if de == ename and ename == "tensor":
                            continue
                        wait(prog.sems[de], dpos)

            for i, (e, fn, deps, info, pos) in enumerate(prog.ops):
                if e != ename:
                    continue
                do_waits(deps)
                if info is not None and info[0] == "dma":
                    idx, before = info[1], info[2]
                    if before > 0:
                        wait(prog.dpool[ename][idx], before)
                    inst = fn(eng)
                    inst.then_inc(prog.dpool[ename][idx], 16)
                elif info is not None and info[0] == "cc":
                    inst = fn(eng)
                    inst.then_inc(info[1])
                else:
                    inst = fn(eng)
                    inst.then_inc(prog.sems[ename], 1)
            if ename == "sync":
                for q, pool in prog.dpool.items():
                    for idx, sem in enumerate(pool):
                        if prog.dcount[q][idx] > 0:
                            wait(sem, prog.dcount[q][idx])
                for i in range(prog.ncc):
                    wait(prog.ccsems[i], 1)

        getattr(block, ename)(body)


class Arena:

    def __init__(self, nc, ctx, cols_f32):
        self.t = ctx.enter_context(nc.sbuf_tensor("arena", [128, cols_f32], F32))
        self.cap = cols_f32
        self.off = 0
        self.peak = 0

    def reset(self, to=0):
        self.off = to

    def f32(self, parts, cols):
        o = self.off
        self.off += cols
        self.peak = max(self.peak, self.off)
        assert self.off <= self.cap, f"arena overflow {self.off}>{self.cap}"
        return self.t[0:parts, o:o + cols]

    def bf16(self, parts, cols):
        cf = (cols + 1) // 2
        o = self.off
        self.off += cf
        self.peak = max(self.peak, self.off)
        assert self.off <= self.cap, f"arena overflow {self.off}>{self.cap}"
        return self.t.bitcast(BF16)[0:parts, 2 * o:2 * o + cols]


class Rows:

    def __init__(self, nc, ctx, cols):
        self.t = ctx.enter_context(nc.sbuf_tensor("rows", [1, cols], F32))
        self.cap = cols
        self.off = 0

    def reset(self):
        self.off = 0

    def f32(self, cols):
        o = self.off
        self.off += cols
        assert self.off <= self.cap, "rows overflow"
        return self.t[0:1, o:o + cols]


def _f32(a):
    return np.ascontiguousarray(np.asarray(a, dtype=np.float32))


def _bf(a):
    return np.ascontiguousarray(np.asarray(a, dtype=np.float32).astype(BF))


def _rope_tables_np():
    freqs = (1.0 / (np.float32(10000.0) **
                    (np.arange(0, 128, 2, dtype=np.float32) / np.float32(128.0))))
    pos = np.arange(S, dtype=np.float32)
    emb = pos[:, None] * freqs[None, :32]
    return np.cos(emb).astype(np.float32).T.copy(), \
        np.sin(emb).astype(np.float32).T.copy()


def q_pack_perm8(r):
    perm = np.zeros(1024, np.int64)
    for c in range(4):
        for half in range(2):
            j = 2 * c + half
            h = 8 * r + j
            perm[c * 128 + half * 64: c * 128 + half * 64 + 64] = \
                h * 128 + np.arange(64)
    for g in range(2):
        for hh in range(4):
            j = 4 * g + hh
            h = 8 * r + j
            perm[(4 + g) * 128 + hh * 32:(4 + g) * 128 + hh * 32 + 32] = \
                h * 128 + 64 + 2 * np.arange(32)
    for g in range(2):
        for hh in range(4):
            j = 4 * g + hh
            h = 8 * r + j
            perm[(6 + g) * 128 + hh * 32:(6 + g) * 128 + hh * 32 + 32] = \
                h * 128 + 64 + 2 * np.arange(32) + 1
    return perm


def q_head_rows8(j):
    return ((j // 2) * 128 + (j % 2) * 64,
            (4 + j // 4) * 128 + (j % 4) * 32,
            (6 + j // 4) * 128 + (j % 4) * 32)


def stage_inputs(ins):
    sh = {}
    x = ins["L0_attn_x"]
    xf = _f32(x.reshape(T, D).T)

    sh["wq_down"] = _bf(ins["L0_attn_Wq_down"])
    wkv = np.asarray(ins["L0_attn_Wkv_down"], np.float32)
    lora = np.zeros((D, KVPP), np.float32)
    lora[:, :KVP] = wkv[:, :KVP]
    sh["wkv_lora"] = _bf(lora)
    wr = wkv[:, KVP:]
    sh["wkv_rope"] = _bf(np.concatenate([wr[:, 0::2], wr[:, 1::2]], axis=1))

    sh["gq"] = _f32(np.asarray(ins["L0_attn_gq"]).reshape(8, 128).T)
    gkv = np.zeros(KVPP, np.float32)
    gkv[:KVP] = np.asarray(ins["L0_attn_gkv"])
    sh["gkv"] = _f32(gkv.reshape(11, 128).T)
    sh["wo"] = _bf(ins["L0_attn_Wo"])

    cosT, sinT = _rope_tables_np()
    sh["cos4"] = _f32(np.tile(cosT, (4, 1)))
    sh["sin4"] = _f32(np.tile(sinT, (4, 1)))

    wkvu = np.asarray(ins["L0_attn_Wkv_up"], np.float32)
    wkvu_p = np.zeros((KVPP, 3072), np.float32)
    wkvu_p[:KVP] = wkvu
    per_head = wkvu_p.reshape(KVPP, H, 192)

    for L in ("L0", "L1"):
        gw = np.asarray(ins[f"{L}_ffn_gate_w"], np.float32)
        uw = np.asarray(ins[f"{L}_ffn_up_w"], np.float32)
        dw = np.asarray(ins[f"{L}_ffn_down_w"], np.float32)
        sh[f"{L}_gws"] = _bf(gw.reshape(44, 128, 16, 128).transpose(0, 3, 2, 1)
                              .reshape(44, 128, 2048))
        sh[f"{L}_uws"] = _bf(uw.reshape(44, 128, 16, 128).transpose(0, 3, 2, 1)
                              .reshape(44, 128, 2048))
        sh[f"{L}_dws"] = _bf(dw.reshape(16, 128, 44, 128).transpose(0, 3, 2, 1)
                              .reshape(16, 128, 5632))
        sh[f"{L}_gs"] = _f32(np.asarray(ins[f"{L}_ffn_gate_s"]).reshape(44, 128).T)
        sh[f"{L}_us"] = _f32(np.asarray(ins[f"{L}_ffn_up_s"]).reshape(44, 128).T)
        sh[f"{L}_ds"] = _f32(np.asarray(ins[f"{L}_ffn_down_s"]).reshape(16, 128).T)

    sh["ones_colb"] = np.ones((128, 1), BF)
    sh["ones_colf"] = np.ones((128, 1), np.float32)
    sh["ones_row"] = np.ones((1, 128), np.float32)
    sh["ones512"] = np.ones((1, 512), np.float32)
    sh["identd"] = np.eye(128).astype(BF)
    sh["ident8f"] = np.eye(8).astype(np.float32)
    kk2 = np.arange(128)[:, None]
    qq2 = np.arange(512)[None, :]
    me = np.zeros((128, 4, 512), np.float32)
    for jj in range(4):
        me[:, jj, :] = np.where(jj * 128 + kk2 > qq2, NEG, 0.0)
    sh["maskE"] = _f32(me.reshape(128, 2048))
    ml = np.zeros((128, 384), np.float32)
    ml[64:128, 0:128] = NEG
    ml[0:64, 256:384] = NEG
    sh["maskL"] = _f32(ml)
    sh["io_row"] = _f32(np.tile(np.arange(64, dtype=np.float32)[None, :], (128, 1)))
    sh["io_pp"] = _f32(np.arange(64, dtype=np.float32).reshape(64, 1))

    wq1 = np.asarray(ins["L1_attn_Wq"], np.float32)
    wk1 = np.asarray(ins["L1_attn_Wk"], np.float32)
    wv1 = np.asarray(ins["L1_attn_Wv"], np.float32)
    bq1 = np.asarray(ins["L1_attn_bq"], np.float32)
    bk1 = np.asarray(ins["L1_attn_bk"], np.float32)
    bv1 = np.asarray(ins["L1_attn_bv"], np.float32)

    i128 = np.arange(128, dtype=np.uint32)[:, None]
    cores = []
    for c in range(NCORES):
        b, r = c // 2, c % 2
        pc = {}
        pc["x_f"] = _f32(xf[:, c * TC:(c + 1) * TC])
        pc["wq_up"] = _bf(np.asarray(ins["L0_attn_Wq_up"], np.float32)[:, q_pack_perm8(r)])
        hs = slice(8 * r, 8 * r + 8)
        pc["wkv_up_k"] = _bf(per_head[:, hs, :64].reshape(KVPP, 512))
        pc["wkv_up_v"] = _bf(per_head[:, hs, 64:].reshape(KVPP, 1024))
        ps = slice(r * 2048, r * 2048 + 2048)
        pc["cs_own"] = _f32(np.concatenate([cosT[:, ps], sinT[:, ps]], 0))
        pc["sc_own"] = _f32(np.concatenate([sinT[:, ps], cosT[:, ps]], 0))
        cs = slice(1024 * r, 1024 * r + 1024)
        pc["wq1"] = _bf(wq1[:, cs])
        pc["wk1"] = _bf(wk1[:, cs])
        pc["wv1"] = _bf(wv1[:, cs])
        pc["bq1"] = _f32(bq1[cs].reshape(1, 1024))
        pc["bqpp"] = _f32(bq1[cs].reshape(8, 128).T)
        pc["bkpp"] = _f32(bk1[cs].reshape(8, 128).T)
        pc["bk1"] = _f32(bk1[cs].reshape(1, 1024))
        pc["bv1"] = _f32(bv1[cs].reshape(1, 1024))
        pc["idx_fcq"] = (r * 1024 + np.arange(8, dtype=np.uint32)[None, :] * 128
                         + i128).astype(np.uint32)
        pc["idx_fckv"] = (r * 1408 + np.arange(11, dtype=np.uint32)[None, :] * 128
                          + i128).astype(np.uint32)
        pc["idx_fkr"] = (r * 64 + np.arange(64, dtype=np.uint32)[:, None]
                         ).astype(np.uint32)
        pc["idx_h"] = ((8 * r + np.arange(8, dtype=np.uint32)[None, :]) * 128
                       + i128).astype(np.uint32)
        pc["idx_t"] = (r * 2048 + np.arange(16, dtype=np.uint32)[None, :] * 128
                       + i128).astype(np.uint32)
        pc["idx_c"] = (2 * (np.arange(16, dtype=np.uint32)[None, :] * 128 + i128)
                       + r).astype(np.uint32)
        cores.append(pc)
    return sh, cores


def em_AB(p, t, C, ar, rw):
    xsb = ar.bf16(128, 16 * TC)
    for tt in range(4):
        for k in range(16):
            p.op("gpsimd", lambda e, k=k, tt=tt: e.dma_start(
                out=xsb[:, k * TC + tt * 512: k * TC + (tt + 1) * 512],
                in_=t["x_f"][k * 128:(k + 1) * 128, tt * 512:(tt + 1) * 512]),
                writes=[f"x{k}_{tt}"], dma=True)
    a0 = ar.off
    wsb = ar.bf16(128, 16 * QP)
    craw = ar.f32(128, 8 * 512)
    sqs = ar.bf16(128, 512)
    outsb = ar.bf16(128, 8 * 512)
    rowbuf = rw.f32(3 * 512)
    for k in range(16):
        p.op("sync", lambda e, k=k: e.dma_start(
            out=wsb[:, k * QP:(k + 1) * QP], in_=t["wq_down"][k * 128:(k + 1) * 128, :]),
            writes=[f"A:w{k}"], dma=True)
    NT = TC // 512

    def epiA(m):
        bank = m % 4
        cp = p.op("vector", lambda e, m=m, bank=bank: e.tensor_copy(
            craw[:, m * 512:(m + 1) * 512], p.pb[bank][:, :]),
            reads=[f"pb{bank}"], writes=[f"A:craw{m}"])
        p.op("scalar", lambda e, bank=bank: e.activation(
            sqs[:, :], p.pb[bank][:, :], AF.Square),
            reads=[f"pb{bank}"], writes=["A:sqs"], after=[cp])
        p.op("tensor", lambda e, m=m: e.matmul(
            p.pb[4][0:1, :], C["onc"][:, 0:1], sqs[:, :],
            start=(m == 0), stop=(m == 7)),
            reads=["A:sqs", "onc"],
            writes=["pb4"] if m in (0, 7) else [])

    for tt in range(NT):
        t0 = tt * 512
        for m in range(8):
            bank = m % 4
            for k in range(16):
                p.op("tensor", lambda e, k=k, m=m, t0=t0, bank=bank: e.matmul(
                    p.pb[bank][:, :],
                    wsb[:, k * QP + m * 128: k * QP + (m + 1) * 128],
                    xsb[:, k * TC + t0: k * TC + t0 + 512],
                    start=(k == 0), stop=(k == 15)),
                    reads=[f"x{k}_{tt}", f"A:w{k}"],
                    writes=[f"pb{bank}"] if k in (0, 15) else [])
            if m > 0:
                epiA(m - 1)
        epiA(7)
        p.op("vector", lambda e: e.tensor_scalar(
            rowbuf[0:1, 0:512], p.pb[4][0:1, :], 1.0 / QP, EPS, ALU.mult, ALU.add),
            reads=["pb4"], writes=["A:row0"])
        p.op("scalar", lambda e: e.activation(
            rowbuf[0:1, 512:1024], rowbuf[0:1, 0:512], AF.Ln),
            reads=["A:row0"], writes=["A:row1"])
        p.op("scalar", lambda e: e.activation(
            rowbuf[0:1, 1024:1536], rowbuf[0:1, 512:1024], AF.Exp, scale=-0.5),
            reads=["A:row1"], writes=["A:row2"])
        p.op("tensor", lambda e: e.matmul(
            p.pb[5][:, :], C["onr"][0:1, 0:128], rowbuf[0:1, 1024:1536],
            start=True, stop=True),
            reads=["A:row2", "onr"], writes=["pb5"])
        for m in range(8):
            p.op("vector", lambda e, m=m: e.scalar_tensor_tensor(
                outsb[:, m * 512:(m + 1) * 512],
                craw[:, m * 512:(m + 1) * 512],
                C["gq"][:, m:m + 1], p.pb[5][:, :], ALU.mult, ALU.mult),
                reads=[f"A:craw{m}", "gq", "pb5"], writes=[f"A:out{m}"])
            p.op("gpsimd", lambda e, m=m, t0=t0: e.indirect_dma_start(
                t["cq_sh"][:, :],
                IndirectOffsetOnAxis(ap=C["idx_fcq"][:, m:m + 1], axis=0),
                outsb[:, m * 512:(m + 1) * 512], None, element_offset=t0),
                reads=[f"A:out{m}", "idx"], dma=True)
    p.fence()
    ar.reset(a0)
    wsb = ar.bf16(128, 16 * KVPP)
    wrs = ar.bf16(128, 16 * 64)
    css = ar.f32(64, 2048)
    scs = ar.f32(64, 2048)
    craw = ar.f32(128, 11 * 512)
    kraw = ar.f32(64, 2048)
    ksc1 = ar.f32(64, 1024)
    ksc2 = ar.f32(64, 1024)
    kout = ar.bf16(64, 2048)
    sqs = ar.bf16(128, 512)
    outsb = ar.bf16(128, 11 * 512)
    for k in range(16):
        p.op("sync", lambda e, k=k: e.dma_start(
            out=wsb[:, k * KVPP:(k + 1) * KVPP],
            in_=t["wkv_lora"][k * 128:(k + 1) * 128, :]),
            writes=[f"B:w{k}"], dma=True)
        p.op("sync", lambda e, k=k: e.dma_start(
            out=wrs[:, k * 64:(k + 1) * 64],
            in_=t["wkv_rope"][k * 128:(k + 1) * 128, :]),
            writes=[f"B:wr{k}"], dma=True)
    p.op("sync", lambda e: e.dma_start(out=css[:, :], in_=t["cs_own"]),
         writes=["B:cs"], dma=True)
    p.op("sync", lambda e: e.dma_start(out=scs[:, :], in_=t["sc_own"]),
         writes=["B:sc"], dma=True)
    rowbuf = rw.f32(3 * 512)
    def epiB(m):
        bank = m % 4
        cp = p.op("vector", lambda e, m=m, bank=bank: e.tensor_copy(
            craw[:, m * 512:(m + 1) * 512], p.pb[bank][:, :]),
            reads=[f"pb{bank}"], writes=[f"B:craw{m}"])
        p.op("scalar", lambda e, bank=bank: e.activation(
            sqs[:, :], p.pb[bank][:, :], AF.Square),
            reads=[f"pb{bank}"], writes=["B:sqs"], after=[cp])
        p.op("tensor", lambda e, m=m: e.matmul(
            p.pb[4][0:1, :], C["onc"][:, 0:1], sqs[:, :],
            start=(m == 0), stop=(m == 10)),
            reads=["B:sqs", "onc"],
            writes=["pb4"] if m in (0, 10) else [])

    for tt in range(NT):
        t0 = tt * 512
        for m in range(11):
            bank = m % 4
            for k in range(16):
                p.op("tensor", lambda e, k=k, m=m, t0=t0, bank=bank: e.matmul(
                    p.pb[bank][:, :],
                    wsb[:, k * KVPP + m * 128: k * KVPP + (m + 1) * 128],
                    xsb[:, k * TC + t0: k * TC + t0 + 512],
                    start=(k == 0), stop=(k == 15)),
                    reads=[f"x{k}_{tt}", f"B:w{k}"],
                    writes=[f"pb{bank}"] if k in (0, 15) else [])
            if m > 0:
                epiB(m - 1)
        for k in range(16):
            p.op("tensor", lambda e, k=k, t0=t0: e.matmul(
                p.pb[5][0:64, :],
                wrs[:, k * 64:(k + 1) * 64],
                xsb[:, k * TC + t0: k * TC + t0 + 512],
                start=(k == 0), stop=(k == 15)),
                reads=[f"x{k}_{tt}", f"B:wr{k}"],
                writes=["pb5"] if k in (0, 15) else [])
        epiB(10)
        p.op("vector", lambda e, t0=t0: e.tensor_copy(
            kraw[:, t0:t0 + 512], p.pb[5][0:64, :]),
            reads=["pb5"], writes=["B:kraw"])
        p.op("vector", lambda e: e.tensor_scalar(
            rowbuf[0:1, 0:512], p.pb[4][0:1, :], 1.0 / KVP, EPS, ALU.mult, ALU.add),
            reads=["pb4"], writes=["B:row0"])
        p.op("scalar", lambda e: e.activation(
            rowbuf[0:1, 512:1024], rowbuf[0:1, 0:512], AF.Ln),
            reads=["B:row0"], writes=["B:row1"])
        p.op("scalar", lambda e: e.activation(
            rowbuf[0:1, 1024:1536], rowbuf[0:1, 512:1024], AF.Exp, scale=-0.5),
            reads=["B:row1"], writes=["B:row2"])
        p.op("tensor", lambda e: e.matmul(
            p.pb[6][:, :], C["onr"][0:1, 0:128], rowbuf[0:1, 1024:1536],
            start=True, stop=True),
            reads=["B:row2", "onr"], writes=["pb6"])
        for m in range(11):
            p.op("vector", lambda e, m=m: e.scalar_tensor_tensor(
                outsb[:, m * 512:(m + 1) * 512],
                craw[:, m * 512:(m + 1) * 512],
                C["gkv"][:, m:m + 1], p.pb[6][:, :], ALU.mult, ALU.mult),
                reads=[f"B:craw{m}", "gkv", "pb6"], writes=[f"B:out{m}"])
            p.op("gpsimd", lambda e, m=m, t0=t0: e.indirect_dma_start(
                t["ckv_sh"][:, :],
                IndirectOffsetOnAxis(ap=C["idx_fckv"][:, m:m + 1], axis=0),
                outsb[:, m * 512:(m + 1) * 512], None, element_offset=t0),
                reads=[f"B:out{m}", "idx"], dma=True)
    for hh in range(2):
        c0 = hh * 1024
        cs_ = slice(c0, c0 + 1024)
        p.op("vector", lambda e, cs_=cs_: e.tensor_mul(
            ksc1[0:32, :], kraw[0:32, cs_], css[0:32, cs_]),
            reads=["B:kraw", "B:cs"], writes=["B:ksc1a"])
        p.op("vector", lambda e, cs_=cs_: e.tensor_mul(
            ksc1[32:64, :], kraw[32:64, cs_], css[32:64, cs_]),
            reads=["B:kraw", "B:cs"], writes=["B:ksc1b"])
        p.op("vector", lambda e, cs_=cs_: e.tensor_mul(
            ksc2[0:32, :], kraw[0:32, cs_], scs[0:32, cs_]),
            reads=["B:kraw", "B:sc"], writes=["B:ksc2a"])
        p.op("vector", lambda e, cs_=cs_: e.tensor_mul(
            ksc2[32:64, :], kraw[32:64, cs_], scs[32:64, cs_]),
            reads=["B:kraw", "B:sc"], writes=["B:ksc2b"])
        p.op("vector", lambda e, cs_=cs_: e.tensor_copy(kraw[0:32, cs_], ksc1[32:64, :]),
             reads=["B:ksc1b"], writes=["B:kraw"])
        p.op("vector", lambda e, cs_=cs_: e.tensor_sub(
            kout[0:32, cs_], ksc1[0:32, :], kraw[0:32, cs_]),
            reads=["B:ksc1a", "B:kraw"], writes=[f"B:ne{hh}"])
        p.op("vector", lambda e, cs_=cs_: e.tensor_copy(kraw[32:64, cs_], ksc2[0:32, :]),
             reads=["B:ksc2a"], writes=["B:kraw"])
        p.op("vector", lambda e, cs_=cs_: e.tensor_add(
            kout[32:64, cs_], ksc2[32:64, :], kraw[32:64, cs_]),
            reads=["B:ksc2b", "B:kraw"], writes=[f"B:no{hh}"])
    p.op("gpsimd", lambda e: e.indirect_dma_start(
        t["kr_sh"][:, :],
        IndirectOffsetOnAxis(ap=C["idx_fkr"][0:64, 0:1], axis=0),
        kout[0:64, :], None),
        reads=["B:ne0", "B:no0", "B:ne1", "B:no1", "idx"], dma=True)


def em_C(p, t, C, ar, rw):
    csb = ar.bf16(128, 8 * S)
    wsb = ar.bf16(128, 8 * QP)
    c4 = ar.f32(128, S)
    s4 = ar.f32(128, S)
    rraw = ar.f32(128, 4 * 512)
    rsc1 = ar.f32(128, 512)
    rsc2 = ar.f32(128, 512)
    outsb = ar.bf16(128, 8 * 512)
    for k in range(8):
        for pp in range(2):
            p.op("sync", lambda e, k=k, pp=pp: e.dma_start(
                out=csb[:, k * S + pp * 2048: k * S + pp * 2048 + 2048],
                in_=t["cq_sh"][pp * 1024 + k * 128: pp * 1024 + (k + 1) * 128, :]),
                writes=[f"C:c{k}"], dma=True)
        p.op("sync", lambda e, k=k: e.dma_start(
            out=wsb[:, k * QP:(k + 1) * QP],
            in_=t["wq_up"][k * 128:(k + 1) * 128, :]),
            writes=[f"C:w{k}"], dma=True)
    p.op("sync", lambda e: e.dma_start(out=c4[:, :], in_=t["cos4"]),
         writes=["C:cos"], dma=True)
    p.op("sync", lambda e: e.dma_start(out=s4[:, :], in_=t["sin4"]),
         writes=["C:sin"], dma=True)
    for tt in range(S // 512):
        t0 = tt * 512
        for m in range(8):
            bank = m % 4
            for k in range(8):
                p.op("tensor", lambda e, k=k, m=m, t0=t0, bank=bank: e.matmul(
                    p.pb[bank][:, :],
                    wsb[:, k * QP + m * 128: k * QP + (m + 1) * 128],
                    csb[:, k * S + t0: k * S + t0 + 512],
                    start=(k == 0), stop=(k == 7)),
                    reads=[f"C:c{k}", f"C:w{k}"],
                    writes=[f"pb{bank}"] if k in (0, 7) else [])
            if m < 4:
                p.op("vector", lambda e, m=m, bank=bank: e.tensor_copy(
                    outsb[:, m * 512:(m + 1) * 512], p.pb[bank][:, :]),
                    reads=[f"pb{bank}"], writes=[f"C:out{m}"])
            else:
                p.op("vector", lambda e, m=m, bank=bank: e.tensor_copy(
                    rraw[:, (m - 4) * 512:(m - 3) * 512], p.pb[bank][:, :]),
                    reads=[f"pb{bank}"], writes=[f"C:rr{m - 4}"])
        for g in range(2):
            ek, ok = f"C:rr{g}", f"C:rr{2 + g}"
            E = rraw[:, g * 512:(g + 1) * 512]
            O = rraw[:, (2 + g) * 512:(3 + g) * 512]
            p.op("vector", lambda e, E=E, t0=t0: e.tensor_mul(
                rsc1[:, :], E, c4[:, t0:t0 + 512]),
                reads=[ek, "C:cos"], writes=["C:rs1"])
            p.op("vector", lambda e, O=O, t0=t0: e.tensor_mul(
                rsc2[:, :], O, s4[:, t0:t0 + 512]),
                reads=[ok, "C:sin"], writes=["C:rs2"])
            p.op("vector", lambda e, g=g: e.tensor_sub(
                outsb[:, (4 + g) * 512:(5 + g) * 512], rsc1[:, :], rsc2[:, :]),
                reads=["C:rs1", "C:rs2"], writes=[f"C:out{4 + g}"])
            p.op("vector", lambda e, O=O, t0=t0: e.tensor_mul(
                rsc1[:, :], O, c4[:, t0:t0 + 512]),
                reads=[ok, "C:cos"], writes=["C:rs1"])
            p.op("vector", lambda e, E=E, t0=t0: e.tensor_mul(
                rsc2[:, :], E, s4[:, t0:t0 + 512]),
                reads=[ek, "C:sin"], writes=["C:rs2"])
            p.op("vector", lambda e, g=g: e.tensor_add(
                outsb[:, (6 + g) * 512:(7 + g) * 512], rsc1[:, :], rsc2[:, :]),
                reads=["C:rs1", "C:rs2"], writes=[f"C:out{6 + g}"])
        for m in range(8):
            p.op("sync", lambda e, m=m, t0=t0: e.dma_start(
                out=t["q0"][m * 128:(m + 1) * 128, t0:t0 + 512],
                in_=outsb[:, m * 512:(m + 1) * 512]),
                reads=[f"C:out{m}"], dma=True)


def em_D(p, t, C, ar, rw):
    csb = ar.bf16(128, 11 * S)
    wks = ar.bf16(128, 11 * 512)
    wvs = ar.bf16(128, 11 * 1024)
    outsb = ar.bf16(128, 4 * 512)
    vosb = ar.bf16(128, 2 * 512)
    for k in range(11):
        for pp in range(2):
            p.op("sync", lambda e, k=k, pp=pp: e.dma_start(
                out=csb[:, k * S + pp * 2048: k * S + pp * 2048 + 2048],
                in_=t["ckv_sh"][pp * 1408 + k * 128: pp * 1408 + (k + 1) * 128, :]),
                writes=[f"D:c{k}"], dma=True)
        p.op("sync", lambda e, k=k: e.dma_start(
            out=wks[:, k * 512:(k + 1) * 512],
            in_=t["wkv_up_k"][k * 128:(k + 1) * 128, :]),
            writes=[f"D:wk{k}"], dma=True)
        p.op("sync", lambda e, k=k: e.dma_start(
            out=wvs[:, k * 1024:(k + 1) * 1024],
            in_=t["wkv_up_v"][k * 128:(k + 1) * 128, :]),
            writes=[f"D:wv{k}"], dma=True)
    for tt in range(S // 512):
        t0 = tt * 512
        for m in range(4):
            bank = m % 4
            for k in range(11):
                p.op("tensor", lambda e, k=k, m=m, t0=t0, bank=bank: e.matmul(
                    p.pb[bank][:, :],
                    wks[:, k * 512 + m * 128: k * 512 + (m + 1) * 128],
                    csb[:, k * S + t0: k * S + t0 + 512],
                    start=(k == 0), stop=(k == 10)),
                    reads=[f"D:c{k}", f"D:wk{k}"],
                    writes=[f"pb{bank}"] if k in (0, 10) else [])
            p.op("vector", lambda e, m=m, bank=bank: e.tensor_copy(
                outsb[:, m * 512:(m + 1) * 512], p.pb[bank][:, :]),
                reads=[f"pb{bank}"], writes=[f"D:out{m}"])
            p.op("sync", lambda e, m=m, t0=t0: e.dma_start(
                out=t["knope"][m * 128:(m + 1) * 128, t0:t0 + 512],
                in_=outsb[:, m * 512:(m + 1) * 512]),
                reads=[f"D:out{m}"], dma=True)
    for ttile in range(S // 128):
        for v4 in range(2):
            bank = 4 + (2 * ttile + v4) % 4
            for k in range(11):
                p.op("tensor", lambda e, k=k, ttile=ttile, v4=v4, bank=bank: e.matmul(
                    p.pb[bank][:, :],
                    csb[:, k * S + ttile * 128: k * S + (ttile + 1) * 128],
                    wvs[:, k * 1024 + v4 * 512: k * 1024 + (v4 + 1) * 512],
                    start=(k == 0), stop=(k == 10)),
                    reads=[f"D:c{k}", f"D:wv{k}"],
                    writes=[f"pb{bank}"] if k in (0, 10) else [])
            p.op("vector", lambda e, v4=v4, bank=bank: e.tensor_copy(
                vosb[:, v4 * 512:(v4 + 1) * 512], p.pb[bank][:, :]),
                reads=[f"pb{bank}"], writes=[f"D:vo{v4}"])
            p.op("sync", lambda e, ttile=ttile, v4=v4: e.dma_start(
                out=t["vt"][ttile * 128:(ttile + 1) * 128, v4 * 512:(v4 + 1) * 512],
                in_=vosb[:, v4 * 512:(v4 + 1) * 512]),
                reads=[f"D:vo{v4}"], dma=True)


def _bound_c(p, q_key, q_sb, sq_sb, onc, srow, slot, pf):
    for c8 in range(8):
        sb0 = (c8 % 2) * 512
        bk = 6 + c8 % 2
        p.op("scalar", lambda e, c8=c8, sb0=sb0: e.activation(
            sq_sb[:, sb0:sb0 + 512], q_sb[:, c8 * 512:(c8 + 1) * 512], AF.Square),
            reads=[q_key], writes=[f"{pf}:bsq{c8 % 2}"])
        p.op("tensor", lambda e, sb0=sb0, bk=bk: e.matmul(
            p.pb[bk][0:1, 0:512], onc[:, 0:1], sq_sb[:, sb0:sb0 + 512],
            start=True, stop=True),
            reads=[f"{pf}:bsq{c8 % 2}", "onc"], writes=[f"pb{bk}"])
        p.op("vector", lambda e, c8=c8, slot=slot, bk=bk: e.reduce_max(
            srow[0:1, 16 + slot * 8 + c8:17 + slot * 8 + c8],
            p.pb[bk][0:1, 0:512], axis=AX),
            reads=[f"pb{bk}"], writes=[f"{pf}:m8_{slot}_{c8}"])
    p.op("vector", lambda e, slot=slot: e.reduce_max(
        srow[0:1, slot:slot + 1],
        srow[0:1, 16 + slot * 8:16 + slot * 8 + 8], axis=AX),
        reads=[f"{pf}:m8_{slot}_{c8}" for c8 in range(8)], writes=[f"{pf}:mx{slot}"])


def _negc_from_bounds(p, C, srow, negc_col, pf, pr=0):
    p.op("vector", lambda e: e.tensor_mul(
        srow[0:1, 2:3], srow[0:1, 0:1], srow[0:1, 1:2]),
        reads=[f"{pf}:mx0", f"{pf}:mx1"], writes=[f"{pf}:prod"])
    p.op("scalar", lambda e: e.activation(
        srow[0:1, 3:4], srow[0:1, 2:3], AF.Ln),
        reads=[f"{pf}:prod"], writes=[f"{pf}:lnp"])
    p.op("scalar", lambda e: e.activation(
        srow[0:1, 4:5], srow[0:1, 3:4], AF.Exp, scale=0.5),
        reads=[f"{pf}:lnp"], writes=[f"{pf}:sqp"])
    p.op("vector", lambda e: e.tensor_scalar_mul(
        srow[0:1, 5:6], srow[0:1, 4:5], -SCALE),
        reads=[f"{pf}:sqp"], writes=[f"{pf}:negc_row"])
    p.op("tensor", lambda e: e.matmul(
        p.pb[6][:, 0:1], C["onr"][0:1, 0:128], srow[0:1, 5:6],
        start=True, stop=True),
        reads=[f"{pf}:negc_row", "onr"], writes=["pb6"])
    p.op("vector", lambda e: e.tensor_copy(negc_col[:, 0:1], p.pb[6][:, 0:1]),
         reads=["pb6"], writes=[f"{pf}:negc{pr}"])


def em_E(p, t, C, ar, rw):
    q_sb = ar.bf16(128, 2 * S)
    k_sb = ar.bf16(128, 2 * S)
    vt_sb = ar.bf16(128, 2 * S)
    msk = ar.f32(128, 2048)
    sq_sb = ar.bf16(128, 2 * 512)
    negc = ar.f32(128, 2)
    probs = ar.bf16(128, 4 * 512)
    pacc = ar.f32(128, 4 * 512)
    rbc = ar.f32(128, 512)
    ctxo = ar.bf16(128, 512)
    srow = rw.f32(80)
    rrow = rw.f32(512)
    p.op("sync", lambda e: e.dma_start(out=msk[:, :], in_=t["maskE"]),
         writes=["E:msk"], dma=True)

    def loads(j):
        pr = j % 2
        o = pr * S
        nof, evf, odf = q_head_rows8(j)
        p.op("sync", lambda e, nof=nof, o=o: e.dma_start(
            out=q_sb[0:64, o:o + S], in_=t["q0"][nof:nof + 64, :]),
            writes=[f"E:q{pr}"], dma=True)
        p.op("sync", lambda e, evf=evf, o=o: e.dma_start(
            out=q_sb[64:96, o:o + S], in_=t["q0"][evf:evf + 32, :]),
            writes=[f"E:q{pr}"], dma=True)
        p.op("sync", lambda e, odf=odf, o=o: e.dma_start(
            out=q_sb[96:128, o:o + S], in_=t["q0"][odf:odf + 32, :]),
            writes=[f"E:q{pr}"], dma=True)
        p.op("sync", lambda e, j=j, o=o: e.dma_start(
            out=k_sb[0:64, o:o + S], in_=t["knope"][j * 64:(j + 1) * 64, :]),
            writes=[f"E:k{pr}"], dma=True)
        for pp in range(2):
            p.op("sync", lambda e, pp=pp, o=o: e.dma_start(
                out=k_sb[64:96, o + pp * 2048: o + (pp + 1) * 2048],
                in_=t["kr_sh"][pp * 64: pp * 64 + 32, :]),
                writes=[f"E:k{pr}"], dma=True)
            p.op("sync", lambda e, pp=pp, o=o: e.dma_start(
                out=k_sb[96:128, o + pp * 2048: o + (pp + 1) * 2048],
                in_=t["kr_sh"][pp * 64 + 32: pp * 64 + 64, :]),
                writes=[f"E:k{pr}"], dma=True)
        for g in range(8):
            p.op("sync", lambda e, j=j, g=g, o=o: e.dma_start(
                out=vt_sb[:, o + g * 512: o + (g + 1) * 512].rearrange(
                    "p (a d) -> p a d", d=128),
                in_=t["vt"][g * 512:(g + 1) * 512, j * 128:(j + 1) * 128].rearrange(
                    "(a p) d -> p a d", p=128)),
                writes=[f"E:v{pr}"], dma=True)

    loads(0)
    for j in range(8):
        pr = j % 2
        o = pr * S
        if j < 7:
            loads(j + 1)
        _bound_c(p, f"E:q{pr}", q_sb[:, o:o + S], sq_sb, C["onc"], srow, 0, "E")
        _bound_c(p, f"E:k{pr}", k_sb[:, o:o + S], sq_sb, C["onc"], srow, 1, "E")
        _negc_from_bounds(p, C, srow, negc[:, pr:pr + 1], "E", pr)

        def finkb(sl, kb, nkb, dn, pv):
            b0 = kb % 4
            ps = kb % 4
            jj = kb - 4 * sl
            if jj >= 0:
                p.op("vector", lambda e, jj=jj, b0=b0: e.tensor_add(
                    p.pb[b0][:, :], p.pb[b0][:, :], msk[:, jj * 512:(jj + 1) * 512]),
                    reads=[f"pb{b0}", "E:msk"], writes=[f"pb{b0}"])
            p.op("scalar", lambda e, b0=b0, ps=ps, pr=pr: e.activation(
                probs[:, ps * 512:(ps + 1) * 512], p.pb[b0][:, :], AF.Exp,
                bias=negc[:, pr:pr + 1], scale=SCALE),
                reads=[f"pb{b0}", f"E:negc{pr}"], writes=[f"E:probs{ps}"])
            sp = sl % 2
            pc = 2 * sp + kb % 2
            if kb < 2:
                p.op("gpsimd", lambda e, ps=ps, pc=pc: e.tensor_copy(
                    pacc[:, pc * 512:(pc + 1) * 512],
                    probs[:, ps * 512:(ps + 1) * 512]),
                    reads=[f"E:probs{ps}"], writes=[f"E:pacc{pc}"])
            else:
                p.op("gpsimd", lambda e, ps=ps, pc=pc: e.tensor_add(
                    pacc[:, pc * 512:(pc + 1) * 512],
                    pacc[:, pc * 512:(pc + 1) * 512],
                    probs[:, ps * 512:(ps + 1) * 512]),
                    reads=[f"E:probs{ps}", f"E:pacc{pc}"], writes=[f"E:pacc{pc}"])
            p.op("tensor", lambda e, kb=kb, ps=ps, nkb=nkb, pv=pv, o=o: e.matmul(
                p.pb[pv][:, :], vt_sb[:, o + kb * 128: o + (kb + 1) * 128],
                probs[:, ps * 512:(ps + 1) * 512],
                start=(kb == 0), stop=(kb == nkb - 1)),
                reads=[f"E:probs{ps}", f"E:v{pr}"],
                writes=[f"pb{pv}"] if kb in (0, nkb - 1) else [])
            if kb == nkb - 1:
                p.op("tensor", lambda e, dn=dn, sp=sp: e.matmul(
                    p.pb[dn][0:1, :], C["oncf"][:, 0:1],
                    pacc[:, (2 * sp) * 512:(2 * sp + 1) * 512],
                    start=True, stop=False),
                    reads=[f"E:pacc{2 * sp}", "oncf"], writes=[f"pb{dn}"])
                p.op("tensor", lambda e, dn=dn, sp=sp: e.matmul(
                    p.pb[dn][0:1, :], C["oncf"][:, 0:1],
                    pacc[:, (2 * sp + 1) * 512:(2 * sp + 2) * 512],
                    start=False, stop=True),
                    reads=[f"E:pacc{2 * sp + 1}", "oncf"], writes=[f"pb{dn}"])

        def slepi(sl, dn, pv):
            p.op("vector", lambda e, dn=dn: e.reciprocal(rrow[0:1, :], p.pb[dn][0:1, :]),
                 reads=[f"pb{dn}"], writes=["E:rrow"])
            p.op("tensor", lambda e, dn=dn: e.matmul(
                p.pb[dn][:, :], C["onr"][0:1, 0:128], rrow[0:1, :],
                start=True, stop=True),
                reads=["E:rrow", "onr"], writes=[f"pb{dn}"])
            p.op("vector", lambda e, dn=dn: e.tensor_copy(rbc[:, :], p.pb[dn][:, :]),
                 reads=[f"pb{dn}"], writes=["E:rbc"])
            p.op("vector", lambda e, pv=pv: e.tensor_mul(ctxo[:, :], p.pb[pv][:, :], rbc[:, :]),
                 reads=[f"pb{pv}", "E:rbc"], writes=["E:ctxo"])
            p.op("gpsimd", lambda e, j=j, sl=sl: e.indirect_dma_start(
                t["ctx0_sh"][:, :],
                IndirectOffsetOnAxis(ap=C["idx_h"][:, j:j + 1], axis=0),
                ctxo[:, :], None, element_offset=sl * 512),
                reads=["E:ctxo", "idx"], dma=True)

        prev = None
        for sl in range(8):
            nkb = 4 * sl + 4
            dn = 4 + (sl % 2) * 2
            pv = 5 + (sl % 2) * 2
            for kb in range(nkb):
                b0 = kb % 4
                p.op("tensor", lambda e, sl=sl, kb=kb, b0=b0, o=o: e.matmul(
                    p.pb[b0][:, :],
                    k_sb[:, o + kb * 128: o + (kb + 1) * 128],
                    q_sb[:, o + sl * 512: o + (sl + 1) * 512],
                    start=True, stop=True),
                    reads=[f"E:k{pr}", f"E:q{pr}"], writes=[f"pb{b0}"])
                if kb == 2 and prev is not None:
                    slepi(*prev)
                if kb > 1:
                    finkb(sl, kb - 2, nkb, dn, pv)
            finkb(sl, nkb - 2, nkb, dn, pv)
            finkb(sl, nkb - 1, nkb, dn, pv)
            prev = (sl, dn, pv)
        slepi(*prev)


def _ln_apply(p, C, pf, u1, nchunk, dim, outfn, srow, t1, outs):
    A_ = srow[0:1, 0:512]
    Bm = srow[0:1, 512:1024]
    Cc = srow[0:1, 1024:1536]
    p.op("vector", lambda e: e.tensor_scalar(A_, p.pb[6][0:1, :], 1.0 / dim, None, ALU.mult),
         reads=["pb6"], writes=[f"{pf}:lnA"])
    p.op("vector", lambda e: e.tensor_mul(Bm, A_, A_), reads=[f"{pf}:lnA"],
         writes=[f"{pf}:lnB"])
    p.op("vector", lambda e: e.tensor_scalar(Cc, p.pb[7][0:1, :], 1.0 / dim, EPS,
                                             ALU.mult, ALU.add),
         reads=["pb7"], writes=[f"{pf}:lnC"])
    p.op("vector", lambda e: e.tensor_sub(Cc, Cc, Bm),
         reads=[f"{pf}:lnB", f"{pf}:lnC"], writes=[f"{pf}:lnC"])
    p.op("scalar", lambda e: e.activation(Cc, Cc, AF.Ln),
         reads=[f"{pf}:lnC"], writes=[f"{pf}:lnC"])
    p.op("scalar", lambda e: e.activation(Cc, Cc, AF.Exp, scale=-0.5),
         reads=[f"{pf}:lnC"], writes=[f"{pf}:lnC"])
    p.op("vector", lambda e: e.tensor_mul(Bm, A_, Cc),
         reads=[f"{pf}:lnA", f"{pf}:lnC"], writes=[f"{pf}:lnB"])
    p.op("vector", lambda e: e.tensor_scalar_mul(Bm, Bm, -1.0),
         reads=[f"{pf}:lnB"], writes=[f"{pf}:lnB"])
    p.op("tensor", lambda e: e.matmul(
        p.pb[4][:, :], C["onr"][0:1, 0:128], Cc, start=True, stop=True),
        reads=[f"{pf}:lnC", "onr"], writes=["pb4"])
    p.op("tensor", lambda e: e.matmul(
        p.pb[5][:, :], C["onr"][0:1, 0:128], Bm, start=True, stop=True),
        reads=[f"{pf}:lnB", "onr"], writes=["pb5"])
    for m in range(nchunk):
        p.op("vector", lambda e, m=m: e.tensor_mul(
            t1[:, :], u1[:, m * 512:(m + 1) * 512], p.pb[4][:, :]),
            reads=[f"{pf}:u1_{m}", "pb4"], writes=[f"{pf}:t1"])
        p.op("vector", lambda e: e.tensor_add(outs[:, :], t1[:, :], p.pb[5][:, :]),
             reads=[f"{pf}:t1", "pb5"], writes=[f"{pf}:outs"])
        outfn(m, outs)


def em_F(p, t, C, ar, rw):
    ctx0v = t["ctx0_sh"].rearrange("h (p w) -> (h p) w", p=2)
    csb = ar.bf16(128, 16 * TC)
    wsb = ar.bf16(128, 16 * D)
    xc = ar.f32(128, 2 * 512)
    u1 = ar.f32(128, 16 * 512)
    sq = ar.bf16(128, 512)
    accx = ar.f32(128, 512)
    accsq = ar.f32(128, 512)
    t1 = ar.f32(128, 512)
    outs = ar.f32(128, 512)
    srow = rw.f32(3 * 512)
    for k in range(16):
        p.op("gpsimd", lambda e, k=k: e.indirect_dma_start(
            csb[:, k * TC:(k + 1) * TC], None,
            ctx0v[:, :], IndirectOffsetOnAxis(ap=C["idx_c"][:, k:k + 1], axis=0)),
            writes=[f"F:c{k}"], reads=["idx"], dma=True)
        p.op("sync", lambda e, k=k: e.dma_start(
            out=wsb[:, k * D:(k + 1) * D], in_=t["wo"][k * 128:(k + 1) * 128, :]),
            writes=[f"F:w{k}"], dma=True)
    def epiF(m):
        bank = m % 4
        xb = m % 2
        p.op("vector", lambda e, m=m, bank=bank, xb=xb: e.tensor_add(
            u1[:, m * 512:(m + 1) * 512], p.pb[bank][:, :],
            xc[:, xb * 512:(xb + 1) * 512]),
            reads=[f"pb{bank}", f"F:xc{xb}"], writes=[f"F:u1_{m}"])
        if m == 0:
            p.op("gpsimd", lambda e: e.tensor_copy(accx[:, :], u1[:, 0:512]),
                 reads=["F:u1_0"], writes=["F:accx"])
            p.op("scalar", lambda e: e.activation(
                accsq[:, :], u1[:, 0:512], AF.Square),
                reads=["F:u1_0"], writes=["F:accsq"])
        else:
            p.op("gpsimd", lambda e, m=m: e.tensor_add(
                accx[:, :], accx[:, :], u1[:, m * 512:(m + 1) * 512]),
                reads=[f"F:u1_{m}", "F:accx"], writes=["F:accx"])
            p.op("scalar", lambda e, m=m: e.activation(
                sq[:, :], u1[:, m * 512:(m + 1) * 512], AF.Square),
                reads=[f"F:u1_{m}"], writes=["F:sq"])
            p.op("gpsimd", lambda e: e.tensor_add(accsq[:, :], accsq[:, :], sq[:, :]),
                 reads=["F:sq", "F:accsq"], writes=["F:accsq"])
        if m == 15:
            p.op("tensor", lambda e: e.matmul(
                p.pb[6][0:1, :], C["oncf"][:, 0:1], accx[:, :],
                start=True, stop=True),
                reads=["F:accx", "oncf"], writes=["pb6"])
            p.op("tensor", lambda e: e.matmul(
                p.pb[7][0:1, :], C["oncf"][:, 0:1], accsq[:, :],
                start=True, stop=True),
                reads=["F:accsq", "oncf"], writes=["pb7"])

    for tt in range(TC // 512):
        t0 = tt * 512
        for m in range(16):
            bank = m % 4
            xb = m % 2
            p.op("sync", lambda e, m=m, t0=t0, xb=xb: e.dma_start(
                out=xc[:, xb * 512:(xb + 1) * 512],
                in_=t["x_f"][m * 128:(m + 1) * 128, t0:t0 + 512]),
                writes=[f"F:xc{xb}"], dma=True)
            for k in range(16):
                p.op("tensor", lambda e, k=k, m=m, t0=t0, bank=bank: e.matmul(
                    p.pb[bank][:, :],
                    wsb[:, k * D + m * 128: k * D + (m + 1) * 128],
                    csb[:, k * TC + t0: k * TC + t0 + 512],
                    start=(k == 0), stop=(k == 15)),
                    reads=[f"F:c{k}", f"F:w{k}"],
                    writes=[f"pb{bank}"] if k in (0, 15) else [])
            if m > 0:
                epiF(m - 1)
        epiF(15)

        def outfn(m, o, t0=t0):
            p.op("sync", lambda e, m=m, t0=t0, o=o: e.dma_start(
                out=t["x1"][m * 128:(m + 1) * 128, t0:t0 + 512], in_=o[:, :]),
                reads=["F:outs"], dma=True)
        _ln_apply(p, C, "F", u1, 16, D, outfn, srow, t1, outs)


def em_G(p, t, C, ar, rw, L, xin, xout, scat):
    pf = f"G{L}"
    TH = 512
    xb_sb = ar.bf16(128, 2 * 16 * TH)
    gw = ar.bf16(128, 2 * 2048)
    uw = ar.bf16(128, 2 * 2048)
    dw = ar.bf16(128, 2 * 5632)
    gssb = ar.f32(128, 44)
    ussb = ar.f32(128, 44)
    dssb = ar.f32(128, 16)
    g_sb = ar.f32(128, 2 * 512)
    u_sb = ar.f32(128, 2 * 512)
    hdn = ar.bf16(128, 2 * 44 * TH)
    xc = ar.f32(128, 2 * 512)
    u2 = ar.f32(128, 2 * 512)
    u2t = ar.f32(128, 2 * 512)
    sq = ar.bf16(128, 512)
    accx = ar.f32(128, 512)
    accsq = ar.f32(128, 512)
    rsb = ar.f32(128, 512)
    bsb2 = ar.f32(128, 512)
    outs = ar.f32(128, 512)
    outs_bf = ar.bf16(128, 512)
    srow = rw.f32(3 * 512)
    p.op("sync", lambda e: e.dma_start(out=gssb[:, :], in_=t[f"{L}_gs"]),
         writes=[f"{pf}:gs"], dma=True)
    p.op("sync", lambda e: e.dma_start(out=ussb[:, :], in_=t[f"{L}_us"]),
         writes=[f"{pf}:us"], dma=True)
    p.op("sync", lambda e: e.dma_start(out=dssb[:, :], in_=t[f"{L}_ds"]),
         writes=[f"{pf}:ds"], dma=True)

    def load_xb(q):
        qp = q % 2
        h0 = q * TH
        for k in range(16):
            p.op("gpsimd", lambda e, k=k, h0=h0, qp=qp: e.dma_start(
                out=xb_sb[:, qp * 16 * TH + k * TH: qp * 16 * TH + (k + 1) * TH],
                in_=xin[k * 128:(k + 1) * 128, h0:h0 + TH]),
                writes=[f"{pf}:xb{qp}_{k}"], dma=True)

    def load_gu(i):
        wsl = i % 2
        p.op("scalar", lambda e, i=i, wsl=wsl: e.dma_start(
            out=gw[:, wsl * 2048:(wsl + 1) * 2048], in_=t[f"{L}_gws"][i]),
            writes=[f"{pf}:gw{wsl}"], dma=True)
        p.op("gpsimd", lambda e, i=i, wsl=wsl: e.dma_start(
            out=uw[:, wsl * 2048:(wsl + 1) * 2048], in_=t[f"{L}_uws"][i]),
            writes=[f"{pf}:uw{wsl}"], dma=True)

    def load_dw(f):
        wsl = f % 2
        p.op("sync", lambda e, f=f, wsl=wsl: e.dma_start(
            out=dw[:, wsl * 5632:(wsl + 1) * 5632], in_=t[f"{L}_dws"][f]),
            writes=[f"{pf}:dw{wsl}"], dma=True)

    def gate_up(q, i):
        qp = q % 2
        wsl = i % 2
        bg = i % 2
        bu = 2 + i % 2
        if i + 1 < 44:
            load_gu(i + 1)
        for k in range(16):
            p.op("tensor", lambda e, k=k, wsl=wsl, qp=qp, bg=bg: e.matmul(
                p.pb[bg][:, :],
                gw[:, wsl * 2048 + k * 128: wsl * 2048 + (k + 1) * 128],
                xb_sb[:, qp * 16 * TH + k * TH: qp * 16 * TH + (k + 1) * TH],
                start=(k == 0), stop=(k == 15)),
                reads=[f"{pf}:gw{wsl}", f"{pf}:xb{qp}_{k}"],
                writes=[f"pb{bg}"] if k in (0, 15) else [])
        for k in range(16):
            p.op("tensor", lambda e, k=k, wsl=wsl, qp=qp, bu=bu: e.matmul(
                p.pb[bu][:, :],
                uw[:, wsl * 2048 + k * 128: wsl * 2048 + (k + 1) * 128],
                xb_sb[:, qp * 16 * TH + k * TH: qp * 16 * TH + (k + 1) * TH],
                start=(k == 0), stop=(k == 15)),
                reads=[f"{pf}:uw{wsl}", f"{pf}:xb{qp}_{k}"],
                writes=[f"pb{bu}"] if k in (0, 15) else [])
        p.op("scalar", lambda e, i=i, bg=bg: e.activation(
            g_sb[:, bg * 512:(bg + 1) * 512], p.pb[bg][:, :], AF.Silu,
            scale=gssb[:, i:i + 1]),
            reads=[f"pb{bg}", f"{pf}:gs"], writes=[f"{pf}:g{bg}"])
        p.op("vector", lambda e, i=i, bg=bg, bu=bu: e.tensor_scalar_mul(
            u_sb[:, bg * 512:(bg + 1) * 512], p.pb[bu][:, :],
            ussb[:, i:i + 1]),
            reads=[f"pb{bu}", f"{pf}:us"], writes=[f"{pf}:u{bg}"])
        p.op("vector", lambda e, i=i, qp=qp, bg=bg: e.tensor_mul(
            hdn[:, qp * 44 * TH + i * TH: qp * 44 * TH + (i + 1) * TH],
            g_sb[:, bg * 512:(bg + 1) * 512], u_sb[:, bg * 512:(bg + 1) * 512]),
            reads=[f"{pf}:g{bg}", f"{pf}:u{bg}"], writes=[f"{pf}:hdn{qp}_{i}"])

    def down_f(q, f):
        qp = q % 2
        h0 = q * TH
        wsl = f % 2
        bk = 4 + f % 2
        if f + 1 < 16:
            load_dw(f + 1)
        p.op("sync", lambda e, f=f, h0=h0, wsl=wsl: e.dma_start(
            out=xc[:, wsl * 512:(wsl + 1) * 512],
            in_=xin[f * 128:(f + 1) * 128, h0:h0 + TH]),
            writes=[f"{pf}:xc{wsl}"], dma=True)
        for ik in range(44):
            p.op("tensor", lambda e, ik=ik, wsl=wsl, qp=qp, bk=bk: e.matmul(
                p.pb[bk][:, :],
                dw[:, wsl * 5632 + ik * 128: wsl * 5632 + (ik + 1) * 128],
                hdn[:, qp * 44 * TH + ik * TH: qp * 44 * TH + (ik + 1) * TH],
                start=(ik == 0), stop=(ik == 43)),
                reads=[f"{pf}:dw{wsl}", f"{pf}:hdn{qp}_{ik}"],
                writes=[f"pb{bk}"] if ik in (0, 43) else [])

    def down_epi(q, f):
        h0 = q * TH
        wsl = f % 2
        bk = 4 + f % 2
        p.op("vector", lambda e, f=f, wsl=wsl, bk=bk: e.scalar_tensor_tensor(
            u2[:, wsl * 512:(wsl + 1) * 512], p.pb[bk][:, :],
            dssb[:, f:f + 1], xc[:, wsl * 512:(wsl + 1) * 512],
            ALU.mult, ALU.add),
            reads=[f"pb{bk}", f"{pf}:ds", f"{pf}:xc{wsl}"],
            writes=[f"{pf}:u2_{wsl}"])
        p.op("gpsimd", lambda e, f=f, h0=h0, wsl=wsl: e.dma_start(
            out=t["u2d"][f * 128:(f + 1) * 128, h0:h0 + TH],
            in_=u2[:, wsl * 512:(wsl + 1) * 512]),
            reads=[f"{pf}:u2_{wsl}"], writes=[f"{pf}:u2d_{q % 2}_{f}"], dma=True)
        if f == 0:
            p.op("gpsimd", lambda e, wsl=wsl: e.tensor_copy(
                accx[:, :], u2[:, wsl * 512:(wsl + 1) * 512]),
                reads=[f"{pf}:u2_{wsl}"], writes=[f"{pf}:accx"])
            p.op("scalar", lambda e, wsl=wsl: e.activation(
                accsq[:, :], u2[:, wsl * 512:(wsl + 1) * 512], AF.Square),
                reads=[f"{pf}:u2_{wsl}"], writes=[f"{pf}:accsq"])
        else:
            p.op("gpsimd", lambda e, wsl=wsl: e.tensor_add(
                accx[:, :], accx[:, :], u2[:, wsl * 512:(wsl + 1) * 512]),
                reads=[f"{pf}:u2_{wsl}", f"{pf}:accx"], writes=[f"{pf}:accx"])
            p.op("scalar", lambda e, wsl=wsl: e.activation(
                sq[:, :], u2[:, wsl * 512:(wsl + 1) * 512], AF.Square),
                reads=[f"{pf}:u2_{wsl}"], writes=[f"{pf}:sq"])
            p.op("gpsimd", lambda e: e.tensor_add(accsq[:, :], accsq[:, :], sq[:, :]),
                 reads=[f"{pf}:sq", f"{pf}:accsq"], writes=[f"{pf}:accsq"])
        if f == 15:
            p.op("tensor", lambda e: e.matmul(
                p.pb[6][0:1, :], C["oncf"][:, 0:1], accx[:, :],
                start=True, stop=True),
                reads=[f"{pf}:accx", "oncf"], writes=["pb6"])
            p.op("tensor", lambda e: e.matmul(
                p.pb[7][0:1, :], C["oncf"][:, 0:1], accsq[:, :],
                start=True, stop=True),
                reads=[f"{pf}:accsq", "oncf"], writes=["pb7"])

    def rows_part(q):
        A_ = srow[0:1, 0:512]
        Bm = srow[0:1, 512:1024]
        Cc = srow[0:1, 1024:1536]
        p.op("vector", lambda e, A_=A_: e.tensor_scalar(
            A_, p.pb[6][0:1, :], 1.0 / D, None, ALU.mult),
            reads=["pb6"], writes=[f"{pf}:lnA"])
        p.op("vector", lambda e, A_=A_, Bm=Bm: e.tensor_mul(Bm, A_, A_),
             reads=[f"{pf}:lnA"], writes=[f"{pf}:lnB"])
        p.op("vector", lambda e, Cc=Cc: e.tensor_scalar(
            Cc, p.pb[7][0:1, :], 1.0 / D, EPS, ALU.mult, ALU.add),
            reads=["pb7"], writes=[f"{pf}:lnC"])
        p.op("vector", lambda e, Cc=Cc, Bm=Bm: e.tensor_sub(Cc, Cc, Bm),
             reads=[f"{pf}:lnB", f"{pf}:lnC"], writes=[f"{pf}:lnC"])
        p.op("scalar", lambda e, Cc=Cc: e.activation(Cc, Cc, AF.Ln),
             reads=[f"{pf}:lnC"], writes=[f"{pf}:lnC"])
        p.op("scalar", lambda e, Cc=Cc: e.activation(Cc, Cc, AF.Exp, scale=-0.5),
             reads=[f"{pf}:lnC"], writes=[f"{pf}:lnC"])
        p.op("vector", lambda e, A_=A_, Bm=Bm, Cc=Cc: e.tensor_mul(Bm, A_, Cc),
             reads=[f"{pf}:lnA", f"{pf}:lnC"], writes=[f"{pf}:lnB"])
        p.op("vector", lambda e, Bm=Bm: e.tensor_scalar_mul(Bm, Bm, -1.0),
             reads=[f"{pf}:lnB"], writes=[f"{pf}:lnB"])
        p.op("tensor", lambda e, Cc=Cc: e.matmul(
            p.pb[6][:, :], C["onr"][0:1, 0:128], Cc, start=True, stop=True),
            reads=[f"{pf}:lnC", "onr"], writes=["pb6"])
        p.op("tensor", lambda e, Bm=Bm: e.matmul(
            p.pb[7][:, :], C["onr"][0:1, 0:128], Bm, start=True, stop=True),
            reads=[f"{pf}:lnB", "onr"], writes=["pb7"])
        p.op("vector", lambda e: e.tensor_copy(rsb[:, :], p.pb[6][:, :]),
             reads=["pb6"], writes=[f"{pf}:rsb"])
        p.op("vector", lambda e: e.tensor_copy(bsb2[:, :], p.pb[7][:, :]),
             reads=["pb7"], writes=[f"{pf}:bsb2"])

    def norm_iter(q, f):
        h0 = q * TH
        wsl = f % 2
        p.op("sync", lambda e, f=f, h0=h0, wsl=wsl: e.dma_start(
            out=u2t[:, wsl * 512:(wsl + 1) * 512],
            in_=t["u2d"][f * 128:(f + 1) * 128, h0:h0 + TH]),
            reads=[f"{pf}:u2d_{q % 2}_{f}"], writes=[f"{pf}:u2t{wsl}"], dma=True)
        p.op("vector", lambda e, wsl=wsl: e.tensor_mul(
            outs[:, :], u2t[:, wsl * 512:(wsl + 1) * 512], rsb[:, :]),
            reads=[f"{pf}:u2t{wsl}", f"{pf}:rsb"], writes=[f"{pf}:outs"])
        p.op("vector", lambda e: e.tensor_add(outs[:, :], outs[:, :], bsb2[:, :]),
             reads=[f"{pf}:outs", f"{pf}:bsb2"], writes=[f"{pf}:outs"])
        p.op("sync", lambda e, f=f, h0=h0: e.dma_start(
            out=xout[f * 128:(f + 1) * 128, h0:h0 + TH],
            in_=outs[:, :]),
            reads=[f"{pf}:outs"], dma=True)
        if scat:
            p.op("vector", lambda e: e.tensor_copy(outs_bf[:, :], outs[:, :]),
                 reads=[f"{pf}:outs"], writes=[f"{pf}:obf"])
            p.op("gpsimd", lambda e, f=f, h0=h0: e.indirect_dma_start(
                t["x2_sh"][:, :],
                IndirectOffsetOnAxis(ap=C["idx_t"][:, f:f + 1], axis=0),
                outs_bf[:, :], None, element_offset=h0),
                reads=[f"{pf}:obf", "idx"], dma=True)

    NQ = TC // TH
    load_xb(0)
    load_gu(0)
    for q in range(NQ):
        if q + 1 < NQ:
            load_xb(q + 1)
        if q > 0:
            load_dw(0)
        nxt_down = 0
        nxt_norm = 0
        for i in range(44):
            gate_up(q, i)
            if q > 0:
                want = (i + 1) * 16 // 44
                while nxt_down < want:
                    down_f(q - 1, nxt_down)
                    if nxt_down > 0:
                        down_epi(q - 1, nxt_down - 1)
                    nxt_down += 1
            if q > 1:
                while nxt_norm < (i + 1) * 16 // 44:
                    norm_iter(q - 2, nxt_norm)
                    nxt_norm += 1
        if q > 0:
            while nxt_down < 16:
                down_f(q - 1, nxt_down)
                if nxt_down > 0:
                    down_epi(q - 1, nxt_down - 1)
                nxt_down += 1
            down_epi(q - 1, 15)
            rows_part(q - 1)
        if q > 1:
            while nxt_norm < 16:
                norm_iter(q - 2, nxt_norm)
                nxt_norm += 1
        if q + 1 < NQ:
            load_gu(0)
    load_dw(0)
    for f in range(16):
        down_f(NQ - 1, f)
        if f > 0:
            down_epi(NQ - 1, f - 1)
        norm_iter(NQ - 2, f)
    down_epi(NQ - 1, 15)
    rows_part(NQ - 1)
    for f in range(16):
        norm_iter(NQ - 1, f)


def em_H(p, t, C, ar, rw):
    xsb = ar.bf16(128, 16 * S)
    wsb = ar.bf16(128, 16 * 1024)
    outsb = ar.bf16(128, 4 * 512)
    bpp = ar.f32(128, 16)
    bsb = rw.f32(3 * 1024)
    on5 = rw.f32(512)
    p.op("sync", lambda e: e.dma_start(out=on5[0:1, :], in_=t["ones512"]),
         writes=["H:on5"], dma=True)
    for k in range(16):
        for pp in range(2):
            p.op("sync", lambda e, k=k, pp=pp: e.dma_start(
                out=xsb[:, k * S + pp * 2048: k * S + pp * 2048 + 2048],
                in_=t["x2_sh"][pp * 2048 + k * 128: pp * 2048 + (k + 1) * 128, :]),
                writes=[f"H:x{k}"], dma=True)
    for nm, boff in (("bq1", 0), ("bk1", 1024), ("bv1", 2048)):
        p.op("sync", lambda e, nm=nm, boff=boff: e.dma_start(
            out=bsb[0:1, boff:boff + 1024], in_=t[nm]),
            writes=[f"H:{nm}"], dma=True)
    p.op("sync", lambda e: e.dma_start(out=bpp[:, 0:8], in_=t["bqpp"]),
         writes=["H:bpp"], dma=True)
    p.op("sync", lambda e: e.dma_start(out=bpp[:, 8:16], in_=t["bkpp"]),
         writes=["H:bpp"], dma=True)
    for wnm, bo, outd in (("wq1", 0, "q1"), ("wk1", 8, "k1")):
        for k in range(16):
            p.op("sync", lambda e, k=k, wnm=wnm: e.dma_start(
                out=wsb[:, k * 1024:(k + 1) * 1024],
                in_=t[wnm][k * 128:(k + 1) * 128, :]),
                writes=[f"H:w{k}"], dma=True)
        for tt in range(S // 512):
            t0 = tt * 512
            for m in range(8):
                bank = m % 4
                for k in range(16):
                    p.op("tensor", lambda e, k=k, m=m, t0=t0, bank=bank: e.matmul(
                        p.pb[bank][:, :],
                        wsb[:, k * 1024 + m * 128: k * 1024 + (m + 1) * 128],
                        xsb[:, k * S + t0: k * S + t0 + 512],
                        start=(k == 0), stop=(k == 15)),
                        reads=[f"H:x{k}", f"H:w{k}"],
                        writes=[f"pb{bank}"] if k in (0, 15) else [])
                p.op("vector", lambda e, m=m, bank=bank, bo=bo: e.tensor_scalar(
                    outsb[:, bank * 512:(bank + 1) * 512], p.pb[bank][:, :],
                    bpp[:, bo + m:bo + m + 1], None, ALU.add),
                    reads=[f"pb{bank}", "H:bpp"], writes=[f"H:o{bank}"])
                p.op("sync", lambda e, m=m, t0=t0, bank=bank, outd=outd: e.dma_start(
                    out=t[outd][m * 128:(m + 1) * 128, t0:t0 + 512],
                    in_=outsb[:, bank * 512:(bank + 1) * 512]),
                    reads=[f"H:o{bank}"], dma=True)
    for wnm, brow, outd in (("wv1", 2048, "v1t"), ("wq1", 0, "q1t")):
        for k in range(16):
            p.op("sync", lambda e, k=k, wnm=wnm: e.dma_start(
                out=wsb[:, k * 1024:(k + 1) * 1024],
                in_=t[wnm][k * 128:(k + 1) * 128, :]),
                writes=[f"H:w{k}"], dma=True)
        for ttile in range(S // 128):
            for v4 in range(2):
                bank = 4 + (2 * ttile + v4) % 4
                p.op("tensor", lambda e, v4=v4, bank=bank, brow=brow: e.matmul(
                    p.pb[bank][:, :], C["onr"][0:1, 0:128],
                    bsb[0:1, brow + v4 * 512: brow + (v4 + 1) * 512],
                    start=True, stop=False),
                    reads=["H:bv1", "H:bq1", "onr"], writes=[f"pb{bank}"])
                for k in range(16):
                    p.op("tensor", lambda e, k=k, ttile=ttile, v4=v4, bank=bank: e.matmul(
                        p.pb[bank][:, :],
                        xsb[:, k * S + ttile * 128: k * S + (ttile + 1) * 128],
                        wsb[:, k * 1024 + v4 * 512: k * 1024 + (v4 + 1) * 512],
                        start=False, stop=(k == 15)),
                        reads=[f"H:x{k}", f"H:w{k}"],
                        writes=[f"pb{bank}"] if k == 15 else [])
                p.op("vector", lambda e, v4=v4, bank=bank: e.tensor_copy(
                    outsb[:, v4 * 512:(v4 + 1) * 512], p.pb[bank][:, :]),
                    reads=[f"pb{bank}"], writes=[f"H:o{v4}"])
                p.op("sync", lambda e, ttile=ttile, v4=v4, outd=outd: e.dma_start(
                    out=t[outd][ttile * 128:(ttile + 1) * 128, v4 * 512:(v4 + 1) * 512],
                    in_=outsb[:, v4 * 512:(v4 + 1) * 512]),
                    reads=[f"H:o{v4}"], dma=True)


def em_I(p, t, C, ar, rw):
    q_sb = ar.bf16(128, 2 * S)
    k_sb = ar.bf16(128, 2 * S)
    vt_sb = ar.bf16(128, 2 * S)
    qt_all = ar.bf16(128, 2 * S)
    msk = ar.f32(128, 384)
    iorow = ar.f32(128, 64)
    iopp = ar.f32(64, 1)
    ident = ar.bf16(128, 128)
    idf8 = ar.f32(8, 8)
    sq_sb = ar.bf16(128, 2 * 512)
    negc = ar.f32(128, 1)
    na = ar.f32(8, S)
    nb = ar.f32(8, S)
    mx8 = ar.f32(8, 8)
    rkall = ar.f32(128, 256)
    probs = ar.bf16(128, 384)
    lctx = ar.bf16(128, S)
    P_sb = ar.bf16(128, 128)
    qgT_sb = ar.bf16(64, 128)
    qg_sb = ar.bf16(128, 64)
    gp_sb = ar.bf16(128, 64)
    rg_pp = ar.f32(64, 1)
    gctx_sb = ar.bf16(64, 128)
    cb_sb = ar.f32(128, 2 * 512)
    pi_sb = ar.bf16(64, 2 * 512)
    t2_sb = ar.f32(128, 2 * 512)
    ctxo = ar.bf16(128, 2 * 512)
    nrow = ar.f32(1, S)
    rloc = ar.f32(1, S)
    mra = ar.f32(1, S)
    srow = rw.f32(64)
    rgrow = rw.f32(64)
    rr_sb = rw.f32(512)
    mr_sb = rw.f32(512)
    cbrow = rw.f32(1024)
    pbb6 = p.pb[6].bitcast(BF16)
    for (tl, dsrc, key) in [(msk, "maskL", "I:msk"), (iorow, "io_row", "I:iorow"),
                            (iopp, "io_pp", "I:iopp"), (ident, "identd", "I:ident"),
                            (idf8, "ident8f", "I:id8")]:
        p.op("sync", lambda e, tl=tl, dsrc=dsrc: e.dma_start(out=tl[:, :], in_=t[dsrc]),
             writes=[key], dma=True)
    for j in range(8):
        p.op("sync", lambda e, j=j: e.dma_start(
            out=q_sb[:, 0:S], in_=t["q1"][j * 128:(j + 1) * 128, :]),
            writes=["I:q0"], dma=True)
        for c8 in range(8):
            sb0 = (c8 % 2) * 512
            bk = c8 % 2
            p.op("scalar", lambda e, c8=c8, sb0=sb0: e.activation(
                sq_sb[:, sb0:sb0 + 512], q_sb[:, c8 * 512:(c8 + 1) * 512], AF.Square),
                reads=["I:q0"], writes=[f"I:sq{c8 % 2}"])
            p.op("tensor", lambda e, sb0=sb0, bk=bk: e.matmul(
                p.pb[bk][0:1, :], C["onc"][:, 0:1], sq_sb[:, sb0:sb0 + 512],
                start=True, stop=True),
                reads=[f"I:sq{c8 % 2}", "onc"], writes=[f"pb{bk}"])
            p.op("vector", lambda e, c8=c8, bk=bk: e.tensor_copy(
                nrow[0:1, c8 * 512:(c8 + 1) * 512], p.pb[bk][0:1, :]),
                reads=[f"pb{bk}"], writes=["I:nrow"])
        p.op("sync", lambda e, j=j: e.dma_start(
            out=t["rankm"][j:j + 1, :], in_=nrow[0:1, :]),
            reads=["I:nrow"], writes=["I:na_d"], dma=True)
    p.op("sync", lambda e: e.dma_start(out=na[:, :], in_=t["rankm"]),
         reads=["I:na_d"], writes=["I:na"], dma=True)
    bufs = [na, nb]
    for rnd in range(8):
        cur, nxt = bufs[rnd % 2], bufs[(rnd + 1) % 2]
        ck, nk = ("I:na" if rnd % 2 == 0 else "I:nb"), ("I:nb" if rnd % 2 == 0 else "I:na")
        p.op("vector", lambda e, cur=cur: e.max(out=mx8[:, :], in_=cur[:, :]),
             reads=[ck], writes=["I:mx8"])
        if rnd == 7:
            p.op("vector", lambda e: e.memset(mx8[:, 6:8], -1.0),
                 reads=[], writes=["I:mx8"])
        p.op("vector", lambda e, cur=cur, nxt=nxt: e.match_replace(
            out=nxt[:, :], in_to_replace=mx8[:, :], in_values=cur[:, :],
            imm_value=-1.0),
            reads=[ck, "I:mx8"], writes=[nk])
    p.op("vector", lambda e: e.tensor_scalar(
        nb[:, :], na[:, :], -0.5, None, ALU.is_le),
        reads=["I:na"], writes=["I:nb"])
    p.op("vector", lambda e: e.memset(nb[:, 0:1], 1.0), reads=[], writes=["I:nb"])
    p.op("vector", lambda e: e.memset(nb[:, S - 1:S], 1.0), reads=[], writes=["I:nb"])
    p.op("vector", lambda e: e.tensor_tensor_scan(
        na[:, :], nb[:, :], nb[:, :], 0.0, ALU.add, ALU.bypass),
        reads=["I:nb"], writes=["I:na"])
    p.op("vector", lambda e: e.tensor_sub(na[:, :], na[:, :], nb[:, :]),
         reads=["I:na", "I:nb"], writes=["I:na"])
    p.op("sync", lambda e: e.dma_start(out=t["maskm"], in_=nb[:, :]),
         reads=["I:nb"], writes=["I:maskmD"], dma=True)
    p.op("vector", lambda e: e.tensor_scalar(
        nb[:, :], nb[:, :], -1e9, 1e9, ALU.mult, ALU.add),
        reads=["I:nb"], writes=["I:nb"])
    p.op("vector", lambda e: e.tensor_add(na[:, :], na[:, :], nb[:, :]),
         reads=["I:na", "I:nb"], writes=["I:na"])
    p.op("sync", lambda e: e.dma_start(out=t["rankm"], in_=na[:, :]),
         reads=["I:na"], writes=["I:rankmD"], dma=True)
    for c in range(32):
        p.op("tensor", lambda e, c=c: e.transpose(
            p.pb[6][:, 0:8], na[0:8, c * 128:(c + 1) * 128], idf8[0:8, 0:8]),
            reads=["I:na", "I:id8"], writes=["pb6"])
        p.op("vector", lambda e, c=c: e.tensor_copy(
            rkall[:, c * 8:(c + 1) * 8], p.pb[6][:, 0:8]),
            reads=["pb6"], writes=["I:rkall"])
    NB = 32

    def loads(j):
        pr = j % 2
        o = pr * S
        p.op("sync", lambda e, j=j, o=o: e.dma_start(
            out=q_sb[:, o:o + S], in_=t["q1"][j * 128:(j + 1) * 128, :]),
            writes=[f"I:q{pr}"], dma=True)
        p.op("sync", lambda e, j=j, o=o: e.dma_start(
            out=k_sb[:, o:o + S], in_=t["k1"][j * 128:(j + 1) * 128, :]),
            writes=[f"I:k{pr}"], dma=True)
        for g in range(8):
            p.op("sync", lambda e, j=j, g=g, o=o: e.dma_start(
                out=vt_sb[:, o + g * 512: o + (g + 1) * 512].rearrange(
                    "p (a d) -> p a d", d=128),
                in_=t["v1t"][g * 512:(g + 1) * 512, j * 128:(j + 1) * 128].rearrange(
                    "(a p) d -> p a d", p=128)),
                writes=[f"I:v{pr}"], dma=True)
            p.op("sync", lambda e, j=j, g=g, o=o: e.dma_start(
                out=qt_all[:, o + g * 512: o + (g + 1) * 512].rearrange(
                    "p (a d) -> p a d", d=128),
                in_=t["q1t"][g * 512:(g + 1) * 512, j * 128:(j + 1) * 128].rearrange(
                    "(a p) d -> p a d", p=128)),
                writes=[f"I:qt{pr}"], dma=True)

    loads(0)
    for j in range(8):
        pr = j % 2
        o = pr * S
        if j < 7:
            loads(j + 1)
        _bound_c(p, f"I:q{pr}", q_sb[:, o:o + S], sq_sb, C["onc"], srow, 0, "I")
        _bound_c(p, f"I:k{pr}", k_sb[:, o:o + S], sq_sb, C["onc"], srow, 1, "I")
        _negc_from_bounds(p, C, srow, negc[:, 0:1], "I")

        def qb0(tb):
            return 0 if tb % 2 == 0 else 7

        def epi_local(tb):
            bqk = qb0(tb)
            qlo = max(0, (tb - 1) * 128)
            qhi = min(S, (tb + 2) * 128)
            Nq = qhi - qlo
            mlo = 128 if tb == 0 else 0
            del qhi
            p.op("vector", lambda e, Nq=Nq, mlo=mlo, bqk=bqk: e.tensor_add(
                p.pb[bqk][:, 0:Nq], p.pb[bqk][:, 0:Nq], msk[:, mlo:mlo + Nq]),
                reads=[f"pb{bqk}", "I:msk"], writes=[f"pb{bqk}"])
            p.op("scalar", lambda e, Nq=Nq, bqk=bqk: e.activation(
                probs[:, 0:Nq], p.pb[bqk][:, 0:Nq], AF.Exp,
                bias=negc[:, 0:1], scale=SCALE),
                reads=[f"pb{bqk}", "I:negc0"], writes=["I:probs"])
            for cg in (-1, 0, 1):
                qb = tb + cg
                if qb < 0 or qb >= NB:
                    continue
                col = qb * 128 - qlo
                first = (tb == max(0, qb - 1))
                last = (tb == min(NB - 1, qb + 1))
                cbank = 2 if qb % 3 == 0 else (3 if qb % 3 == 1 else 6)
                dbank = 1 if qb % 3 == 0 else (4 if qb % 3 == 1 else 5)
                p.op("tensor", lambda e, col=col, dbank=dbank, first=first,
                     last=last: e.matmul(
                    p.pb[dbank][0:1, 0:128], C["onc"][:, 0:1], probs[:, col:col + 128],
                    start=first, stop=last),
                    reads=["I:probs", "onc"],
                    writes=[f"pb{dbank}"] if (first or last) else [])
                p.op("tensor", lambda e, tb=tb, col=col, cbank=cbank, first=first,
                     last=last, o=o: e.matmul(
                    p.pb[cbank][:, 0:128], vt_sb[:, o + tb * 128: o + (tb + 1) * 128],
                    probs[:, col:col + 128], start=first, stop=last),
                    reads=["I:probs", f"I:v{pr}"],
                    writes=[f"pb{cbank}"] if (first or last) else [])
            done = [tb - 1] if tb < NB - 1 else [tb - 1, tb]
            for qb in done:
                if qb < 0:
                    continue
                cbank = 2 if qb % 3 == 0 else (3 if qb % 3 == 1 else 6)
                dbank = 1 if qb % 3 == 0 else (4 if qb % 3 == 1 else 5)
                p.op("vector", lambda e, qb=qb, dbank=dbank: e.reciprocal(
                    rloc[0:1, qb * 128:(qb + 1) * 128], p.pb[dbank][0:1, 0:128]),
                    reads=[f"pb{dbank}"], writes=["I:rloc"])
                p.op("vector", lambda e, qb=qb, cbank=cbank: e.tensor_copy(
                    lctx[:, qb * 128:(qb + 1) * 128], p.pb[cbank][:, 0:128]),
                    reads=[f"pb{cbank}"], writes=["I:lctx"])

        for tb in range(NB):
            qlo = max(0, (tb - 1) * 128)
            qhi = min(S, (tb + 2) * 128)
            Nq = qhi - qlo
            bqk = qb0(tb)
            p.op("tensor", lambda e, tb=tb, qlo=qlo, Nq=Nq, bqk=bqk, o=o: e.matmul(
                p.pb[bqk][:, 0:Nq], k_sb[:, o + tb * 128: o + (tb + 1) * 128],
                q_sb[:, o + qlo: o + qlo + Nq], start=True, stop=True),
                reads=[f"I:k{pr}", f"I:q{pr}"], writes=[f"pb{bqk}"])
            if tb > 0:
                epi_local(tb - 1)
        epi_local(NB - 1)
        for c in range(32):
            pp = c % 2
            p.op("vector", lambda e, c=c, pp=pp, j=j: e.tensor_scalar(
                P_sb[:, pp * 64:(pp + 1) * 64], iorow[:, :],
                rkall[:, c * 8 + j:c * 8 + j + 1], None, ALU.is_equal),
                reads=["I:iorow", "I:rkall"], writes=[f"I:P{pp}"])
            p.op("tensor", lambda e, c=c, pp=pp, o=o: e.matmul(
                p.pb[0][0:64, 0:128], P_sb[:, pp * 64:(pp + 1) * 64],
                qt_all[:, o + c * 128: o + (c + 1) * 128],
                start=(c == 0), stop=(c == 31)),
                reads=[f"I:P{pp}", f"I:qt{pr}"],
                writes=["pb0"] if c in (0, 31) else [])
        p.op("vector", lambda e: e.tensor_copy(qgT_sb[:, :], p.pb[0][0:64, 0:128]),
             reads=["pb0"], writes=["I:qgT"])
        p.op("tensor", lambda e: e.transpose(
            pbb6[:, 0:64], qgT_sb[:, :], ident[0:64, 0:64]),
            reads=["I:qgT", "I:ident"], writes=["pb6"])
        p.op("vector", lambda e: e.tensor_copy(qg_sb[:, :], pbb6[:, 0:64]),
             reads=["pb6"], writes=["I:qg"])
        for kb in range(32):
            p.op("tensor", lambda e, kb=kb, o=o: e.matmul(
                p.pb[1][:, 0:64], k_sb[:, o + kb * 128: o + (kb + 1) * 128], qg_sb[:, :],
                start=True, stop=True),
                reads=[f"I:k{pr}", "I:qg"], writes=["pb1"])
            p.op("scalar", lambda e: e.activation(
                gp_sb[:, :], p.pb[1][:, 0:64], AF.Exp,
                bias=negc[:, 0:1], scale=SCALE),
                reads=["pb1", "I:negc0"], writes=["I:gp"])
            p.op("tensor", lambda e, kb=kb: e.matmul(
                p.pb[2][0:1, 0:64], C["onc"][:, 0:1], gp_sb[:, :],
                start=(kb == 0), stop=(kb == 31)),
                reads=["I:gp", "onc"],
                writes=["pb2"] if kb in (0, 31) else [])
            p.op("tensor", lambda e, kb=kb, o=o: e.matmul(
                p.pb[3][0:64, 0:128], gp_sb[:, :], vt_sb[:, o + kb * 128: o + (kb + 1) * 128],
                start=(kb == 0), stop=(kb == 31)),
                reads=["I:gp", f"I:v{pr}"],
                writes=["pb3"] if kb in (0, 31) else [])
        p.op("vector", lambda e: e.reciprocal(rgrow[0:1, 0:64], p.pb[2][0:1, 0:64]),
             reads=["pb2"], writes=["I:rgrow"])
        p.op("tensor", lambda e: e.matmul(
            p.pb[5][0:64, 0:1], rgrow[0:1, 0:64], C["onr"][0:1, 0:1],
            start=True, stop=True),
            reads=["I:rgrow", "onr"], writes=["pb5"])
        p.op("vector", lambda e: e.tensor_copy(rg_pp[:, 0:1], p.pb[5][0:64, 0:1]),
             reads=["pb5"], writes=["I:rg"])
        p.op("vector", lambda e: e.tensor_scalar_mul(
            gctx_sb[:, :], p.pb[3][0:64, 0:128], rg_pp[:, 0:1]),
            reads=["pb3", "I:rg"], writes=["I:gctx"])
        p.op("sync", lambda e, j=j: e.dma_start(
            out=nrow[0:1, :], in_=t["rankm"][j:j + 1, :]),
            reads=["I:rankmD"], writes=["I:nrow"], dma=True)
        p.op("sync", lambda e, j=j: e.dma_start(
            out=mra[0:1, :], in_=t["maskm"][j:j + 1, :]),
            reads=["I:maskmD"], writes=["I:mra"], dma=True)
        for c8 in range(8):
            cp = c8 % 2
            co = cp * 512
            bb5 = 5 if cp == 0 else 2
            bb7 = 7 if cp == 0 else 1
            bb4 = 4 if cp == 0 else 3
            p.op("vector", lambda e, c8=c8, co=co, cp=cp: e.tensor_scalar(
                cbrow[0:1, co:co + 512], mra[0:1, c8 * 512:(c8 + 1) * 512], -1.0, 1.0,
                ALU.mult, ALU.add),
                reads=["I:mra"], writes=[f"I:cbrow{cp}"])
            p.op("vector", lambda e, c8=c8, co=co, cp=cp: e.tensor_mul(
                cbrow[0:1, co:co + 512], cbrow[0:1, co:co + 512],
                rloc[0:1, c8 * 512:(c8 + 1) * 512]),
                reads=[f"I:cbrow{cp}", "I:rloc"], writes=[f"I:cbrow{cp}"])
            p.op("tensor", lambda e, co=co, cp=cp, bb5=bb5: e.matmul(
                p.pb[bb5][:, :], C["onr"][0:1, 0:128], cbrow[0:1, co:co + 512],
                start=True, stop=True),
                reads=[f"I:cbrow{cp}", "onr"], writes=[f"pb{bb5}"])
            p.op("vector", lambda e, co=co, cp=cp, bb5=bb5: e.tensor_copy(
                cb_sb[:, co:co + 512], p.pb[bb5][:, :]),
                reads=[f"pb{bb5}"], writes=[f"I:cb{cp}"])
            p.op("tensor", lambda e, c8=c8, bb7=bb7: e.matmul(
                p.pb[bb7][0:64, :], C["onr"][0:1, 0:64],
                nrow[0:1, c8 * 512:(c8 + 1) * 512],
                start=True, stop=True),
                reads=["I:nrow", "onr"], writes=[f"pb{bb7}"])
            p.op("vector", lambda e, co=co, cp=cp, bb7=bb7: e.tensor_scalar(
                pi_sb[:, co:co + 512], p.pb[bb7][0:64, :], iopp[:, 0:1], None,
                ALU.is_equal),
                reads=[f"pb{bb7}", "I:iopp"], writes=[f"I:pi{cp}"])
            p.op("tensor", lambda e, co=co, cp=cp, bb4=bb4: e.matmul(
                p.pb[bb4][:, :], gctx_sb[:, :], pi_sb[:, co:co + 512],
                start=True, stop=True),
                reads=["I:gctx", f"I:pi{cp}"], writes=[f"pb{bb4}"])
            p.op("vector", lambda e, c8=c8, co=co, cp=cp: e.tensor_mul(
                t2_sb[:, co:co + 512], lctx[:, c8 * 512:(c8 + 1) * 512],
                cb_sb[:, co:co + 512]),
                reads=["I:lctx", f"I:cb{cp}"], writes=[f"I:t2{cp}"])
            p.op("vector", lambda e, co=co, cp=cp, bb4=bb4: e.tensor_add(
                ctxo[:, co:co + 512], t2_sb[:, co:co + 512], p.pb[bb4][:, :]),
                reads=[f"I:t2{cp}", f"pb{bb4}"], writes=[f"I:ctxo{cp}"])
            p.op("gpsimd", lambda e, j=j, c8=c8, co=co, cp=cp: e.indirect_dma_start(
                t["ctx1_sh"][:, :],
                IndirectOffsetOnAxis(ap=C["idx_h"][:, j:j + 1], axis=0),
                ctxo[:, co:co + 512], None, element_offset=c8 * 512),
                reads=[f"I:ctxo{cp}", "idx"], dma=True)


def em_J(p, t, C, ar, rw):
    ctx1v = t["ctx1_sh"].rearrange("h (p w) -> (h p) w", p=2)
    a1sb = ar.bf16(128, 16 * TC)
    xc = ar.f32(128, 2 * 512)
    u1 = ar.f32(128, 16 * 512)
    sq = ar.bf16(128, 512)
    accx = ar.f32(128, 512)
    accsq = ar.f32(128, 512)
    t1 = ar.f32(128, 512)
    outs = ar.f32(128, 512)
    srow = rw.f32(3 * 512)
    for k in range(16):
        p.op("gpsimd", lambda e, k=k: e.indirect_dma_start(
            a1sb[:, k * TC:(k + 1) * TC], None,
            ctx1v[:, :], IndirectOffsetOnAxis(ap=C["idx_c"][:, k:k + 1], axis=0)),
            writes=[f"J:a{k}"], reads=["idx"], dma=True)
    def epiJ(m):
        if m == 0:
            p.op("gpsimd", lambda e: e.tensor_copy(accx[:, :], u1[:, 0:512]),
                 reads=["J:u1_0"], writes=["J:accx"])
            p.op("scalar", lambda e: e.activation(
                accsq[:, :], u1[:, 0:512], AF.Square),
                reads=["J:u1_0"], writes=["J:accsq"])
        else:
            p.op("gpsimd", lambda e, m=m: e.tensor_add(
                accx[:, :], accx[:, :], u1[:, m * 512:(m + 1) * 512]),
                reads=[f"J:u1_{m}", "J:accx"], writes=["J:accx"])
            p.op("scalar", lambda e, m=m: e.activation(
                sq[:, :], u1[:, m * 512:(m + 1) * 512], AF.Square),
                reads=[f"J:u1_{m}"], writes=["J:sq"])
            p.op("gpsimd", lambda e: e.tensor_add(accsq[:, :], accsq[:, :], sq[:, :]),
                 reads=["J:sq", "J:accsq"], writes=["J:accsq"])
        if m == 15:
            p.op("tensor", lambda e: e.matmul(
                p.pb[6][0:1, :], C["oncf"][:, 0:1], accx[:, :],
                start=True, stop=True),
                reads=["J:accx", "oncf"], writes=["pb6"])
            p.op("tensor", lambda e: e.matmul(
                p.pb[7][0:1, :], C["oncf"][:, 0:1], accsq[:, :],
                start=True, stop=True),
                reads=["J:accsq", "oncf"], writes=["pb7"])

    for tt in range(TC // 512):
        t0 = tt * 512
        for m in range(16):
            xb = m % 2
            p.op("sync", lambda e, m=m, t0=t0, xb=xb: e.dma_start(
                out=xc[:, xb * 512:(xb + 1) * 512],
                in_=t["x2"][m * 128:(m + 1) * 128, t0:t0 + 512]),
                writes=[f"J:xc{xb}"], dma=True)
            p.op("vector", lambda e, m=m, t0=t0: e.tensor_copy(
                u1[:, m * 512:(m + 1) * 512],
                a1sb[:, m * TC + t0: m * TC + t0 + 512]),
                reads=[f"J:a{m}"], writes=[f"J:u1_{m}"])
            p.op("vector", lambda e, m=m, xb=xb: e.tensor_add(
                u1[:, m * 512:(m + 1) * 512],
                u1[:, m * 512:(m + 1) * 512],
                xc[:, xb * 512:(xb + 1) * 512]),
                reads=[f"J:u1_{m}", f"J:xc{xb}"], writes=[f"J:u1_{m}"])
            if m > 0:
                epiJ(m - 1)
        epiJ(15)

        def outfn(m, o, t0=t0):
            p.op("sync", lambda e, m=m, t0=t0, o=o: e.dma_start(
                out=t["x3"][m * 128:(m + 1) * 128, t0:t0 + 512], in_=o[:, :]),
                reads=["J:outs"], dma=True)
        _ln_apply(p, C, "J", u1, 16, D, outfn, srow, t1, outs)


PHASES_ALL = ("AB", "C", "D", "E", "F", "G0", "H", "I", "J", "G1")


def build_fused(upto="G1", dbg=False):
    nc = bass.Bass("TRN2", target_bir_lowering=False)
    phases = PHASES_ALL[:PHASES_ALL.index(upto) + 1]
    t = {}

    def ein(name, shape, dt=F32):
        t[name] = nc.dram_tensor(name, shape, dt, kind="ExternalInput").ap()

    ein("x_f", [D, TC]); ein("wq_down", [D, QP], BF16); ein("gq", [128, 8])
    ein("wkv_lora", [D, KVPP], BF16); ein("wkv_rope", [D, 64], BF16); ein("gkv", [128, 11])
    ein("wq_up", [QP, 1024], BF16); ein("wkv_up_k", [KVPP, 512], BF16)
    ein("wkv_up_v", [KVPP, 1024], BF16)
    ein("wo", [D, D], BF16)
    ein("cs_own", [64, TC]); ein("sc_own", [64, TC])
    ein("cos4", [128, S]); ein("sin4", [128, S])
    for L in ("L0", "L1"):
        ein(f"{L}_gws", [44, 128, 2048], BF16); ein(f"{L}_uws", [44, 128, 2048], BF16)
        ein(f"{L}_dws", [16, 128, 5632], BF16)
        ein(f"{L}_gs", [128, 44]); ein(f"{L}_us", [128, 44]); ein(f"{L}_ds", [128, 16])
    ein("wq1", [D, 1024], BF16); ein("wk1", [D, 1024], BF16)
    ein("wv1", [D, 1024], BF16)
    ein("bq1", [1, 1024]); ein("bk1", [1, 1024]); ein("bv1", [1, 1024])
    ein("bqpp", [128, 8]); ein("bkpp", [128, 8])
    ein("maskE", [128, 2048]); ein("maskL", [128, 384])
    ein("io_row", [128, 64]); ein("io_pp", [64, 1]); ein("identd", [128, 128], BF16)
    ein("ident8f", [8, 8])
    ein("ones_colb", [128, 1], BF16); ein("ones_colf", [128, 1])
    ein("ones_row", [1, 128]); ein("ones512", [1, 512])
    ein("idx_fcq", [128, 8], U32); ein("idx_fckv", [128, 11], U32)
    ein("idx_fkr", [64, 1], U32); ein("idx_h", [128, 8], U32)
    ein("idx_t", [128, 16], U32); ein("idx_c", [128, 16], U32)

    def sh_t(name, shape, dt=BF16):
        t[name] = nc.dram_tensor(name, shape, dt, kind="Internal",
                                 addr_space="Shared").ap()

    sh_t("cq_sh", [2 * 1024, 2048]); sh_t("ckv_sh", [2 * 1408, 2048])
    sh_t("kr_sh", [2 * 64, 2048])
    sh_t("ctx0_sh", [2048, S]); sh_t("x2_sh", [2 * 2048, 2048])
    sh_t("ctx1_sh", [2048, S])

    ikind = "ExternalOutput" if dbg else "Internal"

    def loc(name, shape, dt, kind):
        t[name] = nc.dram_tensor(name, shape, dt, kind=kind).ap()

    loc("q0", [1024, S], BF16, ikind)
    loc("knope", [512, S], BF16, ikind)
    loc("vt", [S, 1024], BF16, ikind)
    loc("x1", [D, TC], F32, ikind)
    loc("x2", [D, TC], F32, ikind)
    loc("x3", [D, TC], F32, ikind)
    loc("q1", [1024, S], BF16, ikind)
    loc("k1", [1024, S], BF16, ikind)
    loc("v1t", [S, 1024], BF16, ikind)
    loc("q1t", [S, 1024], BF16, "Internal")
    loc("u2d", [D, TC], F32, "Internal")
    loc("rankm", [8, S], F32, "Internal")
    loc("maskm", [8, S], F32, "Internal")
    loc("bar_in", [1, 128], F32, "Internal")
    for i in range(4):
        loc(f"bar_out{i}", [1, 128], F32, "Internal")
    t["xout"] = nc.dram_tensor("xout", [D, TC], F32, kind="ExternalOutput").ap()

    with ExitStack() as ctx:
        C = {}
        C["onc"] = ctx.enter_context(nc.sbuf_tensor("c_onc", [128, 1], BF16))
        C["oncf"] = ctx.enter_context(nc.sbuf_tensor("c_oncf", [128, 1], F32))
        C["onr"] = ctx.enter_context(nc.sbuf_tensor("c_onr", [1, 128], F32))
        C["gq"] = ctx.enter_context(nc.sbuf_tensor("c_gq", [128, 8], F32))
        C["gkv"] = ctx.enter_context(nc.sbuf_tensor("c_gkv", [128, 11], F32))
        C["idx_fcq"] = ctx.enter_context(nc.sbuf_tensor("c_ifcq", [128, 8], U32))
        C["idx_fckv"] = ctx.enter_context(nc.sbuf_tensor("c_ifckv", [128, 11], U32))
        C["idx_fkr"] = ctx.enter_context(nc.sbuf_tensor("c_ifkr", [64, 1], U32))
        C["idx_h"] = ctx.enter_context(nc.sbuf_tensor("c_ih", [128, 8], U32))
        C["idx_t"] = ctx.enter_context(nc.sbuf_tensor("c_it", [128, 16], U32))
        C["idx_c"] = ctx.enter_context(nc.sbuf_tensor("c_ic", [128, 16], U32))
        ar = Arena(nc, ctx, 48744)
        rw = Rows(nc, ctx, 4096)
        p = Prog(nc, ctx)
        p.pb = [ctx.enter_context(nc.psum_tensor(f"pb{i}", [128, 512], F32))
                for i in range(8)]

        for nm in ("onc", "oncf", "onr", "gq", "gkv",
                   "idx_fcq", "idx_fckv", "idx_fkr", "idx_h", "idx_t", "idx_c"):
            src = {"onc": "ones_colb", "oncf": "ones_colf", "onr": "ones_row"}.get(nm, nm)
            key = "idx" if nm.startswith("idx") else nm
            p.op("sync", lambda e, nm=nm, src=src: e.dma_start(out=C[nm][:, :], in_=t[src]),
                 writes=[key], dma=True)
        p.op("sync", lambda e: e.dma_start(out=t["bar_in"], in_=C["onr"][0:1, :]),
             reads=["onr"], writes=["barin"], dma=True)

        def barrier(i):
            p.fence()
            p.cc(lambda e, i=i: e.collective_compute(
                "AllReduce", mybir.AluOpType.add, replica_groups=PAIRS,
                ins=[t["bar_in"].opt()], outs=[t[f"bar_out{i}"].opt()]))
            p.fence()

        for ph in phases:
            ar.reset(0)
            rw.reset()
            if ph == "AB":
                em_AB(p, t, C, ar, rw)
                barrier(0)
            elif ph == "C":
                em_C(p, t, C, ar, rw)
                p.fence()
            elif ph == "D":
                em_D(p, t, C, ar, rw)
                p.fence()
            elif ph == "E":
                em_E(p, t, C, ar, rw)
                barrier(1)
            elif ph == "F":
                em_F(p, t, C, ar, rw)
                p.fence()
            elif ph == "G0":
                em_G(p, t, C, ar, rw, "L0", t["x1"], t["x2"], scat=True)
                barrier(2)
            elif ph == "H":
                em_H(p, t, C, ar, rw)
                p.fence()
            elif ph == "I":
                em_I(p, t, C, ar, rw)
                barrier(3)
            elif ph == "J":
                em_J(p, t, C, ar, rw)
                p.fence()
            elif ph == "G1":
                em_G(p, t, C, ar, rw, "L1", t["x3"], t["xout"], scat=False)
        p.emit()
    return nc


def kernel(**ins):
    shd, cores = stage_inputs(ins)
    nc = build_fused()
    maps = []
    for c in range(NCORES):
        m = dict(shd)
        m.update(cores[c])
        maps.append(m)
    res = run_bass_kernel_spmd(nc, maps, core_ids=list(range(NCORES)))
    out = np.zeros((B, S, D), np.float32)
    for c in range(NCORES):
        b, r = c // 2, c % 2
        xo = np.asarray(res.results[c]["xout"])
        out[b, r * 2048:(r + 1) * 2048, :] = xo.T
    return out
